# version 19
# baseline (speedup 1.0000x reference)
import contextlib
from collections import defaultdict

import numpy as np
import ml_dtypes

import concourse.bass as bass
import concourse.mybir as mybir
from concourse.bass_utils import run_bass_kernel_spmd

BFNP = ml_dtypes.bfloat16
F32 = mybir.dt.float32
BF16 = mybir.dt.bfloat16
AF = mybir.ActivationFunctionType
ALU = mybir.AluOpType
AX = mybir.AxisListType

L, D, H, DH, INNER = 2048, 1024, 16, 64, 4096
NT, ND, NO = L // 128, D // 128, INNER // 128
MAGIC = float(np.float32(1.5 * 2 ** 23))
EPS = 1e-5
SLOPES = [2.0 ** (-(h + 1) * 0.5) for h in range(H)]
GWH = [128, 256, 256, 512, 512] + [512] * 11
WTH = [127.0 + 45.0 / s for s in SLOPES]
ENG_SEM = {"tensor": "TS", "vector": "VS", "scalar": "SS", "gpsimd": "GS", "sync": "SY"}


def _attn_plan():
    plans = {}
    for h in range(H):
        gw, W = GWH[h], WTH[h]
        for qh in range(2):
            steps = []
            kiv = {j: [] for j in (2 * qh, 2 * qh + 1)}
            for ki in range(8 * qh + 8):
                k0 = 128 * ki
                for j in (2 * qh, 2 * qh + 1):
                    c0 = 512 * j
                    exps, zeros, mask = [], [], None
                    for qg0 in range(c0, c0 + 512, gw):
                        if qg0 + gw <= k0 or qg0 - k0 > W:
                            zeros.append((qg0, qg0 + gw))
                        else:
                            exps.append((qg0, qg0 - k0))
                            if k0 >= qg0:
                                if k0 > qg0:
                                    zeros.append((qg0, k0))
                                mask = k0
                    if exps:
                        wlo = min(q for q, _ in exps)
                        whi = max(q for q, _ in exps) + gw
                        steps.append(dict(ki=ki, j=j, wlo=wlo, whi=whi,
                                          exps=exps, zeros=zeros, mask=mask))
                        kiv[j].append(ki)
            plans[(h, qh)] = (steps, kiv)
    return plans


ATTN_PLANS = _attn_plan()
_EB_COLS = {}
for (_h, _qh), (_steps, _kiv) in ATTN_PLANS.items():
    for _st in _steps:
        for _qg0, _delta in _st["exps"]:
            _EB_COLS.setdefault((_h, _delta), len(_EB_COLS))


def _make_consts():
    kl = np.arange(128, dtype=np.float64)
    eb = np.zeros((128, max(8, len(_EB_COLS))), dtype=np.float32)
    for (h, delta), c in _EB_COLS.items():
        eb[:, c] = (SLOPES[h] * (kl - delta - GWH[h] / 2)).astype(np.float32)
    m01 = (np.arange(128)[:, None] <= np.arange(128)[None, :]).astype(BFNP)
    return eb, m01


class Arena:
    def __init__(self, nc, name, dtype, ncols):
        self.t = nc.alloc_sbuf_tensor(name, [128, ncols], dtype)
        self.n = ncols
        self.off = 0
        self.stack = []

    def alloc(self, cols, parts=128):
        assert self.off + cols <= self.n, f"arena overflow {self.off}+{cols}>{self.n}"
        ap = self.t[0:parts, self.off:self.off + cols]
        self.off += cols
        return ap

    def alloc_at(self, off, cols, parts=128):
        assert off + cols <= self.n
        return self.t[0:parts, off:off + cols]

    def mark(self):
        self.stack.append(self.off)

    def reset(self):
        self.off = self.stack.pop()


class Builder:
    def __init__(self):
        nc = bass.Bass("TRN2", target_bir_lowering=False)
        self.nc = nc
        self.sched = []
        self.C = defaultdict(int)
        self.stack = contextlib.ExitStack()
        self.sem = {s: self.stack.enter_context(nc.semaphore(s))
                    for s in ("TS", "VS", "SS", "GS", "SY")}

        d = nc.dram_tensor
        self.x = d("x", [L, D], F32, kind="ExternalInput").ap()
        self.w_in = d("w_in", [D, 3 * D], F32, kind="ExternalInput").ap()
        self.b_in = d("b_in", [3 * D], F32, kind="ExternalInput").ap()
        self.w_out = d("w_out", [D, D], F32, kind="ExternalInput").ap()
        self.b_out = d("b_out", [D], F32, kind="ExternalInput").ap()
        self.w_gate = d("w_gate", [INNER, D], F32, kind="ExternalInput").ap()
        self.w_up = d("w_up", [INNER, D], F32, kind="ExternalInput").ap()
        self.w_down = d("w_down", [D, INNER], F32, kind="ExternalInput").ap()
        self.y = d("y", [L, D], F32, kind="ExternalOutput").ap()
        self.s_xnb = d("s_xnb", [L, D], BF16).ap()
        self.s_n = d("s_n", [L, D], BF16).ap()
        self.s_attn = d("s_attn", [D, L], BF16).ap()
        self.s_acc = d("s_acc", [L, D], F32).ap()
        self.s_mg = d("s_mg", [INNER, D], BF16).ap()
        self.s_mu = d("s_mu", [INNER, D], BF16).ap()
        self.s_md = d("s_md", [D, INNER], BF16).ap()
        self.s_n2 = d("s_n2", [L, INNER], BF16).ap()
        self.s_d1 = d("s_d1", [4, 512], F32).ap()
        self.s_d2 = d("s_d2", [4, 1024], BF16).ap()

        eb_np, m01_np = _make_consts()
        self.c_eb = nc.inline_tensor(eb_np, name="c_eb")
        self.c_m01 = nc.inline_tensor(m01_np, name="c_m01")
        self.n_eb = eb_np.shape[1]

        self.fa = Arena(nc, "fa", F32, 8832)
        self.ba = Arena(nc, "ba", BF16, 88320)

        self.eb = self.fa.alloc(self.n_eb)
        self.cpool = self.fa.alloc(4)
        self.dq1_all = self.fa.alloc(NT)
        self.dq2_all = self.fa.alloc(NT)
        self.dqg_all = self.fa.alloc(NT)
        self.dqu_all = self.fa.alloc(NT)
        self.ddq_all = self.fa.alloc(NT)
        self.b_qk = self.fa.alloc(16)
        self.wsv = {w: self.fa.alloc(1) for w in "gud"}
        self.mav = {w: self.fa.alloc(1) for w in "gud"}
        self.stat = self.fa.alloc(8)
        self.ones_col_f = self.fa.alloc(1)
        self.ones_row128 = self.fa.alloc(128, parts=1)
        self.ones_row64 = self.ones_row128[0:1, 0:64]

        self.m01 = self.ba.alloc(128)
        self.ones_row = self.ba.alloc(2048, parts=1)
        self.bv_row = self.ba.alloc(1024, parts=1)
        self.bo_row = self.ba.alloc(1024, parts=1)

        self.mdone = {}
        self.wqacc = self.fa.alloc(36)

        self.P = [nc.alloc_psum_tensor(f"pb{i}", [128, 512], F32) for i in range(8)]
        self.bguard = {i: [] for i in range(8)}

    def em(self, engine, fn):
        self.sched.append((engine, fn))

    def _sem(self, name):
        if name not in self.sem:
            self.sem[name] = self.stack.enter_context(self.nc.semaphore(name))
        return self.sem[name]

    def wait(self, engine, sem, val):
        if val > 0:
            self.em(engine, lambda e, s=self._sem(sem), v=val: e.wait_ge(s, v))

    def waitg(self, engine, guard):
        best = {}
        for s, v in guard:
            best[s] = max(best.get(s, 0), v)
        for s, v in best.items():
            self.wait(engine, s, v)

    def dma(self, queue, out, in_, sem, transpose=False, slow=False, gate=()):
        sh = self._sem(sem)
        self.waitg(queue, gate)
        self.C[sem] += 16
        v = self.C[sem]
        if transpose:
            self.em(queue, lambda e, o=out, i=in_, s=sh: e.dma_start_transpose(out=o, in_=i).then_inc(s, 16))
        else:
            self.em(queue, lambda e, o=out, i=in_, s=sh, sl=slow: e.dma_start(out=o, in_=i, allow_slow_non_contiguous=sl).then_inc(s, 16))
        return (sem, v)

    def op(self, engine, fn):
        s = ENG_SEM[engine]
        prev = 0 if engine == "tensor" else self.C[s]
        self.C[s] += 1
        v = self.C[s]

        def run(e, f=fn, sh=self.sem[s], p=prev, vv=v):
            if p > 0:
                e.wait_ge(sh, p)
            return f(e).then_inc(sh, 1)
        self.em(engine, run)
        return (s, v)

    def barrier(self):
        vals = [(s, self.C[s]) for s in list(self.C) if self.C[s] > 0]
        for eng in ("sync", "tensor", "vector", "scalar", "gpsimd"):
            for s, v in vals:
                self.wait(eng, s, v)

    def build(self):
        self.ph_init()
        self.ph_ln_quant()
        self.ph_attention()
        self.ph_outproj()
        self.ph_ffn_gateup()
        self.ph_ffn_down()
        self.finish()
        return self.materialize()

    def ph_init(self):
        B = self
        B.dma("sync", B.eb, B.c_eb[:], "DIN")
        B.dma("sync", B.m01, B.c_m01[:], "DIN")
        bq = B.b_in[0:D].rearrange("(j p) -> p j", p=128)
        bk = B.b_in[D:2 * D].rearrange("(j p) -> p j", p=128)
        B.dma("sync", B.b_qk[:, 0:8], bq, "DIN", slow=True)
        B.dma("sync", B.b_qk[:, 8:16], bk, "DIN", slow=True)
        B.fa.mark()
        bvf = B.fa.alloc(1024, parts=1)
        bof = B.fa.alloc(1024, parts=1)
        B.dma("sync", bvf, B.b_in[2 * D:3 * D][None, :], "DIN")
        g = B.dma("sync", bof, B.b_out[None, :], "DIN")
        B.wait("vector", "DIN", g[1])
        B.op("vector", lambda e: e.memset(B.cpool[:, 0:1], -0.5))
        B.op("vector", lambda e: e.memset(B.cpool[:, 1:2], 0.5))
        B.op("vector", lambda e: e.memset(B.cpool[:, 2:3], 0.125))
        B.op("vector", lambda e: e.memset(B.cpool[:, 3:4], MAGIC))
        B.op("vector", lambda e: e.memset(B.ones_col_f, 1.0))
        B.op("vector", lambda e: e.memset(B.ones_row128, 1.0))
        B.op("vector", lambda e: e.memset(B.ones_row, 1.0))
        B.op("vector", lambda e, s=bvf: e.tensor_copy(out=B.bv_row, in_=s))
        B.op("vector", lambda e, s=bof: e.tensor_copy(out=B.bo_row, in_=s))
        B.op("vector", lambda e: e.tensor_scalar_mul(out=B.b_qk[:, 0:8], in0=B.b_qk[:, 0:8], scalar1=0.125))
        B.fa.reset()

    def ph_ln_quant(self):
        B = self
        B.barrier()
        fa, ba = B.fa, B.ba
        ba.mark()
        B.op_region = ba.off
        B.afm_t = [ba.alloc(2048) for _ in range(ND)]
        B.wo_sb = ba.alloc(ND * 1024)
        ba.mark()
        B.xnfm_all = ba.alloc(ND * 2048)
        B.xn_fm_t = [B.xnfm_all[:, 2048 * c:2048 * c + 2048] for c in range(ND)]
        B.wqm2 = [ba.alloc(1024) for _ in range(2)]
        B.wqsg = [ba.alloc(1024) for _ in range(2)]
        fa.mark()
        B.wqb = [fa.alloc(1024) for _ in range(2)]
        B.wq_gen = B.wquant_gen(B.wqb, B.wqm2, B.wqsg)
        B.att_wv_region = ba.off
        B.att_wv = ba.alloc(8 * 1024)
        B.att_v = ba.alloc(NT * 1040)
        B.att_wqk = [ba.alloc(2 * 1024) for _ in range(2)]
        B.att_wstage = [fa.alloc(1024) for _ in range(2)]
        B.att_stg_cons = [None, None]
        wvv0 = B.w_in[:, 2 * D:3 * D].rearrange("(c p) f -> c p f", p=128)
        wov0 = B.w_out.rearrange("(c p) f -> c p f", p=128)
        B.att_wcast = []
        for c in range(ND):
            sl = c % 2
            dv = B.dma("sync", B.att_wstage[sl], wvv0[c], "WST%d" % sl,
                       gate=[B.att_stg_cons[sl]] if B.att_stg_cons[sl] else [])
            B.waitg("gpsimd", [dv])
            B.att_stg_cons[sl] = B.op("gpsimd", lambda e, cc=c, s2=sl: e.tensor_copy(
                out=B.att_wv[:, 1024 * cc:1024 * cc + 1024], in_=B.att_wstage[s2]))
            B.att_wcast.append(B.att_stg_cons[sl])
        for c in range(ND):
            sl = c % 2
            dv = B.dma("sync", B.att_wstage[sl], wov0[c], "WST%d" % sl,
                       gate=[B.att_stg_cons[sl]] if B.att_stg_cons[sl] else [])
            B.waitg("gpsimd", [dv])
            B.att_stg_cons[sl] = B.op("gpsimd", lambda e, cc=c, s2=sl: e.tensor_copy(
                out=B.wo_sb[:, 1024 * cc:1024 * cc + 1024], in_=B.att_wstage[s2]))
        B.att_vs_ones = B.op("vector", lambda e: e.memset(B.att_v, 1.0))
        fa.mark(); ba.mark()
        xt = [fa.alloc(1024) for _ in range(2)]
        xn = [fa.alloc(1024) for _ in range(2)]
        dump = xn[0]
        ls = fa.alloc(8 * 16)
        nb = [ba.alloc(1024) for _ in range(2)]
        xb = [ba.alloc(1024) for _ in range(2)]
        xv = B.x.rearrange("(t p) d -> t p d", p=128)
        nv = B.s_n.rearrange("(t p) d -> t p d", p=128)

        def C(i):
            return ls[:, 16 * i:16 * i + 16]

        def Ct(i, t):
            return ls[:, 16 * i + t:16 * i + t + 1]

        sq_cons = [[], []]
        gq = None
        for t in range(NT):
            par = t % 2
            dv = B.dma("sync", xt[par], xv[t], "LX%d" % par, gate=sq_cons[par])
            B.waitg("scalar", [dv])
            B.op("scalar", lambda e, p=par, tt=t: e.activation(
                out=dump, in_=xt[p], func=AF.Identity, accum_out=Ct(0, tt)))
            gq = B.op("scalar", lambda e, p=par, tt=t: e.activation(
                out=dump, in_=xt[p], func=AF.Square, accum_out=Ct(1, tt)))
            sq_cons[par] = [gq]
            B.pump(2)
        B.waitg("vector", [gq])
        B.op("vector", lambda e: e.tensor_scalar_mul(out=C(2), in0=C(0), scalar1=1.0 / 1024))
        B.op("vector", lambda e: e.tensor_scalar_mul(out=C(3), in0=C(1), scalar1=1.0 / 1024))
        B.op("vector", lambda e: e.tensor_mul(out=C(5), in0=C(2), in1=C(2)))
        B.op("vector", lambda e: e.tensor_sub(out=C(3), in0=C(3), in1=C(5)))
        B.op("vector", lambda e: e.tensor_scalar_add(out=C(3), in0=C(3), scalar1=EPS))
        g = B.op("vector", lambda e: e.reciprocal(out=C(4), in_=C(3)))
        B.waitg("scalar", [g])
        g = B.op("scalar", lambda e: e.activation(out=C(4), in_=C(4), func=AF.Sqrt))
        B.waitg("vector", [g])
        for _ in range(2):
            B.op("vector", lambda e: e.tensor_mul(out=C(5), in0=C(4), in1=C(4)))
            B.op("vector", lambda e: e.tensor_mul(out=C(5), in0=C(5), in1=C(3)))
            B.op("vector", lambda e: e.tensor_scalar(out=C(5), in0=C(5), scalar1=-0.5, scalar2=1.5, op0=ALU.mult, op1=ALU.add))
            B.op("vector", lambda e: e.tensor_mul(out=C(4), in0=C(4), in1=C(5)))
        B.op("vector", lambda e: e.tensor_mul(out=C(5), in0=C(2), in1=C(4)))
        gmr = B.op("vector", lambda e: e.tensor_scalar_mul(out=C(5), in0=C(5), scalar1=-1.0))
        B.waitg("scalar", [gmr])
        B.tx_done = {}
        sguard = [[], []]
        xt_cons = [list(sq_cons[0]), list(sq_cons[1])]
        xn_cons = [[], []]
        for t in range(NT):
            par = t % 2
            dv = B.dma("sync", xt[par], xv[t], "LX%d" % par, gate=xt_cons[par])
            B.waitg("scalar", [dv])
            B.waitg("scalar", xn_cons[par])
            gap = B.op("scalar", lambda e, p=par, tt=t: e.activation(
                out=xn[p], in_=xt[p], func=AF.Identity, scale=Ct(4, tt), bias=Ct(5, tt)))
            B.waitg("scalar", sguard[par])
            gx = B.op("scalar", lambda e, p=par: e.copy(out=xb[p], in_=xn[p]))
            B.waitg("vector", [gap])
            gab = B.op("vector", lambda e, p=par, tt=t: e.reduce_max(out=Ct(6, tt), in_=xn[p], axis=AX.X, apply_absolute_value=True))
            B.op("vector", lambda e, tt=t: e.tensor_scalar_max(out=Ct(6, tt), in0=Ct(6, tt), scalar1=1e-5))
            B.op("vector", lambda e, tt=t: e.tensor_scalar_mul(out=B.dq1_all[:, tt:tt + 1], in0=Ct(6, tt), scalar1=1.0 / 127.0))
            B.op("vector", lambda e, tt=t: e.reciprocal(out=Ct(7, tt), in_=Ct(6, tt)))
            gs1 = B.op("vector", lambda e, tt=t: e.tensor_scalar_mul(out=Ct(7, tt), in0=Ct(7, tt), scalar1=127.0))
            B.waitg("scalar", [gs1])
            gy = B.op("scalar", lambda e, p=par, tt=t: e.activation(
                out=xt[p], in_=xn[p], func=AF.Identity, scale=Ct(7, tt), bias=B.cpool[:, 3:4]))
            B.waitg("vector", [gy])
            gn = B.op("vector", lambda e, p=par: e.tensor_scalar(out=nb[p], in0=xt[p], scalar1=MAGIC, scalar2=None, op0=ALU.subtract))
            xt_cons[par] = [gn]
            xn_cons[par] = [gy, gab]
            B.waitg("gpsimd", [gn])
            s2 = B.dma("gpsimd", nv[t], nb[par], "LS%d" % par)
            txg = [B.dma("sync", B.xnfm_all.rearrange("p (c t) -> p c t", c=ND)[:, :, 128 * t:128 * t + 128],
                         xb[par].rearrange("t (c p) -> t c p", c=ND),
                         "TX%d" % par, transpose=True, gate=[gx])]
            sguard[par] = [s2] + txg
            B.tx_done[t] = txg
            B.emit_v_unit(t)
            B.pump(2)
        B.ln_final = sguard[0] + sguard[1]
        fa.reset(); ba.reset()

    def ph_attention(self):
        B = self
        ba, fa = B.ba, B.fa
        ba.mark(); fa.mark()
        xn_fm = B.xn_fm_t
        wv_region = B.att_wv_region
        wv_sb = B.att_wv
        v_sb = B.att_v
        wqk = B.att_wqk
        q_fm2 = [ba.alloc_at(wv_region, 2048), ba.alloc_at(wv_region + 4096, 2048)]
        k_fm2 = [ba.alloc_at(wv_region + 2048, 2048), ba.alloc_at(wv_region + 6144, 2048)]
        p_sb = [[ba.alloc(512) for i in range(3)] for l in range(2)]
        o_attn = ba.alloc(1024)
        o_un = [ba.alloc(512, parts=65) for _ in range(8)]
        srow = fa.alloc(2048, parts=1)
        oun_guard = [[] for _ in range(8)]
        s4_all = fa.alloc(16)
        s4s = [s4_all[:, 4 * k:4 * k + 4] for k in range(4)]
        r4b = fa.alloc(4)
        hif = fa.alloc(4)
        vjk = fa.alloc(4)
        rhl_all = ba.alloc(32)
        rhls = [rhl_all[:, 8 * k:8 * k + 8] for k in range(4)]
        r2blk = fa.alloc(1024, parts=2).bitcast(BF16)
        r2s = [r2blk[0:2, 512 * k:512 * k + 512] for k in range(4)]
        ones2 = fa.alloc(32, parts=2).bitcast(BF16)
        g_srow = [[] for _ in range(4)]
        g_s4 = [[] for _ in range(4)]
        g_rhlw = [[] for _ in range(4)]
        g_d1 = [[] for _ in range(4)]
        g_d2 = [[] for _ in range(4)]
        g_r2 = [[] for _ in range(4)]
        last_nb = {0: [], 1: []}
        last_nq = {0: [], 1: []}
        wstage2 = B.att_wstage
        stg_cons = B.att_stg_cons
        tv = [("TX%d" % p, B.C["TX%d" % p]) for p in range(2)]
        wcast = B.att_wcast
        B.store_done = {h: [] for h in range(H)}
        vs_ones = B.att_vs_ones
        B.waitg("tensor", [wcast[-1], vs_ones])
        SPOOL = {0: (2, 3, 4), 1: (5, 6, 7)}
        OBANK = {0: 0, 1: 1}
        wqk_guard = [[], []]
        qk_guard = []
        pv_guard = [[[], [], []], [[], [], []]]
        sa_guard = [[], []]
        norm_pending = []
        oun_next = [0]
        B.gstep = 0

        def emit_norm(maxn=10 ** 6, due=None):
            n = 0
            while norm_pending and n < maxn:
                if due is not None and norm_pending[0][0] > due:
                    return
                norm_pending.pop(0)[1]()
                n += 1

        def load_wqk(hp2):
            par2 = hp2 % 2
            B.waitg("sync", wqk_guard[par2])
            out = []
            for qk, c0 in ((0, 128 * hp2), (1, D + 128 * hp2)):
                sl = qk
                wvw = B.w_in[:, c0:c0 + 128].rearrange("(c p) f -> p c f", p=128)
                dv = B.dma("sync", wstage2[sl].rearrange("p (c f) -> p c f", f=128), wvw, "WST%d" % sl,
                           gate=[stg_cons[sl]] if stg_cons[sl] else [])
                B.waitg("gpsimd", [dv])
                stg_cons[sl] = B.op("gpsimd", lambda e, p=par2, q=qk, s2=sl: e.tensor_copy(
                    out=wqk[p][:, 1024 * q:1024 * q + 1024], in_=wstage2[s2]))
                out.append(stg_cons[sl])
            return out

        B.waitg("tensor", tv)
        B.waitg("scalar", B.ln_final)
        B.waitg("vector", B.ln_final)
        B.ln_final = None
        g12 = B.op("vector", lambda e: e.memset(ones2[:], 1.0))
        B.op("vector", lambda e: e.memset(vjk[:], 0.0))
        B.waitg("tensor", [g12])
        wq_cast_next = load_wqk(0)
        qk_guard2 = [[], []]
        for hp in range(8):
            par = hp % 2
            q_fm, k_fm = q_fm2[par], k_fm2[par]
            wq_cast = wq_cast_next
            B.waitg("tensor", [wq_cast[-1]])
            qk_ready = []
            for qk in range(2):
                for tc in range(4):
                    bank = (qk * 4 + tc) % 4
                    Pb = B.P[bank]
                    B.waitg("tensor", B.bguard[bank])
                    for c in range(ND - 1):
                        B.em("tensor", lambda e, b=Pb, cc=c, q=qk, t=tc, p=par: e.matmul(
                            b[:], wqk[p][:, 1024 * q + 128 * cc:1024 * q + 128 * cc + 128],
                            xn_fm[cc][:, 512 * t:512 * t + 512],
                            start=(cc == 0), stop=False))
                    tg = B.op("tensor", lambda e, b=Pb, q=qk, t=tc, p=par: e.matmul(
                        b[:], wqk[p][:, 1024 * q + 128 * (ND - 1):1024 * q + 128 * ND],
                        xn_fm[ND - 1][:, 512 * t:512 * t + 512],
                        start=False, stop=True))
                    B.waitg("scalar", [tg])
                    B.waitg("scalar", qk_guard2[par])
                    qk_guard2[par] = []
                    dst = q_fm if qk == 0 else k_fm
                    bias = B.b_qk[:, 8 * qk + hp:8 * qk + hp + 1]
                    if qk == 0:
                        g = B.op("scalar", lambda e, b=Pb, t=tc, dd=dst, bi=bias: e.activation(
                            out=dd[:, 512 * t:512 * t + 512], in_=b[:], func=AF.Identity,
                            scale=B.cpool[:, 2:3], bias=bi))
                    else:
                        g = B.op("scalar", lambda e, b=Pb, t=tc, dd=dst, bi=bias: e.activation(
                            out=dd[:, 512 * t:512 * t + 512], in_=b[:], func=AF.Identity, bias=bi))
                    B.bguard[bank] = [g]
                    qk_ready.append(g)
            B.waitg("tensor", [qk_ready[-1]])
            if hp < 7:
                wq_cast_next = load_wqk(hp + 1)
            if hp >= 2:
                c2 = hp - 2
                B.dma("sync", B.afm_t[c2], B.s_attn[128 * c2:128 * c2 + 128, :], "AFM",
                      gate=B.store_done[2 * c2] + B.store_done[2 * c2 + 1])
            qk_guard = []
            for qh in range(2):
                locs = [0, 1]
                plans = {loc: ATTN_PLANS[(2 * hp + loc, qh)] for loc in locs}
                keys = sorted({(st["j"], st["ki"]) for loc in locs for st in plans[loc][0]})
                stepmap = {loc: {(st["j"], st["ki"]): st for st in plans[loc][0]} for loc in locs}
                ready = {}

                def emit_v(si2, keys=keys, stepmap=stepmap, plans=plans, ready=ready, hp=hp, qh=qh):
                    sp2 = si2 % 3
                    infos = []
                    for loc2 in locs:
                        info = ready.pop((loc2, si2), None)
                        if info is None:
                            continue
                        ki2, j2, eg2, vg2, pbuf2, h2 = info
                        kivj = plans[loc2][1][j2]
                        first = (ki2 == kivj[0])
                        lastk = (ki2 == kivj[-1])
                        B.waitg("tensor", eg2)
                        B.waitg("tensor", vg2)
                        if first:
                            B.waitg("tensor", B.bguard[OBANK[loc2]])
                        infos.append((loc2, ki2, j2, pbuf2, h2, first, lastk))
                    for loc2, ki2, j2, pbuf2, h2, first, lastk in infos:
                        obank = OBANK[loc2]
                        g = B.op("tensor", lambda e, ob=B.P[obank], k=ki2, hh=h2, pb=pbuf2, f1=first, l1=lastk: e.matmul(
                            ob[0:65, :],
                            v_sb[:, 1040 * k + 65 * hh:1040 * k + 65 * hh + 65],
                            pb[:], start=f1, stop=l1))
                        pv_guard[loc2][sp2] = [g]
                        if lastk:
                            _norm_part1(loc2, j2, hp, qh, [g])

                def _norm_part1(loc2, j2, hp2, qh2, vlast):
                    while len(norm_pending) > 9:
                        emit_norm(1)
                    h2 = 2 * hp2 + loc2
                    jj = j2 % 2
                    obank = OBANK[loc2]
                    Ob = B.P[obank]
                    pu = oun_next[0]
                    oun_next[0] = (pu + 1) % 8
                    B.waitg("scalar", vlast)
                    B.waitg("scalar", oun_guard[pu])
                    ru = pu % 4
                    B.waitg("scalar", g_srow[ru])
                    gs0 = B.op("scalar", lambda e, b=Ob, r2=ru: e.copy(
                        out=srow[0:1, 512 * r2:512 * r2 + 512], in_=b[64:65, :]))
                    gs_ = B.op("scalar", lambda e, b=Ob, p2=pu: e.copy(out=o_un[p2][0:64, :], in_=b[0:64, :]))
                    B.bguard[obank] = [gs_]
                    B.waitg("gpsimd", [gs0])
                    d1a = B.dma("gpsimd", B.s_d1[ru], srow[0:1, 512 * ru:512 * ru + 512],
                                "NB%d" % (ru % 2), gate=g_d1[ru] + last_nb[ru % 2])
                    g_srow[ru] = [d1a]
                    d1b = B.dma("gpsimd", s4s[ru], B.s_d1[ru].rearrange("(p c) -> p c", c=4),
                                "NB%d" % (ru % 2), gate=[d1a] + g_s4[ru])
                    g_d1[ru] = [d1b]
                    last_nb[ru % 2] = [d1b]
                    st0 = B.gstep
                    ev = {}

                    def part2a(ru3=ru, d1g=[d1b]):
                        B.waitg("vector", d1g)
                        B.waitg("vector", g_rhlw[ru3])
                        B.op("vector", lambda e, r3=ru3: e.reciprocal(out=r4b[:], in_=s4s[r3]))
                        B.op("vector", lambda e: e.memset(vjk[:], 0.0))
                        B.op("vector", lambda e, r3=ru3: e.tensor_copy(out=rhls[r3][:, 0:4], in_=r4b[:]))
                        B.op("vector", lambda e: e.memset(vjk[:], 0.0))
                        B.op("vector", lambda e: e.tensor_copy(out=hif[:], in_=r4b[:]))
                        B.op("vector", lambda e, r3=ru3: e.tensor_copy(out=hif[:], in_=rhls[r3][:, 0:4]))
                        B.op("vector", lambda e: e.memset(vjk[:], 0.0))
                        B.op("vector", lambda e, r3=ru3: e.tensor_tensor(
                            out=rhls[r3][:, 4:8], in0=r4b[:], in1=hif[:], op=ALU.subtract))
                        B.op("vector", lambda e: e.memset(vjk[:], 0.0))
                        gva = B.op("vector", lambda e: e.memset(vjk[:], 0.0))
                        g_s4[ru3] = [gva]
                        ev["gva"] = gva

                    def part2b(ru3=ru):
                        d2a = B.dma("sync", B.s_d2[ru3].rearrange("(p c) -> p c", c=8), rhls[ru3],
                                    "NQ%d" % (ru3 % 2), gate=[ev["gva"]] + g_d2[ru3] + last_nq[ru3 % 2])
                        g_rhlw[ru3] = [d2a]
                        d2b = B.dma("sync", r2s[ru3].rearrange("t (p c) -> t p c", c=4),
                                    B.s_d2[ru3].rearrange("(p t2 c) -> t2 p c", t2=2, c=4),
                                    "NQ%d" % (ru3 % 2), gate=[d2a] + g_r2[ru3])
                        g_d2[ru3] = [d2b]
                        last_nq[ru3 % 2] = [d2b]
                        ev["d2b"] = d2b

                    def part2c(loc3=loc2, h3=h2, jj3=jj, qh3=qh2, pu3=pu, ru3=ru, gs3=gs_):
                        rbank = SPOOL[loc3][0]
                        Rb = B.P[rbank]
                        B.waitg("tensor", [ev["d2b"]])
                        B.waitg("tensor", B.bguard[rbank])
                        gt = B.op("tensor", lambda e, b=Rb, r3=ru3: e.matmul(
                            b[0:64, :], ones2[:], r2s[r3], start=True, stop=True))
                        g_r2[ru3] = [gt]
                        B.waitg("vector", [gt, gs3])
                        B.waitg("vector", sa_guard[jj3])
                        gv = B.op("vector", lambda e, b=Rb, p2=pu3, j2=jj3: e.tensor_tensor(
                            out=o_attn[0:64, 512 * j2:512 * j2 + 512],
                            in0=o_un[p2][0:64, :], in1=b[0:64, :], op=ALU.mult))
                        oun_guard[pu3] = [gv]
                        B.bguard[rbank] = [gv]
                        B.waitg("gpsimd", [gv])
                        sa_guard[jj3] = [B.dma("gpsimd",
                            B.s_attn[64 * h3:64 * h3 + 64, 1024 * qh3 + 512 * jj3:1024 * qh3 + 512 * jj3 + 512],
                            o_attn[0:64, 512 * jj3:512 * jj3 + 512], "SA%d" % jj3)]
                        B.store_done[h3] = B.store_done[h3] + sa_guard[jj3]
                    norm_pending.append((st0 + 6, part2a))
                    norm_pending.append((st0 + 8, part2b))
                    norm_pending.append((st0 + 10, part2c))

                final_blk = (hp == 7 and qh == 1)
                for si, (j, ki) in enumerate(keys):
                    sp = si % 3
                    B.gstep += 1
                    if final_blk:
                        emit_norm(10 ** 6)
                    else:
                        emit_norm(2, due=B.gstep)
                    for loc in locs:
                        st_ = stepmap[loc].get((j, ki))
                        if st_ is None:
                            continue
                        h = 2 * hp + loc
                        sbank = SPOOL[loc][sp]
                        Sb = B.P[sbank]
                        wlo, whi = st_["wlo"], st_["whi"]
                        B.waitg("tensor", B.bguard[sbank])
                        tg = B.op("tensor", lambda e, b=Sb, l=loc, k=ki, a=wlo, z=whi, kf=k_fm, qf=q_fm: e.matmul(
                            b[0:128, 0:z - a],
                            kf[64 * l:64 * l + 64, 128 * k:128 * k + 128],
                            qf[64 * l:64 * l + 64, a:z],
                            start=True, stop=True, tile_position=(64 * l, 0)))
                        qk_guard.append(tg)
                        B.waitg("scalar", [tg])
                        B.waitg("scalar", pv_guard[loc][sp])
                        pbuf = p_sb[loc][sp]
                        eg = []
                        for qg0, delta in st_["exps"]:
                            gw = GWH[h]
                            col = _EB_COLS[(h, delta)]
                            eg.append(B.op("scalar", lambda e, b=Sb, q=qg0, g0=gw, a=wlo, c=col, pb=pbuf, jj=j: e.activation(
                                out=pb[:, q - 512 * jj:q - 512 * jj + g0],
                                in_=b[0:128, q - a:q - a + g0],
                                func=AF.Exp, bias=B.eb[:, c:c + 1])))
                        B.bguard[sbank] = [eg[-1]]
                        vg = []
                        if st_["zeros"] or st_["mask"] is not None:
                            B.waitg("vector", [eg[-1]])
                            B.waitg("vector", pv_guard[loc][sp])
                        for z0, z1 in st_["zeros"]:
                            vg.append(B.op("vector", lambda e, a=z0, b2=z1, pb=pbuf, jj=j: e.memset(
                                pb[:, a - 512 * jj:b2 - 512 * jj], 0.0)))
                        if st_["mask"] is not None:
                            mk = st_["mask"]
                            vg.append(B.op("vector", lambda e, m=mk, pb=pbuf, jj=j: e.tensor_tensor(
                                out=pb[:, m - 512 * jj:m - 512 * jj + 128],
                                in0=pb[:, m - 512 * jj:m - 512 * jj + 128],
                                in1=B.m01[:], op=ALU.mult)))
                        ready[(loc, si)] = (ki, j, eg[-1:], vg[-1:], pbuf, h)
                    if si % 2 == 1:
                        B.pump(1)
                    if si >= 2:
                        emit_v(si - 2)
                if len(keys) >= 2:
                    emit_v(len(keys) - 2)
                emit_v(len(keys) - 1)
                B.pump(2)
            wqk_guard[par] = list(qk_guard[-1:])
            qk_guard2[par] = list(qk_guard[-1:])
            qk_guard = []
        emit_norm(10 ** 6)
        B.pump(10 ** 6)
        ba.reset(); fa.reset()
        B.ba.reset()
        B.fa.reset()

    def ph_outproj(self):
        B = self
        ba, fa = B.ba, B.fa
        ba.mark()
        B.nfm_all = ba.alloc(ND * 2048)
        B.n_fm_t = [B.nfm_all[:, 2048 * c:2048 * c + 2048] for c in range(ND)]
        B.mg_all = ba.alloc(ND * 4096)
        B.mg_t = [B.mg_all[:, 4096 * c:4096 * c + 4096] for c in range(ND)]
        ba.mark(); fa.mark()
        attn_fm = B.afm_t
        wo_sb = B.wo_sb
        xt = [fa.alloc(1024) for _ in range(2)]
        acc = [fa.alloc(1024) for _ in range(2)]
        for c in (6, 7):
            B.dma("sync", attn_fm[c], B.s_attn[128 * c:128 * c + 128, :], "AFM",
                  gate=B.store_done[2 * c] + B.store_done[2 * c + 1])
        B.barrier()
        B.tr_n = B.dma("sync", B.nfm_all.rearrange("p (c t) -> p c t", c=ND),
                       B.s_n.rearrange("t (c p) -> t c p", c=ND), "TRN", transpose=True)
        B.tr_g = B.dma("sync", B.mg_all.rearrange("p (c t) -> p c t", c=ND),
                       B.s_mg.rearrange("t (c p) -> t c p", c=ND), "TRG", transpose=True,
                       gate=B.mdone.get("g", []))
        B.barrier()
        xv = B.x.rearrange("(t p) d -> t p d", p=128)
        av = B.s_acc.rearrange("(t p) d -> t p d", p=128)
        ld_guard = [[], []]
        stg = [[], []]
        for t in range(NT):
            par = t % 2
            dv = B.dma("sync", xt[par], xv[t], "LX%d" % par, gate=ld_guard[par])
            evs = []
            for half in range(2):
                bank = (2 * t + half) % 4
                Pb = B.P[bank]
                B.waitg("tensor", B.bguard[bank])
                for c in range(ND):
                    B.em("tensor", lambda e, b=Pb, cc=c, tt=t, hh=half: e.matmul(
                        b[:], attn_fm[cc][:, 128 * tt:128 * tt + 128],
                        wo_sb[:, 1024 * cc + 512 * hh:1024 * cc + 512 * hh + 512],
                        start=(cc == 0), stop=False))
                tg = B.op("tensor", lambda e, b=Pb, tt=t, hh=half: e.matmul(
                    b[:], B.ones_row[0:1, 128 * tt:128 * tt + 128],
                    B.bo_row[0:1, 512 * hh:512 * hh + 512], start=False, stop=True))
                B.waitg("vector", [tg, dv])
                B.waitg("vector", stg[par])
                g = B.op("vector", lambda e, b=Pb, hh=half, p=par: e.tensor_tensor(
                    out=acc[p][:, 512 * hh:512 * hh + 512],
                    in0=xt[p][:, 512 * hh:512 * hh + 512], in1=b[:], op=ALU.add))
                B.bguard[bank] = [g]
                evs.append(g)
            ld_guard[par] = [evs[-1]]
            B.waitg("gpsimd", evs)
            stg[par] = [B.dma("gpsimd", av[t], acc[par], "LS%d" % par)]
        ba.reset(); fa.reset()
        B.outproj_done = [evs[-1]]

    def wquant_gen(self, wb, m2, sg):
        B = self
        accs = B.wqacc
        specs = [("g", B.w_gate, B.s_mg, 32, 1024), ("u", B.w_up, B.s_mu, 32, 1024),
                 ("d", B.w_down, B.s_md, 8, 4096)]
        cons_guard = [[], []]
        mstore = [[], []]
        sg_guard = []
        for wname, wsrc, mdst, ntile, fdim in specs:
            nsub = fdim // 1024
            wv = wsrc.rearrange("(t p) f -> t p f", p=128)
            gv = None
            ncols = ntile * nsub
            B.waitg("scalar", sg_guard)
            for t in range(ntile):
                for s_ in range(nsub):
                    idx = (t * nsub + s_) % 2
                    dv = B.dma("sync", wb[idx], wv[t, :, 1024 * s_:1024 * s_ + 1024], "WQ%d" % idx, gate=cons_guard[idx])
                    B.waitg("scalar", [dv])
                    col = t * nsub + s_
                    gv = B.op("scalar", lambda e, i=idx, c0=col: e.activation(
                        out=sg[0], in_=wb[i], func=AF.Abs, accum_out=accs[:, c0:c0 + 1]))
                    cons_guard[idx] = [gv]
                    yield
            B.waitg("vector", [gv])
            gv = B.op("vector", lambda e, nc2=ncols: e.reduce_sum(out=accs[:, 32:33], in_=accs[:, 0:nc2], axis=AX.X))
            B.waitg("tensor", [gv])
            B.waitg("tensor", B.bguard[4])
            gt = B.op("tensor", lambda e: e.matmul(B.P[4][0:1, 0:1], accs[:, 32:33], B.ones_col_f, start=True, stop=True))
            B.waitg("vector", [gt])
            B.op("vector", lambda e, n=float(ntile * fdim * 128): e.tensor_scalar_mul(out=accs[0:1, 33:34], in0=B.P[4][0:1, 0:1], scalar1=1.0 / n))
            B.op("vector", lambda e: e.tensor_scalar_max(out=accs[0:1, 33:34], in0=accs[0:1, 33:34], scalar1=1e-5))
            gv = B.op("vector", lambda e: e.reciprocal(out=accs[0:1, 34:35], in_=accs[0:1, 33:34]))
            B.bguard[4] = [gv]
            B.waitg("tensor", [gv])
            B.waitg("tensor", B.bguard[5])
            gt = B.op("tensor", lambda e: e.matmul(B.P[5][0:128, 0:2], B.ones_row128[0:1, :], accs[0:1, 33:35], start=True, stop=True))
            B.waitg("vector", [gt])
            B.op("vector", lambda e, w=wname: e.tensor_copy(out=B.mav[w], in_=B.P[5][0:128, 0:1]))
            gv = B.op("vector", lambda e, w=wname: e.tensor_copy(out=B.wsv[w], in_=B.P[5][0:128, 1:2]))
            B.bguard[5] = [gv]
            B.waitg("scalar", [gv])
            yield
            mv = mdst.rearrange("(t p) f -> t p f", p=128)
            for t in range(ntile):
                for s_ in range(nsub):
                    idx = (t * nsub + s_) % 2
                    dv = B.dma("sync", wb[idx], wv[t, :, 1024 * s_:1024 * s_ + 1024], "WQ%d" % idx, gate=cons_guard[idx])
                    B.waitg("vector", [dv])
                    B.waitg("vector", sg_guard)
                    B.op("vector", lambda e, i=idx, w=wname: e.tensor_scalar(
                        out=sg[0], in0=wb[i], scalar1=B.wsv[w], scalar2=0.5, op0=ALU.mult, op1=ALU.is_ge))
                    g2 = B.op("vector", lambda e, i=idx, w=wname: e.tensor_scalar(
                        out=sg[1], in0=wb[i], scalar1=B.wsv[w], scalar2=-0.5, op0=ALU.mult, op1=ALU.is_le))
                    B.waitg("vector", mstore[idx])
                    gv = B.op("vector", lambda e, i=idx: e.tensor_sub(out=m2[i], in0=sg[0], in1=sg[1]))
                    sg_guard = [gv]
                    cons_guard[idx] = [gv]
                    B.waitg("gpsimd", [gv])
                    mstore[idx] = [B.dma("gpsimd", mv[t, :, 1024 * s_:1024 * s_ + 1024], m2[idx], "MS%d" % idx)]
                    yield
            self.mdone[wname] = mstore[0] + mstore[1]

    def pump(self, n=1):
        for _ in range(n):
            if next(self.wq_gen, "done") == "done":
                break

    def emit_v_unit(self, ts_):
        B = self
        xn_fm, wv_sb, v_sb = B.xn_fm_t, B.att_wv, B.att_v
        B.waitg("tensor", B.tx_done[ts_])
        if ts_ == 0:
            B.waitg("tensor", [B.att_wcast[7], B.att_vs_ones])
        for half in range(2):
            bank = (ts_ * 2 + half) % 4
            Pb = B.P[bank]
            B.waitg("tensor", B.bguard[bank])
            for c in range(ND):
                B.em("tensor", lambda e, b=Pb, cc=c, t=ts_, hh=half: e.matmul(
                    b[:], xn_fm[cc][:, 128 * t:128 * t + 128],
                    wv_sb[:, 1024 * cc + 512 * hh:1024 * cc + 512 * hh + 512],
                    start=(cc == 0), stop=False))
            tg = B.op("tensor", lambda e, b=Pb, t=ts_, hh=half: e.matmul(
                b[:], B.ones_row[0:1, 128 * t:128 * t + 128],
                B.bv_row[0:1, 512 * hh:512 * hh + 512], start=False, stop=True))
            B.waitg("vector", [tg])
            last = None
            for hs in range(8):
                hg = half * 8 + hs
                last = B.op("vector", lambda e, b=Pb, t=ts_, h1=hs, h2=hg: e.tensor_copy(
                    out=v_sb[:, 1040 * t + 65 * h2:1040 * t + 65 * h2 + 64],
                    in_=b[:, 64 * h1:64 * h1 + 64]))
            B.bguard[bank] = [last]

    def ph_ffn_gateup(self):
        B = self
        ba, fa = B.ba, B.fa
        ba.mark(); fa.mark()
        n_fm, mg = B.n_fm_t, B.mg_t
        B.mu67_off = ba.off
        mu = [ba.alloc_at(B.op_region + 4096 * c, 4096) for c in range(6)] + [ba.alloc(4096) for _ in range(2)]
        n2b = ba.alloc(2048)
        h2 = [fa.alloc(4096) for _ in range(2)]
        st = B.stat
        B.barrier()
        mu_a = ba.alloc_at(B.op_region, 6 * 4096)
        tru = [B.dma("sync", mu_a.rearrange("p (c t) -> p c t", c=6),
                     B.s_mu[:, 0:768].rearrange("t (c p) -> t c p", c=6), "TRU", transpose=True,
                     gate=B.mdone.get("u", []) + B.outproj_done)]
        mu_b = mu[6]
        tru.append(B.dma("sync", ba.t[0:128, mu_b.offset - 0:mu_b.offset + 2 * 4096].rearrange("p (c t) -> p c t", c=2)
                         if False else None, None, "TRU", transpose=True) if False else
                   B.dma("sync", ba.alloc_at(B.mu67_off, 2 * 4096).rearrange("p (c t) -> p c t", c=2),
                         B.s_mu[:, 768:1024].rearrange("t (c p) -> t c p", c=2), "TRU", transpose=True))
        B.op("vector", lambda e: e.tensor_scalar(out=B.dqg_all[:], in0=B.dq1_all[:], scalar1=B.mav["g"], scalar2=None, op0=ALU.mult))
        gdq = B.op("vector", lambda e: e.tensor_scalar(out=B.dqu_all[:], in0=B.dq1_all[:], scalar1=B.mav["u"], scalar2=None, op0=ALU.mult))
        B.waitg("scalar", [gdq])
        n2v = B.s_n2.rearrange("(t p) f -> t p f", p=128)
        hguard = [[], []]
        n2guard = []
        mu_waited = False
        for t in range(NT):
            h_sb = h2[t % 2]
            for oc in range(8):
                bank = oc % 4
                bg_ = B.P[bank]
                B.waitg("tensor", B.bguard[bank])
                for c in range(ND - 1):
                    B.em("tensor", lambda e, b=bg_, cc=c, tt=t, o=oc: e.matmul(
                        b[:], n_fm[cc][:, 128 * tt:128 * tt + 128],
                        mg[cc][:, 512 * o:512 * o + 512],
                        start=(cc == 0), stop=False))
                tg = B.op("tensor", lambda e, b=bg_, tt=t, o=oc: e.matmul(
                    b[:], n_fm[ND - 1][:, 128 * tt:128 * tt + 128],
                    mg[ND - 1][:, 512 * o:512 * o + 512],
                    start=False, stop=True))
                B.waitg("scalar", [tg])
                B.waitg("scalar", hguard[t % 2])
                g1 = B.op("scalar", lambda e, b=bg_, tt=t, o=oc, hs=h_sb: e.activation(
                    out=hs[:, 512 * o:512 * o + 512], in_=b[:], func=AF.Silu,
                    scale=B.dqg_all[:, tt:tt + 1]))
                B.bguard[bank] = [g1]
            if not mu_waited:
                B.waitg("tensor", tru)
                mu_waited = True
            for oc in range(8):
                bank = oc % 4
                bu_ = B.P[bank]
                B.waitg("tensor", B.bguard[bank])
                for c in range(ND - 1):
                    B.em("tensor", lambda e, b=bu_, cc=c, tt=t, o=oc: e.matmul(
                        b[:], n_fm[cc][:, 128 * tt:128 * tt + 128],
                        mu[cc][:, 512 * o:512 * o + 512],
                        start=(cc == 0), stop=False))
                tg = B.op("tensor", lambda e, b=bu_, tt=t, o=oc: e.matmul(
                    b[:], n_fm[ND - 1][:, 128 * tt:128 * tt + 128],
                    mu[ND - 1][:, 512 * o:512 * o + 512],
                    start=False, stop=True))
                B.waitg("vector", [tg])
                B.waitg("vector", hguard[t % 2] if oc == 0 else [])
                gh = B.op("vector", lambda e, b=bu_, tt=t, o=oc, hs=h_sb: e.scalar_tensor_tensor(
                    out=hs[:, 512 * o:512 * o + 512], in0=b[:],
                    scalar=B.dqu_all[:, tt:tt + 1], in1=hs[:, 512 * o:512 * o + 512],
                    op0=ALU.mult, op1=ALU.mult))
                B.bguard[bank] = [gh]
            sc = 2 + (t % 2)
            B.op("vector", lambda e, hs=h_sb, c0=sc: e.reduce_max(out=st[:, c0:c0 + 1], in_=hs, axis=AX.X, apply_absolute_value=True))
            B.op("vector", lambda e, c0=sc: e.tensor_scalar_max(out=st[:, c0:c0 + 1], in0=st[:, c0:c0 + 1], scalar1=1e-5))
            B.op("vector", lambda e, tt=t, c0=sc: e.tensor_scalar_mul(out=B.dq2_all[:, tt:tt + 1], in0=st[:, c0:c0 + 1], scalar1=1.0 / 127.0))
            B.op("vector", lambda e, c0=sc: e.reciprocal(out=st[:, c0:c0 + 1], in_=st[:, c0:c0 + 1]))
            B.op("vector", lambda e, c0=sc: e.tensor_scalar_mul(out=st[:, c0:c0 + 1], in0=st[:, c0:c0 + 1], scalar1=127.0))
            B.op("vector", lambda e, hs=h_sb, c0=sc: e.tensor_scalar(out=hs, in0=hs, scalar1=st[:, c0:c0 + 1], scalar2=MAGIC, op0=ALU.mult, op1=ALU.add))
            hguard[t % 2] = []
            for hf in range(2):
                B.waitg("vector", n2guard)
                gq = B.op("vector", lambda e, h3=hf, hs=h_sb: e.tensor_scalar(
                    out=n2b, in0=hs[:, 2048 * h3:2048 * h3 + 2048], scalar1=MAGIC, scalar2=None, op0=ALU.subtract))
                hguard[t % 2].append(gq)
                B.waitg("gpsimd", [gq])
                n2guard = [B.dma("gpsimd", n2v[t, :, 2048 * hf:2048 * hf + 2048], n2b, "SN2")]
        B.gateup_tail = [tg]
        ba.reset(); fa.reset()
        B.ba.reset()

    def ph_ffn_down(self):
        B = self
        ba, fa = B.ba, B.fa
        ba.reset()
        ba.mark(); fa.mark()
        md_all = ba.alloc(NO * 1024)
        md = [md_all[:, 1024 * c:1024 * c + 1024] for c in range(NO)]
        n2f_all = [ba.alloc(NO * 512) for _ in range(2)]
        n2f2 = [[n2f_all[p][:, 512 * c:512 * c + 512] for c in range(NO)] for p in range(2)]
        acc = [fa.alloc(1024) for _ in range(2)]
        ys = [fa.alloc(1024) for _ in range(2)]
        B.dma("sync", md_all.rearrange("p (c t) -> p c t", c=NO),
              B.s_md.rearrange("t (c p) -> t c p", c=NO), "DTR", transpose=True,
              gate=B.gateup_tail + B.mdone.get("d", []))
        B.op("vector", lambda e: e.tensor_scalar(out=B.ddq_all[:], in0=B.dq2_all[:], scalar1=B.mav["d"], scalar2=None, op0=ALU.mult))
        B.barrier()
        av = B.s_acc.rearrange("(t p) d -> t p d", p=128)
        yv = B.y.rearrange("(t p) d -> t p d", p=128)
        ld_guard = [[], []]
        st_guard = [[], []]
        tr_guard = [[], []]
        trs_done = [[], []]

        def load_n2(tg2):
            pp = tg2 % 2
            trs_done[pp] = [B.dma("sync", n2f_all[pp].rearrange("p (c t) -> p c t", c=NO),
                                  B.s_n2[512 * tg2:512 * tg2 + 512, :].rearrange("t (c p) -> t c p", c=NO),
                                  "TN%d" % pp, transpose=True, gate=tr_guard[pp])]
        load_n2(0)
        for tg_ in range(4):
            pp = tg_ % 2
            n2f = n2f2[pp]
            B.waitg("tensor", trs_done[pp])
            if tg_ < 3:
                load_n2(tg_ + 1)
            tr_guard[pp] = []
            for tl in range(4):
                t = 4 * tg_ + tl
                par = t % 2
                dv = B.dma("sync", acc[par], av[t], "LA%d" % par, gate=ld_guard[par])
                evs = []
                for half in range(2):
                    bank = (2 * t + half) % 4
                    Pb = B.P[bank]
                    B.waitg("tensor", B.bguard[bank])
                    for c in range(NO - 1):
                        B.em("tensor", lambda e, b=Pb, cc=c, lt=tl, hh=half, nf=n2f: e.matmul(
                            b[:], nf[cc][:, 128 * lt:128 * lt + 128],
                            md[cc][:, 512 * hh:512 * hh + 512],
                            start=(cc == 0), stop=False))
                    tgm = B.op("tensor", lambda e, b=Pb, lt=tl, hh=half, nf=n2f: e.matmul(
                        b[:], nf[NO - 1][:, 128 * lt:128 * lt + 128],
                        md[NO - 1][:, 512 * hh:512 * hh + 512],
                        start=False, stop=True))
                    tr_guard[pp].append(tgm)
                    B.waitg("vector", [tgm, dv])
                    B.waitg("vector", st_guard[par])
                    g = B.op("vector", lambda e, b=Pb, tt=t, hh=half, p=par: e.scalar_tensor_tensor(
                        out=ys[p][:, 512 * hh:512 * hh + 512], in0=b[:],
                        scalar=B.ddq_all[:, tt:tt + 1], in1=acc[p][:, 512 * hh:512 * hh + 512],
                        op0=ALU.mult, op1=ALU.add))
                    B.bguard[bank] = [g]
                    evs.append(g)
                ld_guard[par] = [evs[-1]]
                B.waitg("gpsimd", evs)
                st_guard[par] = [B.dma("gpsimd", yv[t], ys[par], "YS%d" % par)]
        ba.reset(); fa.reset()

    def finish(self):
        B = self
        for sname in list(self.C):
            if sname not in ENG_SEM.values() and self.C[sname] > 0:
                B.wait("gpsimd", sname, self.C[sname])

    def materialize(self):
        nc = self.nc
        with self.stack:
            with nc.Block() as block:
                def run(engine_name):
                    def body(eng):
                        for e, fn in self.sched:
                            if e == engine_name:
                                fn(eng)
                    return body
                block.sync(run("sync"))
                block.tensor(run("tensor"))
                block.vector(run("vector"))
                block.scalar(run("scalar"))
                block.gpsimd(run("gpsimd"))
        return nc


_BUILD_CACHE = {}


def _get_nc():
    if "nc" not in _BUILD_CACHE:
        _BUILD_CACHE["nc"] = Builder().build()
    return _BUILD_CACHE["nc"]


def kernel(L0_attn_x, L0_attn_w_in, L0_attn_b_in, L0_attn_w_out, L0_attn_b_out,
           L0_ffn_w_gate, L0_ffn_w_up, L0_ffn_w_down):
    x = np.ascontiguousarray(np.asarray(L0_attn_x, dtype=np.float32))
    w_in = np.ascontiguousarray(np.asarray(L0_attn_w_in, dtype=np.float32))
    b_in = np.ascontiguousarray(np.asarray(L0_attn_b_in, dtype=np.float32))
    w_out = np.ascontiguousarray(np.asarray(L0_attn_w_out, dtype=np.float32))
    b_out = np.ascontiguousarray(np.asarray(L0_attn_b_out, dtype=np.float32))
    w_gate = np.ascontiguousarray(np.asarray(L0_ffn_w_gate, dtype=np.float32))
    w_up = np.ascontiguousarray(np.asarray(L0_ffn_w_up, dtype=np.float32))
    w_down = np.ascontiguousarray(np.asarray(L0_ffn_w_down, dtype=np.float32))

    nc = _get_nc()
    in_maps = [dict(x=x[c], w_in=w_in, b_in=b_in, w_out=w_out, b_out=b_out,
                    w_gate=w_gate, w_up=w_up, w_down=w_down) for c in range(8)]
    res = run_bass_kernel_spmd(nc, in_maps, core_ids=list(range(8)))
    return np.stack([res.results[c]["y"] for c in range(8)], axis=0)


# version 20
# speedup vs baseline: 1.0180x; 1.0180x over previous
import contextlib
from collections import defaultdict

import numpy as np
import ml_dtypes

import concourse.bass as bass
import concourse.mybir as mybir
from concourse.bass_utils import run_bass_kernel_spmd

BFNP = ml_dtypes.bfloat16
F32 = mybir.dt.float32
BF16 = mybir.dt.bfloat16
AF = mybir.ActivationFunctionType
ALU = mybir.AluOpType
AX = mybir.AxisListType

L, D, H, DH, INNER = 2048, 1024, 16, 64, 4096
NT, ND, NO = L // 128, D // 128, INNER // 128
MAGIC = float(np.float32(1.5 * 2 ** 23))
EPS = 1e-5
SLOPES = [2.0 ** (-(h + 1) * 0.5) for h in range(H)]
GWH = [128, 256, 256, 512, 512] + [512] * 11
WTH = [127.0 + 45.0 / s for s in SLOPES]
ENG_SEM = {"tensor": "TS", "vector": "VS", "scalar": "SS", "gpsimd": "GS", "sync": "SY"}


def _attn_plan():
    plans = {}
    for h in range(H):
        gw, W = GWH[h], WTH[h]
        for qh in range(2):
            steps = []
            kiv = {j: [] for j in (2 * qh, 2 * qh + 1)}
            for ki in range(8 * qh + 8):
                k0 = 128 * ki
                for j in (2 * qh, 2 * qh + 1):
                    c0 = 512 * j
                    exps, zeros, mask = [], [], None
                    for qg0 in range(c0, c0 + 512, gw):
                        if qg0 + gw <= k0 or qg0 - k0 > W:
                            zeros.append((qg0, qg0 + gw))
                        else:
                            exps.append((qg0, qg0 - k0))
                            if k0 >= qg0:
                                if k0 > qg0:
                                    zeros.append((qg0, k0))
                                mask = k0
                    if exps:
                        wlo = min(q for q, _ in exps)
                        whi = max(q for q, _ in exps) + gw
                        steps.append(dict(ki=ki, j=j, wlo=wlo, whi=whi,
                                          exps=exps, zeros=zeros, mask=mask))
                        kiv[j].append(ki)
            plans[(h, qh)] = (steps, kiv)
    return plans


ATTN_PLANS = _attn_plan()
_EB_COLS = {}
for (_h, _qh), (_steps, _kiv) in ATTN_PLANS.items():
    for _st in _steps:
        for _qg0, _delta in _st["exps"]:
            _EB_COLS.setdefault((_h, _delta), len(_EB_COLS))


def _make_consts():
    kl = np.arange(128, dtype=np.float64)
    eb = np.zeros((128, max(8, len(_EB_COLS))), dtype=np.float32)
    for (h, delta), c in _EB_COLS.items():
        eb[:, c] = (SLOPES[h] * (kl - delta - GWH[h] / 2)).astype(np.float32)
    m01 = (np.arange(128)[:, None] <= np.arange(128)[None, :]).astype(BFNP)
    return eb, m01


class Arena:
    def __init__(self, nc, name, dtype, ncols):
        self.t = nc.alloc_sbuf_tensor(name, [128, ncols], dtype)
        self.n = ncols
        self.off = 0
        self.stack = []

    def alloc(self, cols, parts=128):
        assert self.off + cols <= self.n, f"arena overflow {self.off}+{cols}>{self.n}"
        ap = self.t[0:parts, self.off:self.off + cols]
        self.off += cols
        return ap

    def alloc_at(self, off, cols, parts=128):
        assert off + cols <= self.n
        return self.t[0:parts, off:off + cols]

    def mark(self):
        self.stack.append(self.off)

    def reset(self):
        self.off = self.stack.pop()


class Builder:
    def __init__(self):
        nc = bass.Bass("TRN2", target_bir_lowering=False)
        self.nc = nc
        self.sched = []
        self.C = defaultdict(int)
        self.stack = contextlib.ExitStack()
        self.sem = {s: self.stack.enter_context(nc.semaphore(s))
                    for s in ("TS", "VS", "SS", "GS", "SY")}

        d = nc.dram_tensor
        self.x = d("x", [L, D], F32, kind="ExternalInput").ap()
        self.w_in = d("w_in", [D, 3 * D], F32, kind="ExternalInput").ap()
        self.b_in = d("b_in", [3 * D], F32, kind="ExternalInput").ap()
        self.w_out = d("w_out", [D, D], F32, kind="ExternalInput").ap()
        self.b_out = d("b_out", [D], F32, kind="ExternalInput").ap()
        self.w_gate = d("w_gate", [INNER, D], F32, kind="ExternalInput").ap()
        self.w_up = d("w_up", [INNER, D], F32, kind="ExternalInput").ap()
        self.w_down = d("w_down", [D, INNER], F32, kind="ExternalInput").ap()
        self.y = d("y", [L, D], F32, kind="ExternalOutput").ap()
        self.s_xnb = d("s_xnb", [L, D], BF16).ap()
        self.s_n = d("s_n", [L, D], BF16).ap()
        self.s_attn = d("s_attn", [D, L], BF16).ap()
        self.s_acc = d("s_acc", [L, D], F32).ap()
        self.s_mg = d("s_mg", [INNER, D], BF16).ap()
        self.s_mu = d("s_mu", [INNER, D], BF16).ap()
        self.s_md = d("s_md", [D, INNER], BF16).ap()
        self.s_n2 = d("s_n2", [L, INNER], BF16).ap()
        self.s_d1 = d("s_d1", [4, 512], F32).ap()
        self.s_d2 = d("s_d2", [4, 1024], BF16).ap()

        eb_np, m01_np = _make_consts()
        self.c_eb = nc.inline_tensor(eb_np, name="c_eb")
        self.c_m01 = nc.inline_tensor(m01_np, name="c_m01")
        self.n_eb = eb_np.shape[1]

        self.fa = Arena(nc, "fa", F32, 8832)
        self.ba = Arena(nc, "ba", BF16, 88320)

        self.eb = self.fa.alloc(self.n_eb)
        self.cpool = self.fa.alloc(4)
        self.dq1_all = self.fa.alloc(NT)
        self.dq2_all = self.fa.alloc(NT)
        self.dqg_all = self.fa.alloc(NT)
        self.dqu_all = self.fa.alloc(NT)
        self.ddq_all = self.fa.alloc(NT)
        self.b_qk = self.fa.alloc(16)
        self.wsv = {w: self.fa.alloc(1) for w in "gud"}
        self.mav = {w: self.fa.alloc(1) for w in "gud"}
        self.stat = self.fa.alloc(8)
        self.ones_col_f = self.fa.alloc(1)
        self.ones_row128 = self.fa.alloc(128, parts=1)
        self.ones_row64 = self.ones_row128[0:1, 0:64]

        self.m01 = self.ba.alloc(128)
        self.ones_row = self.ba.alloc(2048, parts=1)
        self.bv_row = self.ba.alloc(1024, parts=1)
        self.bo_row = self.ba.alloc(1024, parts=1)

        self.mdone = {}
        self.wqacc = self.fa.alloc(36)

        self.P = [nc.alloc_psum_tensor(f"pb{i}", [128, 512], F32) for i in range(8)]
        self.bguard = {i: [] for i in range(8)}

    def em(self, engine, fn):
        self.sched.append((engine, fn))

    def _sem(self, name):
        if name not in self.sem:
            self.sem[name] = self.stack.enter_context(self.nc.semaphore(name))
        return self.sem[name]

    def wait(self, engine, sem, val):
        if val > 0:
            self.em(engine, lambda e, s=self._sem(sem), v=val: e.wait_ge(s, v))

    def waitg(self, engine, guard):
        best = {}
        for s, v in guard:
            best[s] = max(best.get(s, 0), v)
        for s, v in best.items():
            self.wait(engine, s, v)

    def dma(self, queue, out, in_, sem, transpose=False, slow=False, gate=()):
        sh = self._sem(sem)
        self.waitg(queue, gate)
        self.C[sem] += 16
        v = self.C[sem]
        if transpose:
            self.em(queue, lambda e, o=out, i=in_, s=sh: e.dma_start_transpose(out=o, in_=i).then_inc(s, 16))
        else:
            self.em(queue, lambda e, o=out, i=in_, s=sh, sl=slow: e.dma_start(out=o, in_=i, allow_slow_non_contiguous=sl).then_inc(s, 16))
        return (sem, v)

    def op(self, engine, fn):
        s = ENG_SEM[engine]
        prev = 0 if engine == "tensor" else self.C[s]
        self.C[s] += 1
        v = self.C[s]

        def run(e, f=fn, sh=self.sem[s], p=prev, vv=v):
            if p > 0:
                e.wait_ge(sh, p)
            return f(e).then_inc(sh, 1)
        self.em(engine, run)
        return (s, v)

    def barrier(self):
        vals = [(s, self.C[s]) for s in list(self.C) if self.C[s] > 0]
        for eng in ("sync", "tensor", "vector", "scalar", "gpsimd"):
            for s, v in vals:
                self.wait(eng, s, v)

    def build(self):
        self.ph_init()
        self.ph_ln_quant()
        self.ph_attention()
        self.ph_outproj()
        self.ph_ffn_gateup()
        self.ph_ffn_down()
        self.finish()
        return self.materialize()

    def ph_init(self):
        B = self
        B.dma("sync", B.eb, B.c_eb[:], "DIN")
        B.dma("sync", B.m01, B.c_m01[:], "DIN")
        bq = B.b_in[0:D].rearrange("(j p) -> p j", p=128)
        bk = B.b_in[D:2 * D].rearrange("(j p) -> p j", p=128)
        B.dma("sync", B.b_qk[:, 0:8], bq, "DIN", slow=True)
        B.dma("sync", B.b_qk[:, 8:16], bk, "DIN", slow=True)
        B.fa.mark()
        bvf = B.fa.alloc(1024, parts=1)
        bof = B.fa.alloc(1024, parts=1)
        B.dma("sync", bvf, B.b_in[2 * D:3 * D][None, :], "DIN")
        g = B.dma("sync", bof, B.b_out[None, :], "DIN")
        B.wait("vector", "DIN", g[1])
        B.op("vector", lambda e: e.memset(B.cpool[:, 0:1], -0.5))
        B.op("vector", lambda e: e.memset(B.cpool[:, 1:2], 0.5))
        B.op("vector", lambda e: e.memset(B.cpool[:, 2:3], 0.125))
        B.op("vector", lambda e: e.memset(B.cpool[:, 3:4], MAGIC))
        B.op("vector", lambda e: e.memset(B.ones_col_f, 1.0))
        B.op("vector", lambda e: e.memset(B.ones_row128, 1.0))
        B.op("vector", lambda e: e.memset(B.ones_row, 1.0))
        B.op("vector", lambda e, s=bvf: e.tensor_copy(out=B.bv_row, in_=s))
        B.op("vector", lambda e, s=bof: e.tensor_copy(out=B.bo_row, in_=s))
        B.op("vector", lambda e: e.tensor_scalar_mul(out=B.b_qk[:, 0:8], in0=B.b_qk[:, 0:8], scalar1=0.125))
        B.fa.reset()

    def ph_ln_quant(self):
        B = self
        B.barrier()
        fa, ba = B.fa, B.ba
        ba.mark()
        B.op_region = ba.off
        B.afm_t = [ba.alloc(2048) for _ in range(ND)]
        B.wo_sb = ba.alloc(ND * 1024)
        ba.mark()
        B.xnfm_all = ba.alloc(ND * 2048)
        B.xn_fm_t = [B.xnfm_all[:, 2048 * c:2048 * c + 2048] for c in range(ND)]
        B.wqm2 = [ba.alloc(1024) for _ in range(2)]
        B.wqsg = [ba.alloc(1024) for _ in range(2)]
        fa.mark()
        B.wqb = [fa.alloc(1024) for _ in range(2)]
        B.wq_gen = B.wquant_gen(B.wqb, B.wqm2, B.wqsg)
        B.att_wv_region = ba.off
        B.att_wv = ba.alloc(8 * 1024)
        B.att_v = ba.alloc(NT * 1040)
        B.att_wqk = [ba.alloc(2 * 1024) for _ in range(2)]
        B.att_wstage = [fa.alloc(1024) for _ in range(2)]
        B.att_stg_cons = [None, None]
        wvv0 = B.w_in[:, 2 * D:3 * D].rearrange("(c p) f -> c p f", p=128)
        wov0 = B.w_out.rearrange("(c p) f -> c p f", p=128)
        B.att_wcast = []
        for c in range(ND):
            sl = c % 2
            dv = B.dma("sync", B.att_wstage[sl], wvv0[c], "WST%d" % sl,
                       gate=[B.att_stg_cons[sl]] if B.att_stg_cons[sl] else [])
            B.waitg("gpsimd", [dv])
            B.att_stg_cons[sl] = B.op("gpsimd", lambda e, cc=c, s2=sl: e.tensor_copy(
                out=B.att_wv[:, 1024 * cc:1024 * cc + 1024], in_=B.att_wstage[s2]))
            B.att_wcast.append(B.att_stg_cons[sl])
        for c in range(ND):
            sl = c % 2
            dv = B.dma("sync", B.att_wstage[sl], wov0[c], "WST%d" % sl,
                       gate=[B.att_stg_cons[sl]] if B.att_stg_cons[sl] else [])
            B.waitg("gpsimd", [dv])
            B.att_stg_cons[sl] = B.op("gpsimd", lambda e, cc=c, s2=sl: e.tensor_copy(
                out=B.wo_sb[:, 1024 * cc:1024 * cc + 1024], in_=B.att_wstage[s2]))
        B.att_vs_ones = B.op("vector", lambda e: e.memset(B.att_v, 1.0))
        fa.mark(); ba.mark()
        xt = [fa.alloc(1024) for _ in range(2)]
        xn = [fa.alloc(1024) for _ in range(2)]
        dump = xn[0]
        ls = fa.alloc(8 * 16)
        nb = [ba.alloc(1024) for _ in range(2)]
        xb = [ba.alloc(1024) for _ in range(2)]
        xv = B.x.rearrange("(t p) d -> t p d", p=128)
        nv = B.s_n.rearrange("(t p) d -> t p d", p=128)

        def C(i):
            return ls[:, 16 * i:16 * i + 16]

        def Ct(i, t):
            return ls[:, 16 * i + t:16 * i + t + 1]

        sq_cons = [[], []]
        gq = None
        for t in range(NT):
            par = t % 2
            dv = B.dma("sync", xt[par], xv[t], "LX%d" % par, gate=sq_cons[par])
            B.waitg("scalar", [dv])
            B.op("scalar", lambda e, p=par, tt=t: e.activation(
                out=dump, in_=xt[p], func=AF.Identity, accum_out=Ct(0, tt)))
            gq = B.op("scalar", lambda e, p=par, tt=t: e.activation(
                out=dump, in_=xt[p], func=AF.Square, accum_out=Ct(1, tt)))
            sq_cons[par] = [gq]
            B.pump(2)
        B.waitg("vector", [gq])
        B.op("vector", lambda e: e.tensor_scalar_mul(out=C(2), in0=C(0), scalar1=1.0 / 1024))
        B.op("vector", lambda e: e.tensor_scalar_mul(out=C(3), in0=C(1), scalar1=1.0 / 1024))
        B.op("vector", lambda e: e.tensor_mul(out=C(5), in0=C(2), in1=C(2)))
        B.op("vector", lambda e: e.tensor_sub(out=C(3), in0=C(3), in1=C(5)))
        B.op("vector", lambda e: e.tensor_scalar_add(out=C(3), in0=C(3), scalar1=EPS))
        g = B.op("vector", lambda e: e.reciprocal(out=C(4), in_=C(3)))
        B.waitg("scalar", [g])
        g = B.op("scalar", lambda e: e.activation(out=C(4), in_=C(4), func=AF.Sqrt))
        B.waitg("vector", [g])
        for _ in range(2):
            B.op("vector", lambda e: e.tensor_mul(out=C(5), in0=C(4), in1=C(4)))
            B.op("vector", lambda e: e.tensor_mul(out=C(5), in0=C(5), in1=C(3)))
            B.op("vector", lambda e: e.tensor_scalar(out=C(5), in0=C(5), scalar1=-0.5, scalar2=1.5, op0=ALU.mult, op1=ALU.add))
            B.op("vector", lambda e: e.tensor_mul(out=C(4), in0=C(4), in1=C(5)))
        B.op("vector", lambda e: e.tensor_mul(out=C(5), in0=C(2), in1=C(4)))
        gmr = B.op("vector", lambda e: e.tensor_scalar_mul(out=C(5), in0=C(5), scalar1=-1.0))
        B.waitg("scalar", [gmr])
        B.tx_done = {}
        sguard = [[], []]
        xt_cons = [list(sq_cons[0]), list(sq_cons[1])]
        xn_cons = [[], []]
        for t in range(NT):
            par = t % 2
            dv = B.dma("sync", xt[par], xv[t], "LX%d" % par, gate=xt_cons[par])
            B.waitg("scalar", [dv])
            B.waitg("scalar", xn_cons[par])
            gap = B.op("scalar", lambda e, p=par, tt=t: e.activation(
                out=xn[p], in_=xt[p], func=AF.Identity, scale=Ct(4, tt), bias=Ct(5, tt)))
            B.waitg("scalar", sguard[par])
            gx = B.op("scalar", lambda e, p=par: e.copy(out=xb[p], in_=xn[p]))
            B.waitg("vector", [gap])
            gab = B.op("vector", lambda e, p=par, tt=t: e.reduce_max(out=Ct(6, tt), in_=xn[p], axis=AX.X, apply_absolute_value=True))
            B.op("vector", lambda e, tt=t: e.tensor_scalar_max(out=Ct(6, tt), in0=Ct(6, tt), scalar1=1e-5))
            B.op("vector", lambda e, tt=t: e.tensor_scalar_mul(out=B.dq1_all[:, tt:tt + 1], in0=Ct(6, tt), scalar1=1.0 / 127.0))
            B.op("vector", lambda e, tt=t: e.reciprocal(out=Ct(7, tt), in_=Ct(6, tt)))
            gs1 = B.op("vector", lambda e, tt=t: e.tensor_scalar_mul(out=Ct(7, tt), in0=Ct(7, tt), scalar1=127.0))
            B.waitg("scalar", [gs1])
            gy = B.op("scalar", lambda e, p=par, tt=t: e.activation(
                out=xt[p], in_=xn[p], func=AF.Identity, scale=Ct(7, tt), bias=B.cpool[:, 3:4]))
            B.waitg("vector", [gy])
            gn = B.op("vector", lambda e, p=par: e.tensor_scalar(out=nb[p], in0=xt[p], scalar1=MAGIC, scalar2=None, op0=ALU.subtract))
            xt_cons[par] = [gn]
            xn_cons[par] = [gy, gab]
            B.waitg("gpsimd", [gn])
            s2 = B.dma("gpsimd", nv[t], nb[par], "LS%d" % par)
            txg = [B.dma("sync", B.xnfm_all.rearrange("p (c t) -> p c t", c=ND)[:, :, 128 * t:128 * t + 128],
                         xb[par].rearrange("t (c p) -> t c p", c=ND),
                         "TX%d" % par, transpose=True, gate=[gx])]
            sguard[par] = [s2] + txg
            B.tx_done[t] = txg
            B.emit_v_unit(t)
            B.pump(2)
        B.ln_final = sguard[0] + sguard[1]
        fa.reset(); ba.reset()

    def ph_attention(self):
        B = self
        ba, fa = B.ba, B.fa
        ba.mark(); fa.mark()
        xn_fm = B.xn_fm_t
        wv_region = B.att_wv_region
        wv_sb = B.att_wv
        v_sb = B.att_v
        wqk = B.att_wqk
        q_fm2 = [ba.alloc_at(wv_region, 2048), ba.alloc_at(wv_region + 4096, 2048)]
        k_fm2 = [ba.alloc_at(wv_region + 2048, 2048), ba.alloc_at(wv_region + 6144, 2048)]
        p_sb = [[ba.alloc(512) for i in range(3)] for l in range(2)]
        o_attn = ba.alloc(1024)
        o_un = [ba.alloc(512, parts=65) for _ in range(8)]
        srow = fa.alloc(2048, parts=1)
        oun_guard = [[] for _ in range(8)]
        s4_all = fa.alloc(16)
        s4s = [s4_all[:, 4 * k:4 * k + 4] for k in range(4)]
        r4b = fa.alloc(4)
        hif = fa.alloc(4)
        vjk = fa.alloc(4)
        rhl_all = ba.alloc(32)
        rhls = [rhl_all[:, 8 * k:8 * k + 8] for k in range(4)]
        r2blk = fa.alloc(1024, parts=2).bitcast(BF16)
        r2s = [r2blk[0:2, 512 * k:512 * k + 512] for k in range(4)]
        ones2 = fa.alloc(32, parts=2).bitcast(BF16)
        g_srow = [[] for _ in range(4)]
        g_s4 = [[] for _ in range(4)]
        g_rhlw = [[] for _ in range(4)]
        g_d1 = [[] for _ in range(4)]
        g_d2 = [[] for _ in range(4)]
        g_r2 = [[] for _ in range(4)]
        last_na = {0: [], 1: []}
        last_nb = {0: [], 1: []}
        last_nq = {0: [], 1: []}
        wstage2 = B.att_wstage
        stg_cons = B.att_stg_cons
        tv = [("TX%d" % p, B.C["TX%d" % p]) for p in range(2)]
        wcast = B.att_wcast
        B.store_done = {h: [] for h in range(H)}
        vs_ones = B.att_vs_ones
        B.waitg("tensor", [wcast[-1], vs_ones])
        SPOOL = {0: (2, 3, 4), 1: (5, 6, 7)}
        OBANK = {0: 0, 1: 1}
        wqk_guard = [[], []]
        qk_guard = []
        pv_guard = [[[], [], []], [[], [], []]]
        sa_guard = [[], []]
        norm_pending = []
        oun_next = [0]
        ev_ctr = [0]
        B.gstep = 0

        def emit_norm(maxn=10 ** 6, due=None):
            n = 0
            while norm_pending and n < maxn:
                if due is not None and norm_pending[0][0] > due:
                    return
                norm_pending.pop(0)[2]()
                n += 1

        def load_wqk(hp2):
            par2 = hp2 % 2
            B.waitg("sync", wqk_guard[par2])
            out = []
            for qk, c0 in ((0, 128 * hp2), (1, D + 128 * hp2)):
                sl = qk
                wvw = B.w_in[:, c0:c0 + 128].rearrange("(c p) f -> p c f", p=128)
                dv = B.dma("sync", wstage2[sl].rearrange("p (c f) -> p c f", f=128), wvw, "WST%d" % sl,
                           gate=[stg_cons[sl]] if stg_cons[sl] else [])
                B.waitg("gpsimd", [dv])
                stg_cons[sl] = B.op("gpsimd", lambda e, p=par2, q=qk, s2=sl: e.tensor_copy(
                    out=wqk[p][:, 1024 * q:1024 * q + 1024], in_=wstage2[s2]))
                out.append(stg_cons[sl])
            return out

        B.waitg("tensor", tv)
        B.waitg("scalar", B.ln_final)
        B.waitg("vector", B.ln_final)
        B.ln_final = None
        g12 = B.op("vector", lambda e: e.memset(ones2[:], 1.0))
        B.op("vector", lambda e: e.memset(vjk[:], 0.0))
        B.waitg("tensor", [g12])
        wq_cast_next = load_wqk(0)
        qk_guard2 = [[], []]
        for hp in range(8):
            par = hp % 2
            q_fm, k_fm = q_fm2[par], k_fm2[par]
            wq_cast = wq_cast_next
            B.waitg("tensor", [wq_cast[-1]])
            qk_ready = []
            for qk in range(2):
                for tc in range(4):
                    bank = (qk * 4 + tc) % 4
                    Pb = B.P[bank]
                    B.waitg("tensor", B.bguard[bank])
                    for c in range(ND - 1):
                        B.em("tensor", lambda e, b=Pb, cc=c, q=qk, t=tc, p=par: e.matmul(
                            b[:], wqk[p][:, 1024 * q + 128 * cc:1024 * q + 128 * cc + 128],
                            xn_fm[cc][:, 512 * t:512 * t + 512],
                            start=(cc == 0), stop=False))
                    tg = B.op("tensor", lambda e, b=Pb, q=qk, t=tc, p=par: e.matmul(
                        b[:], wqk[p][:, 1024 * q + 128 * (ND - 1):1024 * q + 128 * ND],
                        xn_fm[ND - 1][:, 512 * t:512 * t + 512],
                        start=False, stop=True))
                    B.waitg("scalar", [tg])
                    B.waitg("scalar", qk_guard2[par])
                    qk_guard2[par] = []
                    dst = q_fm if qk == 0 else k_fm
                    bias = B.b_qk[:, 8 * qk + hp:8 * qk + hp + 1]
                    if qk == 0:
                        g = B.op("scalar", lambda e, b=Pb, t=tc, dd=dst, bi=bias: e.activation(
                            out=dd[:, 512 * t:512 * t + 512], in_=b[:], func=AF.Identity,
                            scale=B.cpool[:, 2:3], bias=bi))
                    else:
                        g = B.op("scalar", lambda e, b=Pb, t=tc, dd=dst, bi=bias: e.activation(
                            out=dd[:, 512 * t:512 * t + 512], in_=b[:], func=AF.Identity, bias=bi))
                    B.bguard[bank] = [g]
                    qk_ready.append(g)
            B.waitg("tensor", [qk_ready[-1]])
            if hp < 7:
                wq_cast_next = load_wqk(hp + 1)
            if hp >= 2:
                c2 = hp - 2
                B.dma("sync", B.afm_t[c2], B.s_attn[128 * c2:128 * c2 + 128, :], "AFM",
                      gate=B.store_done[2 * c2] + B.store_done[2 * c2 + 1])
            qk_guard = []
            for qh in range(2):
                locs = [0, 1]
                plans = {loc: ATTN_PLANS[(2 * hp + loc, qh)] for loc in locs}
                keys = sorted({(st["j"], st["ki"]) for loc in locs for st in plans[loc][0]})
                stepmap = {loc: {(st["j"], st["ki"]): st for st in plans[loc][0]} for loc in locs}
                ready = {}

                def emit_v(si2, keys=keys, stepmap=stepmap, plans=plans, ready=ready, hp=hp, qh=qh):
                    sp2 = si2 % 3
                    infos = []
                    for loc2 in locs:
                        info = ready.pop((loc2, si2), None)
                        if info is None:
                            continue
                        ki2, j2, eg2, vg2, pbuf2, h2 = info
                        kivj = plans[loc2][1][j2]
                        first = (ki2 == kivj[0])
                        lastk = (ki2 == kivj[-1])
                        B.waitg("tensor", eg2)
                        B.waitg("tensor", vg2)
                        if first:
                            B.waitg("tensor", B.bguard[OBANK[loc2]])
                        infos.append((loc2, ki2, j2, pbuf2, h2, first, lastk))
                    for loc2, ki2, j2, pbuf2, h2, first, lastk in infos:
                        obank = OBANK[loc2]
                        g = B.op("tensor", lambda e, ob=B.P[obank], k=ki2, hh=h2, pb=pbuf2, f1=first, l1=lastk: e.matmul(
                            ob[0:65, :],
                            v_sb[:, 1040 * k + 65 * hh:1040 * k + 65 * hh + 65],
                            pb[:], start=f1, stop=l1))
                        pv_guard[loc2][sp2] = [g]
                        if lastk:
                            _norm_part1(loc2, j2, hp, qh, [g])

                def _norm_part1(loc2, j2, hp2, qh2, vlast):
                    evid = ev_ctr[0]
                    ev_ctr[0] += 1
                    while norm_pending and norm_pending[0][1] <= evid - 4:
                        emit_norm(1)
                    h2 = 2 * hp2 + loc2
                    jj = j2 % 2
                    obank = OBANK[loc2]
                    Ob = B.P[obank]
                    pu = oun_next[0]
                    oun_next[0] = (pu + 1) % 8
                    B.waitg("scalar", vlast)
                    B.waitg("scalar", oun_guard[pu])
                    ru = pu % 4
                    B.waitg("scalar", g_srow[ru])
                    gs0 = B.op("scalar", lambda e, b=Ob, r2=ru: e.copy(
                        out=srow[0:1, 512 * r2:512 * r2 + 512], in_=b[64:65, :]))
                    gs_ = B.op("scalar", lambda e, b=Ob, p2=pu: e.copy(out=o_un[p2][0:64, :], in_=b[0:64, :]))
                    B.bguard[obank] = [gs_]
                    d1a = B.dma("scalar", B.s_d1[ru], srow[0:1, 512 * ru:512 * ru + 512],
                                "NA%d" % (ru % 2), gate=g_d1[ru] + last_na[ru % 2])
                    last_na[ru % 2] = [d1a]
                    g_srow[ru] = [d1a]
                    d1b = B.dma("gpsimd", s4s[ru], B.s_d1[ru].rearrange("(p c) -> p c", c=4),
                                "NB%d" % (ru % 2), gate=[d1a] + g_s4[ru] + last_nb[ru % 2])
                    g_d1[ru] = [d1b]
                    last_nb[ru % 2] = [d1b]
                    st0 = B.gstep
                    ev = {}

                    def part2a(ru3=ru, d1g=[d1b]):
                        B.waitg("vector", d1g)
                        B.waitg("vector", g_rhlw[ru3])
                        B.op("vector", lambda e, r3=ru3: e.reciprocal(out=r4b[:], in_=s4s[r3]))
                        B.op("vector", lambda e: e.memset(vjk[:], 0.0))
                        B.op("vector", lambda e, r3=ru3: e.tensor_copy(out=rhls[r3][:, 0:4], in_=r4b[:]))
                        B.op("vector", lambda e: e.memset(vjk[:], 0.0))
                        B.op("vector", lambda e: e.tensor_copy(out=hif[:], in_=r4b[:]))
                        B.op("vector", lambda e, r3=ru3: e.tensor_copy(out=hif[:], in_=rhls[r3][:, 0:4]))
                        B.op("vector", lambda e: e.memset(vjk[:], 0.0))
                        B.op("vector", lambda e, r3=ru3: e.tensor_tensor(
                            out=rhls[r3][:, 4:8], in0=r4b[:], in1=hif[:], op=ALU.subtract))
                        B.op("vector", lambda e: e.memset(vjk[:], 0.0))
                        gva = B.op("vector", lambda e: e.memset(vjk[:], 0.0))
                        g_s4[ru3] = [gva]
                        ev["gva"] = gva

                    def part2b(ru3=ru):
                        d2a = B.dma("sync", B.s_d2[ru3].rearrange("(p c) -> p c", c=8), rhls[ru3],
                                    "NQ%d" % (ru3 % 2), gate=[ev["gva"]] + g_d2[ru3] + last_nq[ru3 % 2])
                        g_rhlw[ru3] = [d2a]
                        d2b = B.dma("sync", r2s[ru3].rearrange("t (p c) -> t p c", c=4),
                                    B.s_d2[ru3].rearrange("(p t2 c) -> t2 p c", t2=2, c=4),
                                    "NQ%d" % (ru3 % 2), gate=[d2a] + g_r2[ru3])
                        g_d2[ru3] = [d2b]
                        last_nq[ru3 % 2] = [d2b]
                        ev["d2b"] = d2b

                    def part2c(loc3=loc2, h3=h2, jj3=jj, qh3=qh2, pu3=pu, ru3=ru, gs3=gs_):
                        rbank = SPOOL[loc3][0]
                        Rb = B.P[rbank]
                        B.waitg("tensor", [ev["d2b"]])
                        B.waitg("tensor", B.bguard[rbank])
                        gt = B.op("tensor", lambda e, b=Rb, r3=ru3: e.matmul(
                            b[0:64, :], ones2[:], r2s[r3], start=True, stop=True))
                        g_r2[ru3] = [gt]
                        B.waitg("vector", [gt, gs3])
                        B.waitg("vector", sa_guard[jj3])
                        gv = B.op("vector", lambda e, b=Rb, p2=pu3, j2=jj3: e.tensor_tensor(
                            out=o_attn[0:64, 512 * j2:512 * j2 + 512],
                            in0=o_un[p2][0:64, :], in1=b[0:64, :], op=ALU.mult))
                        oun_guard[pu3] = [gv]
                        B.bguard[rbank] = [gv]
                        B.waitg("gpsimd", [gv])
                        sa_guard[jj3] = [B.dma("gpsimd",
                            B.s_attn[64 * h3:64 * h3 + 64, 1024 * qh3 + 512 * jj3:1024 * qh3 + 512 * jj3 + 512],
                            o_attn[0:64, 512 * jj3:512 * jj3 + 512], "SA%d" % jj3)]
                        B.store_done[h3] = B.store_done[h3] + sa_guard[jj3]
                    norm_pending.append((st0 + 6, evid, part2a))
                    norm_pending.append((st0 + 9, evid, part2b))
                    norm_pending.append((st0 + 13, evid, part2c))

                final_blk = (hp == 7 and qh == 1)
                for si, (j, ki) in enumerate(keys):
                    sp = si % 3
                    B.gstep += 1
                    if final_blk:
                        emit_norm(10 ** 6)
                    else:
                        emit_norm(2, due=B.gstep)
                    for loc in locs:
                        st_ = stepmap[loc].get((j, ki))
                        if st_ is None:
                            continue
                        h = 2 * hp + loc
                        sbank = SPOOL[loc][sp]
                        Sb = B.P[sbank]
                        wlo, whi = st_["wlo"], st_["whi"]
                        B.waitg("tensor", B.bguard[sbank])
                        tg = B.op("tensor", lambda e, b=Sb, l=loc, k=ki, a=wlo, z=whi, kf=k_fm, qf=q_fm: e.matmul(
                            b[0:128, 0:z - a],
                            kf[64 * l:64 * l + 64, 128 * k:128 * k + 128],
                            qf[64 * l:64 * l + 64, a:z],
                            start=True, stop=True, tile_position=(64 * l, 0)))
                        qk_guard.append(tg)
                        B.waitg("scalar", [tg])
                        B.waitg("scalar", pv_guard[loc][sp])
                        pbuf = p_sb[loc][sp]
                        eg = []
                        for qg0, delta in st_["exps"]:
                            gw = GWH[h]
                            col = _EB_COLS[(h, delta)]
                            eg.append(B.op("scalar", lambda e, b=Sb, q=qg0, g0=gw, a=wlo, c=col, pb=pbuf, jj=j: e.activation(
                                out=pb[:, q - 512 * jj:q - 512 * jj + g0],
                                in_=b[0:128, q - a:q - a + g0],
                                func=AF.Exp, bias=B.eb[:, c:c + 1])))
                        B.bguard[sbank] = [eg[-1]]
                        vg = []
                        if st_["zeros"] or st_["mask"] is not None:
                            B.waitg("vector", [eg[-1]])
                            B.waitg("vector", pv_guard[loc][sp])
                        for z0, z1 in st_["zeros"]:
                            vg.append(B.op("vector", lambda e, a=z0, b2=z1, pb=pbuf, jj=j: e.memset(
                                pb[:, a - 512 * jj:b2 - 512 * jj], 0.0)))
                        if st_["mask"] is not None:
                            mk = st_["mask"]
                            vg.append(B.op("vector", lambda e, m=mk, pb=pbuf, jj=j: e.tensor_tensor(
                                out=pb[:, m - 512 * jj:m - 512 * jj + 128],
                                in0=pb[:, m - 512 * jj:m - 512 * jj + 128],
                                in1=B.m01[:], op=ALU.mult)))
                        ready[(loc, si)] = (ki, j, eg[-1:], vg[-1:], pbuf, h)
                    if si % 2 == 1:
                        B.pump(1)
                    if si >= 2:
                        emit_v(si - 2)
                if len(keys) >= 2:
                    emit_v(len(keys) - 2)
                emit_v(len(keys) - 1)
                B.pump(2)
            wqk_guard[par] = list(qk_guard[-1:])
            qk_guard2[par] = list(qk_guard[-1:])
            qk_guard = []
        emit_norm(10 ** 6)
        B.pump(10 ** 6)
        ba.reset(); fa.reset()
        B.ba.reset()
        B.fa.reset()

    def ph_outproj(self):
        B = self
        ba, fa = B.ba, B.fa
        ba.mark()
        B.nfm_all = ba.alloc(ND * 2048)
        B.n_fm_t = [B.nfm_all[:, 2048 * c:2048 * c + 2048] for c in range(ND)]
        B.mg_all = ba.alloc(ND * 4096)
        B.mg_t = [B.mg_all[:, 4096 * c:4096 * c + 4096] for c in range(ND)]
        ba.mark(); fa.mark()
        attn_fm = B.afm_t
        wo_sb = B.wo_sb
        xt = [fa.alloc(1024) for _ in range(2)]
        acc = [fa.alloc(1024) for _ in range(2)]
        for c in (6, 7):
            B.dma("sync", attn_fm[c], B.s_attn[128 * c:128 * c + 128, :], "AFM",
                  gate=B.store_done[2 * c] + B.store_done[2 * c + 1])
        B.barrier()
        B.tr_n = B.dma("sync", B.nfm_all.rearrange("p (c t) -> p c t", c=ND),
                       B.s_n.rearrange("t (c p) -> t c p", c=ND), "TRN", transpose=True)
        B.tr_g = B.dma("sync", B.mg_all.rearrange("p (c t) -> p c t", c=ND),
                       B.s_mg.rearrange("t (c p) -> t c p", c=ND), "TRG", transpose=True,
                       gate=B.mdone.get("g", []))
        B.barrier()
        xv = B.x.rearrange("(t p) d -> t p d", p=128)
        av = B.s_acc.rearrange("(t p) d -> t p d", p=128)
        ld_guard = [[], []]
        stg = [[], []]
        for t in range(NT):
            par = t % 2
            dv = B.dma("sync", xt[par], xv[t], "LX%d" % par, gate=ld_guard[par])
            evs = []
            for half in range(2):
                bank = (2 * t + half) % 4
                Pb = B.P[bank]
                B.waitg("tensor", B.bguard[bank])
                for c in range(ND):
                    B.em("tensor", lambda e, b=Pb, cc=c, tt=t, hh=half: e.matmul(
                        b[:], attn_fm[cc][:, 128 * tt:128 * tt + 128],
                        wo_sb[:, 1024 * cc + 512 * hh:1024 * cc + 512 * hh + 512],
                        start=(cc == 0), stop=False))
                tg = B.op("tensor", lambda e, b=Pb, tt=t, hh=half: e.matmul(
                    b[:], B.ones_row[0:1, 128 * tt:128 * tt + 128],
                    B.bo_row[0:1, 512 * hh:512 * hh + 512], start=False, stop=True))
                B.waitg("vector", [tg, dv])
                B.waitg("vector", stg[par])
                g = B.op("vector", lambda e, b=Pb, hh=half, p=par: e.tensor_tensor(
                    out=acc[p][:, 512 * hh:512 * hh + 512],
                    in0=xt[p][:, 512 * hh:512 * hh + 512], in1=b[:], op=ALU.add))
                B.bguard[bank] = [g]
                evs.append(g)
            ld_guard[par] = [evs[-1]]
            B.waitg("gpsimd", evs)
            stg[par] = [B.dma("gpsimd", av[t], acc[par], "LS%d" % par)]
        ba.reset(); fa.reset()
        B.outproj_done = [evs[-1]]

    def wquant_gen(self, wb, m2, sg):
        B = self
        accs = B.wqacc
        specs = [("g", B.w_gate, B.s_mg, 32, 1024), ("u", B.w_up, B.s_mu, 32, 1024),
                 ("d", B.w_down, B.s_md, 8, 4096)]
        cons_guard = [[], []]
        mstore = [[], []]
        sg_guard = []
        for wname, wsrc, mdst, ntile, fdim in specs:
            nsub = fdim // 1024
            wv = wsrc.rearrange("(t p) f -> t p f", p=128)
            gv = None
            ncols = ntile * nsub
            B.waitg("scalar", sg_guard)
            for t in range(ntile):
                for s_ in range(nsub):
                    idx = (t * nsub + s_) % 2
                    dv = B.dma("sync", wb[idx], wv[t, :, 1024 * s_:1024 * s_ + 1024], "WQ%d" % idx, gate=cons_guard[idx])
                    B.waitg("scalar", [dv])
                    col = t * nsub + s_
                    gv = B.op("scalar", lambda e, i=idx, c0=col: e.activation(
                        out=sg[0], in_=wb[i], func=AF.Abs, accum_out=accs[:, c0:c0 + 1]))
                    cons_guard[idx] = [gv]
                    yield
            B.waitg("vector", [gv])
            gv = B.op("vector", lambda e, nc2=ncols: e.reduce_sum(out=accs[:, 32:33], in_=accs[:, 0:nc2], axis=AX.X))
            B.waitg("tensor", [gv])
            B.waitg("tensor", B.bguard[4])
            gt = B.op("tensor", lambda e: e.matmul(B.P[4][0:1, 0:1], accs[:, 32:33], B.ones_col_f, start=True, stop=True))
            B.waitg("vector", [gt])
            B.op("vector", lambda e, n=float(ntile * fdim * 128): e.tensor_scalar_mul(out=accs[0:1, 33:34], in0=B.P[4][0:1, 0:1], scalar1=1.0 / n))
            B.op("vector", lambda e: e.tensor_scalar_max(out=accs[0:1, 33:34], in0=accs[0:1, 33:34], scalar1=1e-5))
            gv = B.op("vector", lambda e: e.reciprocal(out=accs[0:1, 34:35], in_=accs[0:1, 33:34]))
            B.bguard[4] = [gv]
            B.waitg("tensor", [gv])
            B.waitg("tensor", B.bguard[5])
            gt = B.op("tensor", lambda e: e.matmul(B.P[5][0:128, 0:2], B.ones_row128[0:1, :], accs[0:1, 33:35], start=True, stop=True))
            B.waitg("vector", [gt])
            B.op("vector", lambda e, w=wname: e.tensor_copy(out=B.mav[w], in_=B.P[5][0:128, 0:1]))
            gv = B.op("vector", lambda e, w=wname: e.tensor_copy(out=B.wsv[w], in_=B.P[5][0:128, 1:2]))
            B.bguard[5] = [gv]
            B.waitg("scalar", [gv])
            yield
            mv = mdst.rearrange("(t p) f -> t p f", p=128)
            for t in range(ntile):
                for s_ in range(nsub):
                    idx = (t * nsub + s_) % 2
                    dv = B.dma("sync", wb[idx], wv[t, :, 1024 * s_:1024 * s_ + 1024], "WQ%d" % idx, gate=cons_guard[idx])
                    B.waitg("vector", [dv])
                    B.waitg("vector", sg_guard)
                    B.op("vector", lambda e, i=idx, w=wname: e.tensor_scalar(
                        out=sg[0], in0=wb[i], scalar1=B.wsv[w], scalar2=0.5, op0=ALU.mult, op1=ALU.is_ge))
                    g2 = B.op("vector", lambda e, i=idx, w=wname: e.tensor_scalar(
                        out=sg[1], in0=wb[i], scalar1=B.wsv[w], scalar2=-0.5, op0=ALU.mult, op1=ALU.is_le))
                    B.waitg("vector", mstore[idx])
                    gv = B.op("vector", lambda e, i=idx: e.tensor_sub(out=m2[i], in0=sg[0], in1=sg[1]))
                    sg_guard = [gv]
                    cons_guard[idx] = [gv]
                    B.waitg("gpsimd", [gv])
                    mstore[idx] = [B.dma("gpsimd", mv[t, :, 1024 * s_:1024 * s_ + 1024], m2[idx], "MS%d" % idx)]
                    yield
            self.mdone[wname] = mstore[0] + mstore[1]

    def pump(self, n=1):
        for _ in range(n):
            if next(self.wq_gen, "done") == "done":
                break

    def emit_v_unit(self, ts_):
        B = self
        xn_fm, wv_sb, v_sb = B.xn_fm_t, B.att_wv, B.att_v
        B.waitg("tensor", B.tx_done[ts_])
        if ts_ == 0:
            B.waitg("tensor", [B.att_wcast[7], B.att_vs_ones])
        for half in range(2):
            bank = (ts_ * 2 + half) % 4
            Pb = B.P[bank]
            B.waitg("tensor", B.bguard[bank])
            for c in range(ND):
                B.em("tensor", lambda e, b=Pb, cc=c, t=ts_, hh=half: e.matmul(
                    b[:], xn_fm[cc][:, 128 * t:128 * t + 128],
                    wv_sb[:, 1024 * cc + 512 * hh:1024 * cc + 512 * hh + 512],
                    start=(cc == 0), stop=False))
            tg = B.op("tensor", lambda e, b=Pb, t=ts_, hh=half: e.matmul(
                b[:], B.ones_row[0:1, 128 * t:128 * t + 128],
                B.bv_row[0:1, 512 * hh:512 * hh + 512], start=False, stop=True))
            B.waitg("vector", [tg])
            last = None
            for hs in range(8):
                hg = half * 8 + hs
                last = B.op("vector", lambda e, b=Pb, t=ts_, h1=hs, h2=hg: e.tensor_copy(
                    out=v_sb[:, 1040 * t + 65 * h2:1040 * t + 65 * h2 + 64],
                    in_=b[:, 64 * h1:64 * h1 + 64]))
            B.bguard[bank] = [last]

    def ph_ffn_gateup(self):
        B = self
        ba, fa = B.ba, B.fa
        ba.mark(); fa.mark()
        n_fm, mg = B.n_fm_t, B.mg_t
        B.mu67_off = ba.off
        mu = [ba.alloc_at(B.op_region + 4096 * c, 4096) for c in range(6)] + [ba.alloc(4096) for _ in range(2)]
        n2b = ba.alloc(2048)
        h2 = [fa.alloc(4096) for _ in range(2)]
        st = B.stat
        B.barrier()
        mu_a = ba.alloc_at(B.op_region, 6 * 4096)
        tru = [B.dma("sync", mu_a.rearrange("p (c t) -> p c t", c=6),
                     B.s_mu[:, 0:768].rearrange("t (c p) -> t c p", c=6), "TRU", transpose=True,
                     gate=B.mdone.get("u", []) + B.outproj_done)]
        mu_b = mu[6]
        tru.append(B.dma("sync", ba.t[0:128, mu_b.offset - 0:mu_b.offset + 2 * 4096].rearrange("p (c t) -> p c t", c=2)
                         if False else None, None, "TRU", transpose=True) if False else
                   B.dma("sync", ba.alloc_at(B.mu67_off, 2 * 4096).rearrange("p (c t) -> p c t", c=2),
                         B.s_mu[:, 768:1024].rearrange("t (c p) -> t c p", c=2), "TRU", transpose=True))
        B.op("vector", lambda e: e.tensor_scalar(out=B.dqg_all[:], in0=B.dq1_all[:], scalar1=B.mav["g"], scalar2=None, op0=ALU.mult))
        gdq = B.op("vector", lambda e: e.tensor_scalar(out=B.dqu_all[:], in0=B.dq1_all[:], scalar1=B.mav["u"], scalar2=None, op0=ALU.mult))
        B.waitg("scalar", [gdq])
        n2v = B.s_n2.rearrange("(t p) f -> t p f", p=128)
        hguard = [[], []]
        n2guard = []
        mu_waited = False
        for t in range(NT):
            h_sb = h2[t % 2]
            for oc in range(8):
                bank = oc % 4
                bg_ = B.P[bank]
                B.waitg("tensor", B.bguard[bank])
                for c in range(ND - 1):
                    B.em("tensor", lambda e, b=bg_, cc=c, tt=t, o=oc: e.matmul(
                        b[:], n_fm[cc][:, 128 * tt:128 * tt + 128],
                        mg[cc][:, 512 * o:512 * o + 512],
                        start=(cc == 0), stop=False))
                tg = B.op("tensor", lambda e, b=bg_, tt=t, o=oc: e.matmul(
                    b[:], n_fm[ND - 1][:, 128 * tt:128 * tt + 128],
                    mg[ND - 1][:, 512 * o:512 * o + 512],
                    start=False, stop=True))
                B.waitg("scalar", [tg])
                B.waitg("scalar", hguard[t % 2])
                g1 = B.op("scalar", lambda e, b=bg_, tt=t, o=oc, hs=h_sb: e.activation(
                    out=hs[:, 512 * o:512 * o + 512], in_=b[:], func=AF.Silu,
                    scale=B.dqg_all[:, tt:tt + 1]))
                B.bguard[bank] = [g1]
            if not mu_waited:
                B.waitg("tensor", tru)
                mu_waited = True
            for oc in range(8):
                bank = oc % 4
                bu_ = B.P[bank]
                B.waitg("tensor", B.bguard[bank])
                for c in range(ND - 1):
                    B.em("tensor", lambda e, b=bu_, cc=c, tt=t, o=oc: e.matmul(
                        b[:], n_fm[cc][:, 128 * tt:128 * tt + 128],
                        mu[cc][:, 512 * o:512 * o + 512],
                        start=(cc == 0), stop=False))
                tg = B.op("tensor", lambda e, b=bu_, tt=t, o=oc: e.matmul(
                    b[:], n_fm[ND - 1][:, 128 * tt:128 * tt + 128],
                    mu[ND - 1][:, 512 * o:512 * o + 512],
                    start=False, stop=True))
                B.waitg("vector", [tg])
                B.waitg("vector", hguard[t % 2] if oc == 0 else [])
                gh = B.op("vector", lambda e, b=bu_, tt=t, o=oc, hs=h_sb: e.scalar_tensor_tensor(
                    out=hs[:, 512 * o:512 * o + 512], in0=b[:],
                    scalar=B.dqu_all[:, tt:tt + 1], in1=hs[:, 512 * o:512 * o + 512],
                    op0=ALU.mult, op1=ALU.mult))
                B.bguard[bank] = [gh]
            sc = 2 + (t % 2)
            B.op("vector", lambda e, hs=h_sb, c0=sc: e.reduce_max(out=st[:, c0:c0 + 1], in_=hs, axis=AX.X, apply_absolute_value=True))
            B.op("vector", lambda e, c0=sc: e.tensor_scalar_max(out=st[:, c0:c0 + 1], in0=st[:, c0:c0 + 1], scalar1=1e-5))
            B.op("vector", lambda e, tt=t, c0=sc: e.tensor_scalar_mul(out=B.dq2_all[:, tt:tt + 1], in0=st[:, c0:c0 + 1], scalar1=1.0 / 127.0))
            B.op("vector", lambda e, c0=sc: e.reciprocal(out=st[:, c0:c0 + 1], in_=st[:, c0:c0 + 1]))
            B.op("vector", lambda e, c0=sc: e.tensor_scalar_mul(out=st[:, c0:c0 + 1], in0=st[:, c0:c0 + 1], scalar1=127.0))
            B.op("vector", lambda e, hs=h_sb, c0=sc: e.tensor_scalar(out=hs, in0=hs, scalar1=st[:, c0:c0 + 1], scalar2=MAGIC, op0=ALU.mult, op1=ALU.add))
            hguard[t % 2] = []
            for hf in range(2):
                B.waitg("vector", n2guard)
                gq = B.op("vector", lambda e, h3=hf, hs=h_sb: e.tensor_scalar(
                    out=n2b, in0=hs[:, 2048 * h3:2048 * h3 + 2048], scalar1=MAGIC, scalar2=None, op0=ALU.subtract))
                hguard[t % 2].append(gq)
                B.waitg("gpsimd", [gq])
                n2guard = [B.dma("gpsimd", n2v[t, :, 2048 * hf:2048 * hf + 2048], n2b, "SN2")]
        B.gateup_tail = [tg]
        ba.reset(); fa.reset()
        B.ba.reset()

    def ph_ffn_down(self):
        B = self
        ba, fa = B.ba, B.fa
        ba.reset()
        ba.mark(); fa.mark()
        md_all = ba.alloc(NO * 1024)
        md = [md_all[:, 1024 * c:1024 * c + 1024] for c in range(NO)]
        n2f_all = [ba.alloc(NO * 512) for _ in range(2)]
        n2f2 = [[n2f_all[p][:, 512 * c:512 * c + 512] for c in range(NO)] for p in range(2)]
        acc = [fa.alloc(1024) for _ in range(2)]
        ys = [fa.alloc(1024) for _ in range(2)]
        B.dma("sync", md_all.rearrange("p (c t) -> p c t", c=NO),
              B.s_md.rearrange("t (c p) -> t c p", c=NO), "DTR", transpose=True,
              gate=B.gateup_tail + B.mdone.get("d", []))
        B.op("vector", lambda e: e.tensor_scalar(out=B.ddq_all[:], in0=B.dq2_all[:], scalar1=B.mav["d"], scalar2=None, op0=ALU.mult))
        B.barrier()
        av = B.s_acc.rearrange("(t p) d -> t p d", p=128)
        yv = B.y.rearrange("(t p) d -> t p d", p=128)
        ld_guard = [[], []]
        st_guard = [[], []]
        tr_guard = [[], []]
        trs_done = [[], []]

        def load_n2(tg2):
            pp = tg2 % 2
            trs_done[pp] = [B.dma("sync", n2f_all[pp].rearrange("p (c t) -> p c t", c=NO),
                                  B.s_n2[512 * tg2:512 * tg2 + 512, :].rearrange("t (c p) -> t c p", c=NO),
                                  "TN%d" % pp, transpose=True, gate=tr_guard[pp])]
        load_n2(0)
        for tg_ in range(4):
            pp = tg_ % 2
            n2f = n2f2[pp]
            B.waitg("tensor", trs_done[pp])
            if tg_ < 3:
                load_n2(tg_ + 1)
            tr_guard[pp] = []
            for tl in range(4):
                t = 4 * tg_ + tl
                par = t % 2
                dv = B.dma("sync", acc[par], av[t], "LA%d" % par, gate=ld_guard[par])
                evs = []
                for half in range(2):
                    bank = (2 * t + half) % 4
                    Pb = B.P[bank]
                    B.waitg("tensor", B.bguard[bank])
                    for c in range(NO - 1):
                        B.em("tensor", lambda e, b=Pb, cc=c, lt=tl, hh=half, nf=n2f: e.matmul(
                            b[:], nf[cc][:, 128 * lt:128 * lt + 128],
                            md[cc][:, 512 * hh:512 * hh + 512],
                            start=(cc == 0), stop=False))
                    tgm = B.op("tensor", lambda e, b=Pb, lt=tl, hh=half, nf=n2f: e.matmul(
                        b[:], nf[NO - 1][:, 128 * lt:128 * lt + 128],
                        md[NO - 1][:, 512 * hh:512 * hh + 512],
                        start=False, stop=True))
                    tr_guard[pp].append(tgm)
                    B.waitg("vector", [tgm, dv])
                    B.waitg("vector", st_guard[par])
                    g = B.op("vector", lambda e, b=Pb, tt=t, hh=half, p=par: e.scalar_tensor_tensor(
                        out=ys[p][:, 512 * hh:512 * hh + 512], in0=b[:],
                        scalar=B.ddq_all[:, tt:tt + 1], in1=acc[p][:, 512 * hh:512 * hh + 512],
                        op0=ALU.mult, op1=ALU.add))
                    B.bguard[bank] = [g]
                    evs.append(g)
                ld_guard[par] = [evs[-1]]
                B.waitg("gpsimd", evs)
                st_guard[par] = [B.dma("gpsimd", yv[t], ys[par], "YS%d" % par)]
        ba.reset(); fa.reset()

    def finish(self):
        B = self
        for sname in list(self.C):
            if sname not in ENG_SEM.values() and self.C[sname] > 0:
                B.wait("gpsimd", sname, self.C[sname])

    def materialize(self):
        nc = self.nc
        with self.stack:
            with nc.Block() as block:
                def run(engine_name):
                    def body(eng):
                        for e, fn in self.sched:
                            if e == engine_name:
                                fn(eng)
                    return body
                block.sync(run("sync"))
                block.tensor(run("tensor"))
                block.vector(run("vector"))
                block.scalar(run("scalar"))
                block.gpsimd(run("gpsimd"))
        return nc


_BUILD_CACHE = {}


def _get_nc():
    if "nc" not in _BUILD_CACHE:
        _BUILD_CACHE["nc"] = Builder().build()
    return _BUILD_CACHE["nc"]


def kernel(L0_attn_x, L0_attn_w_in, L0_attn_b_in, L0_attn_w_out, L0_attn_b_out,
           L0_ffn_w_gate, L0_ffn_w_up, L0_ffn_w_down):
    x = np.ascontiguousarray(np.asarray(L0_attn_x, dtype=np.float32))
    w_in = np.ascontiguousarray(np.asarray(L0_attn_w_in, dtype=np.float32))
    b_in = np.ascontiguousarray(np.asarray(L0_attn_b_in, dtype=np.float32))
    w_out = np.ascontiguousarray(np.asarray(L0_attn_w_out, dtype=np.float32))
    b_out = np.ascontiguousarray(np.asarray(L0_attn_b_out, dtype=np.float32))
    w_gate = np.ascontiguousarray(np.asarray(L0_ffn_w_gate, dtype=np.float32))
    w_up = np.ascontiguousarray(np.asarray(L0_ffn_w_up, dtype=np.float32))
    w_down = np.ascontiguousarray(np.asarray(L0_ffn_w_down, dtype=np.float32))

    nc = _get_nc()
    in_maps = [dict(x=x[c], w_in=w_in, b_in=b_in, w_out=w_out, b_out=b_out,
                    w_gate=w_gate, w_up=w_up, w_down=w_down) for c in range(8)]
    res = run_bass_kernel_spmd(nc, in_maps, core_ids=list(range(8)))
    return np.stack([res.results[c]["y"] for c in range(8)], axis=0)


# version 22
# speedup vs baseline: 1.0500x; 1.0315x over previous
import contextlib
from collections import defaultdict

import numpy as np
import ml_dtypes

import concourse.bass as bass
import concourse.mybir as mybir
from concourse.bass_utils import run_bass_kernel_spmd

BFNP = ml_dtypes.bfloat16
F32 = mybir.dt.float32
BF16 = mybir.dt.bfloat16
AF = mybir.ActivationFunctionType
ALU = mybir.AluOpType
AX = mybir.AxisListType

L, D, H, DH, INNER = 2048, 1024, 16, 64, 4096
NT, ND, NO = L // 128, D // 128, INNER // 128
MAGIC = float(np.float32(1.5 * 2 ** 23))
EPS = 1e-5
SLOPES = [2.0 ** (-(h + 1) * 0.5) for h in range(H)]
GWH = [128, 256, 256, 512, 512] + [512] * 11
WTH = [127.0 + 45.0 / s for s in SLOPES]
ENG_SEM = {"tensor": "TS", "vector": "VS", "scalar": "SS", "gpsimd": "GS", "sync": "SY"}


def _attn_plan():
    plans = {}
    for h in range(H):
        gw, W = GWH[h], WTH[h]
        for qh in range(2):
            steps = []
            kiv = {j: [] for j in (2 * qh, 2 * qh + 1)}
            for ki in range(8 * qh + 8):
                k0 = 128 * ki
                for j in (2 * qh, 2 * qh + 1):
                    c0 = 512 * j
                    exps, zeros, mask = [], [], None
                    for qg0 in range(c0, c0 + 512, gw):
                        if qg0 + gw <= k0 or qg0 - k0 > W:
                            zeros.append((qg0, qg0 + gw))
                        else:
                            exps.append((qg0, qg0 - k0))
                            if k0 >= qg0:
                                if k0 > qg0:
                                    zeros.append((qg0, k0))
                                mask = k0
                    if exps:
                        wlo = min(q for q, _ in exps)
                        whi = max(q for q, _ in exps) + gw
                        steps.append(dict(ki=ki, j=j, wlo=wlo, whi=whi,
                                          exps=exps, zeros=zeros, mask=mask))
                        kiv[j].append(ki)
            plans[(h, qh)] = (steps, kiv)
    return plans


ATTN_PLANS = _attn_plan()
_EB_COLS = {}
for (_h, _qh), (_steps, _kiv) in ATTN_PLANS.items():
    for _st in _steps:
        for _qg0, _delta in _st["exps"]:
            _EB_COLS.setdefault((_h, _delta), len(_EB_COLS))


def _make_consts():
    kl = np.arange(128, dtype=np.float64)
    eb = np.zeros((128, max(8, len(_EB_COLS))), dtype=np.float32)
    for (h, delta), c in _EB_COLS.items():
        eb[:, c] = (SLOPES[h] * (kl - delta - GWH[h] / 2)).astype(np.float32)
    m01 = (np.arange(128)[:, None] <= np.arange(128)[None, :]).astype(BFNP)
    return eb, m01


class Arena:
    def __init__(self, nc, name, dtype, ncols):
        self.t = nc.alloc_sbuf_tensor(name, [128, ncols], dtype)
        self.n = ncols
        self.off = 0
        self.stack = []

    def alloc(self, cols, parts=128):
        assert self.off + cols <= self.n, f"arena overflow {self.off}+{cols}>{self.n}"
        ap = self.t[0:parts, self.off:self.off + cols]
        self.off += cols
        return ap

    def alloc_at(self, off, cols, parts=128):
        assert off + cols <= self.n
        return self.t[0:parts, off:off + cols]

    def mark(self):
        self.stack.append(self.off)

    def reset(self):
        self.off = self.stack.pop()


class Builder:
    def __init__(self):
        nc = bass.Bass("TRN2", target_bir_lowering=False)
        self.nc = nc
        self.sched = []
        self.C = defaultdict(int)
        self.stack = contextlib.ExitStack()
        self.sem = {s: self.stack.enter_context(nc.semaphore(s))
                    for s in ("TS", "VS", "SS", "GS", "SY")}

        d = nc.dram_tensor
        self.x = d("x", [L, D], F32, kind="ExternalInput").ap()
        self.w_in = d("w_in", [D, 3 * D], F32, kind="ExternalInput").ap()
        self.b_in = d("b_in", [3 * D], F32, kind="ExternalInput").ap()
        self.w_out = d("w_out", [D, D], F32, kind="ExternalInput").ap()
        self.b_out = d("b_out", [D], F32, kind="ExternalInput").ap()
        self.w_gate = d("w_gate", [INNER, D], F32, kind="ExternalInput").ap()
        self.w_up = d("w_up", [INNER, D], F32, kind="ExternalInput").ap()
        self.w_down = d("w_down", [D, INNER], F32, kind="ExternalInput").ap()
        self.y = d("y", [L, D], F32, kind="ExternalOutput").ap()
        self.s_xnb = d("s_xnb", [L, D], BF16).ap()
        self.s_n = d("s_n", [L, D], BF16).ap()
        self.s_attn = d("s_attn", [D, L], BF16).ap()
        self.s_acc = d("s_acc", [L, D], F32).ap()
        self.s_mg = d("s_mg", [INNER, D], BF16).ap()
        self.s_mu = d("s_mu", [INNER, D], BF16).ap()
        self.s_md = d("s_md", [D, INNER], BF16).ap()
        self.s_n2 = d("s_n2", [L, INNER], BF16).ap()
        self.s_d1 = d("s_d1", [4, 512], F32).ap()
        self.s_d2 = d("s_d2", [4, 1024], BF16).ap()

        eb_np, m01_np = _make_consts()
        self.c_eb = nc.inline_tensor(eb_np, name="c_eb")
        self.c_m01 = nc.inline_tensor(m01_np, name="c_m01")
        self.n_eb = eb_np.shape[1]

        self.fa = Arena(nc, "fa", F32, 8832)
        self.ba = Arena(nc, "ba", BF16, 88320)

        self.eb = self.fa.alloc(self.n_eb)
        self.cpool = self.fa.alloc(4)
        self.dq1_all = self.fa.alloc(NT)
        self.dq2_all = self.fa.alloc(NT)
        self.dqg_all = self.fa.alloc(NT)
        self.dqu_all = self.fa.alloc(NT)
        self.ddq_all = self.fa.alloc(NT)
        self.b_qk = self.fa.alloc(16)
        self.wsv = {w: self.fa.alloc(1) for w in "gud"}
        self.mav = {w: self.fa.alloc(1) for w in "gud"}
        self.stat = self.fa.alloc(8)
        self.ones_col_f = self.fa.alloc(1)
        self.ones_row128 = self.fa.alloc(128, parts=1)
        self.ones_row64 = self.ones_row128[0:1, 0:64]

        self.m01 = self.ba.alloc(128)
        self.ones_row = self.ba.alloc(2048, parts=1)
        self.bv_row = self.ba.alloc(1024, parts=1)
        self.bo_row = self.ba.alloc(1024, parts=1)

        self.mdone = {}
        self.wqacc = self.fa.alloc(36)

        self.P = [nc.alloc_psum_tensor(f"pb{i}", [128, 512], F32) for i in range(8)]
        self.bguard = {i: [] for i in range(8)}

    def em(self, engine, fn):
        self.sched.append((engine, fn))

    def _sem(self, name):
        if name not in self.sem:
            self.sem[name] = self.stack.enter_context(self.nc.semaphore(name))
        return self.sem[name]

    def wait(self, engine, sem, val):
        if val > 0:
            self.em(engine, lambda e, s=self._sem(sem), v=val: e.wait_ge(s, v))

    def waitg(self, engine, guard):
        best = {}
        for s, v in guard:
            best[s] = max(best.get(s, 0), v)
        for s, v in best.items():
            self.wait(engine, s, v)

    def dma(self, queue, out, in_, sem, transpose=False, slow=False, gate=()):
        sh = self._sem(sem)
        self.waitg(queue, gate)
        self.C[sem] += 16
        v = self.C[sem]
        if transpose:
            self.em(queue, lambda e, o=out, i=in_, s=sh: e.dma_start_transpose(out=o, in_=i).then_inc(s, 16))
        else:
            self.em(queue, lambda e, o=out, i=in_, s=sh, sl=slow: e.dma_start(out=o, in_=i, allow_slow_non_contiguous=sl).then_inc(s, 16))
        return (sem, v)

    def op(self, engine, fn):
        s = ENG_SEM[engine]
        prev = 0 if engine == "tensor" else self.C[s]
        self.C[s] += 1
        v = self.C[s]

        def run(e, f=fn, sh=self.sem[s], p=prev, vv=v):
            if p > 0:
                e.wait_ge(sh, p)
            return f(e).then_inc(sh, 1)
        self.em(engine, run)
        return (s, v)

    def barrier(self):
        vals = [(s, self.C[s]) for s in list(self.C) if self.C[s] > 0]
        for eng in ("sync", "tensor", "vector", "scalar", "gpsimd"):
            for s, v in vals:
                self.wait(eng, s, v)

    def build(self):
        self.ph_init()
        self.ph_ln_quant()
        self.ph_attention()
        self.ph_outproj()
        self.ph_ffn_gateup()
        self.ph_ffn_down()
        self.finish()
        return self.materialize()

    def ph_init(self):
        B = self
        B.dma("sync", B.eb, B.c_eb[:], "DIN")
        B.dma("sync", B.m01, B.c_m01[:], "DIN")
        bq = B.b_in[0:D].rearrange("(j p) -> p j", p=128)
        bk = B.b_in[D:2 * D].rearrange("(j p) -> p j", p=128)
        B.dma("sync", B.b_qk[:, 0:8], bq, "DIN", slow=True)
        B.dma("sync", B.b_qk[:, 8:16], bk, "DIN", slow=True)
        B.fa.mark()
        bvf = B.fa.alloc(1024, parts=1)
        bof = B.fa.alloc(1024, parts=1)
        B.dma("sync", bvf, B.b_in[2 * D:3 * D][None, :], "DIN")
        g = B.dma("sync", bof, B.b_out[None, :], "DIN")
        B.wait("vector", "DIN", g[1])
        B.op("vector", lambda e: e.memset(B.cpool[:, 0:1], -0.5))
        B.op("vector", lambda e: e.memset(B.cpool[:, 1:2], 0.5))
        B.op("vector", lambda e: e.memset(B.cpool[:, 2:3], 0.125))
        B.op("vector", lambda e: e.memset(B.cpool[:, 3:4], MAGIC))
        B.op("vector", lambda e: e.memset(B.ones_col_f, 1.0))
        B.op("vector", lambda e: e.memset(B.ones_row128, 1.0))
        B.op("vector", lambda e: e.memset(B.ones_row, 1.0))
        B.op("vector", lambda e, s=bvf: e.tensor_copy(out=B.bv_row, in_=s))
        B.op("vector", lambda e, s=bof: e.tensor_copy(out=B.bo_row, in_=s))
        B.op("vector", lambda e: e.tensor_scalar_mul(out=B.b_qk[:, 0:8], in0=B.b_qk[:, 0:8], scalar1=0.125))
        B.fa.reset()

    def ph_ln_quant(self):
        B = self
        B.barrier()
        fa, ba = B.fa, B.ba
        ba.mark()
        B.op_region = ba.off
        B.afm_t = [ba.alloc(2048) for _ in range(ND)]
        B.wo_sb = ba.alloc(ND * 1024)
        ba.mark()
        B.xnfm_all = ba.alloc(ND * 2048)
        B.xn_fm_t = [B.xnfm_all[:, 2048 * c:2048 * c + 2048] for c in range(ND)]
        B.wqm2 = [ba.alloc(1024) for _ in range(2)]
        B.wqsg = [ba.alloc(1024) for _ in range(2)]
        fa.mark()
        B.wqb = [fa.alloc(1024) for _ in range(2)]
        B.wq_gen = B.wquant_gen(B.wqb, B.wqm2, B.wqsg)
        B.att_wv_region = ba.off
        B.att_wv = ba.alloc(8 * 1024)
        B.att_v = ba.alloc(NT * 1040)
        B.att_wqk = [ba.alloc(2 * 1024) for _ in range(2)]
        B.att_wstage = [fa.alloc(1024) for _ in range(2)]
        B.att_stg_cons = [None, None]
        wvv0 = B.w_in[:, 2 * D:3 * D].rearrange("(c p) f -> c p f", p=128)
        wov0 = B.w_out.rearrange("(c p) f -> c p f", p=128)
        B.att_wcast = []
        for c in range(ND):
            sl = c % 2
            dv = B.dma("sync", B.att_wstage[sl], wvv0[c], "WST%d" % sl,
                       gate=[B.att_stg_cons[sl]] if B.att_stg_cons[sl] else [])
            B.waitg("gpsimd", [dv])
            B.att_stg_cons[sl] = B.op("gpsimd", lambda e, cc=c, s2=sl: e.tensor_copy(
                out=B.att_wv[:, 1024 * cc:1024 * cc + 1024], in_=B.att_wstage[s2]))
            B.att_wcast.append(B.att_stg_cons[sl])
        for c in range(ND):
            sl = c % 2
            dv = B.dma("sync", B.att_wstage[sl], wov0[c], "WST%d" % sl,
                       gate=[B.att_stg_cons[sl]] if B.att_stg_cons[sl] else [])
            B.waitg("gpsimd", [dv])
            B.att_stg_cons[sl] = B.op("gpsimd", lambda e, cc=c, s2=sl: e.tensor_copy(
                out=B.wo_sb[:, 1024 * cc:1024 * cc + 1024], in_=B.att_wstage[s2]))
        B.att_vs_ones = B.op("vector", lambda e: e.memset(B.att_v, 1.0))
        fa.mark(); ba.mark()
        xt = [fa.alloc(1024) for _ in range(2)]
        xn = [fa.alloc(1024) for _ in range(2)]
        dump = xn[0]
        ls = fa.alloc(8 * 16)
        nb = [ba.alloc(1024) for _ in range(2)]
        xb = [ba.alloc(1024) for _ in range(2)]
        xv = B.x.rearrange("(t p) d -> t p d", p=128)
        nv = B.s_n.rearrange("(t p) d -> t p d", p=128)

        def C(i):
            return ls[:, 16 * i:16 * i + 16]

        def Ct(i, t):
            return ls[:, 16 * i + t:16 * i + t + 1]

        sq_cons = [[], []]
        gq = None
        for t in range(NT):
            par = t % 2
            dv = B.dma("sync", xt[par], xv[t], "LX%d" % par, gate=sq_cons[par])
            B.waitg("scalar", [dv])
            B.op("scalar", lambda e, p=par, tt=t: e.activation(
                out=dump, in_=xt[p], func=AF.Identity, accum_out=Ct(0, tt)))
            gq = B.op("scalar", lambda e, p=par, tt=t: e.activation(
                out=dump, in_=xt[p], func=AF.Square, accum_out=Ct(1, tt)))
            sq_cons[par] = [gq]
            B.pump(2)
        B.waitg("vector", [gq])
        B.op("vector", lambda e: e.tensor_scalar_mul(out=C(2), in0=C(0), scalar1=1.0 / 1024))
        B.op("vector", lambda e: e.tensor_scalar_mul(out=C(3), in0=C(1), scalar1=1.0 / 1024))
        B.op("vector", lambda e: e.tensor_mul(out=C(5), in0=C(2), in1=C(2)))
        B.op("vector", lambda e: e.tensor_sub(out=C(3), in0=C(3), in1=C(5)))
        B.op("vector", lambda e: e.tensor_scalar_add(out=C(3), in0=C(3), scalar1=EPS))
        g = B.op("vector", lambda e: e.reciprocal(out=C(4), in_=C(3)))
        B.waitg("scalar", [g])
        g = B.op("scalar", lambda e: e.activation(out=C(4), in_=C(4), func=AF.Sqrt))
        B.waitg("vector", [g])
        for _ in range(2):
            B.op("vector", lambda e: e.tensor_mul(out=C(5), in0=C(4), in1=C(4)))
            B.op("vector", lambda e: e.tensor_mul(out=C(5), in0=C(5), in1=C(3)))
            B.op("vector", lambda e: e.tensor_scalar(out=C(5), in0=C(5), scalar1=-0.5, scalar2=1.5, op0=ALU.mult, op1=ALU.add))
            B.op("vector", lambda e: e.tensor_mul(out=C(4), in0=C(4), in1=C(5)))
        B.op("vector", lambda e: e.tensor_mul(out=C(5), in0=C(2), in1=C(4)))
        gmr = B.op("vector", lambda e: e.tensor_scalar_mul(out=C(5), in0=C(5), scalar1=-1.0))
        B.waitg("scalar", [gmr])
        B.tx_done = {}
        sguard = [[], []]
        xt_cons = [list(sq_cons[0]), list(sq_cons[1])]
        xn_cons = [[], []]
        for t in range(NT):
            par = t % 2
            dv = B.dma("sync", xt[par], xv[t], "LX%d" % par, gate=xt_cons[par])
            B.waitg("scalar", [dv])
            B.waitg("scalar", xn_cons[par])
            gap = B.op("scalar", lambda e, p=par, tt=t: e.activation(
                out=xn[p], in_=xt[p], func=AF.Identity, scale=Ct(4, tt), bias=Ct(5, tt)))
            B.waitg("scalar", sguard[par])
            gx = B.op("scalar", lambda e, p=par: e.copy(out=xb[p], in_=xn[p]))
            B.waitg("vector", [gap])
            gab = B.op("vector", lambda e, p=par, tt=t: e.reduce_max(out=Ct(6, tt), in_=xn[p], axis=AX.X, apply_absolute_value=True))
            B.op("vector", lambda e, tt=t: e.tensor_scalar_max(out=Ct(6, tt), in0=Ct(6, tt), scalar1=1e-5))
            B.op("vector", lambda e, tt=t: e.tensor_scalar_mul(out=B.dq1_all[:, tt:tt + 1], in0=Ct(6, tt), scalar1=1.0 / 127.0))
            B.op("vector", lambda e, tt=t: e.reciprocal(out=Ct(7, tt), in_=Ct(6, tt)))
            gs1 = B.op("vector", lambda e, tt=t: e.tensor_scalar_mul(out=Ct(7, tt), in0=Ct(7, tt), scalar1=127.0))
            B.waitg("scalar", [gs1])
            gy = B.op("scalar", lambda e, p=par, tt=t: e.activation(
                out=xt[p], in_=xn[p], func=AF.Identity, scale=Ct(7, tt), bias=B.cpool[:, 3:4]))
            B.waitg("vector", [gy])
            gn = B.op("vector", lambda e, p=par: e.tensor_scalar(out=nb[p], in0=xt[p], scalar1=MAGIC, scalar2=None, op0=ALU.subtract))
            xt_cons[par] = [gn]
            xn_cons[par] = [gy, gab]
            B.waitg("gpsimd", [gn])
            s2 = B.dma("gpsimd", nv[t], nb[par], "LS%d" % par)
            txg = [B.dma("sync", B.xnfm_all.rearrange("p (c t) -> p c t", c=ND)[:, :, 128 * t:128 * t + 128],
                         xb[par].rearrange("t (c p) -> t c p", c=ND),
                         "TX%d" % par, transpose=True, gate=[gx])]
            sguard[par] = [s2] + txg
            B.tx_done[t] = txg
            B.emit_v_unit(t)
            B.pump(2)
        B.ln_final = sguard[0] + sguard[1]
        fa.reset(); ba.reset()

    def ph_attention(self):
        B = self
        ba, fa = B.ba, B.fa
        ba.mark(); fa.mark()
        xn_fm = B.xn_fm_t
        wv_region = B.att_wv_region
        wv_sb = B.att_wv
        v_sb = B.att_v
        wqk = B.att_wqk
        q_fm2 = [ba.alloc_at(wv_region, 2048), ba.alloc_at(wv_region + 4096, 2048)]
        k_fm2 = [ba.alloc_at(wv_region + 2048, 2048), ba.alloc_at(wv_region + 6144, 2048)]
        p_sb = [[ba.alloc(512) for i in range(3)] for l in range(2)]
        o_attn = ba.alloc(1024)
        o_un = [ba.alloc(512, parts=65) for _ in range(8)]
        srow = fa.alloc(2048, parts=1)
        oun_guard = [[] for _ in range(8)]
        B.r_row = fa.alloc(2048, parts=1)
        rr_guard = [[] for _ in range(4)]
        g_srow = [[] for _ in range(4)]
        wstage2 = B.att_wstage
        stg_cons = B.att_stg_cons
        tv = [("TX%d" % p, B.C["TX%d" % p]) for p in range(2)]
        wcast = B.att_wcast
        B.store_done = {h: [] for h in range(H)}
        vs_ones = B.att_vs_ones
        B.waitg("tensor", [wcast[-1], vs_ones])
        SPOOL = {0: (2, 3, 4), 1: (5, 6, 7)}
        OBANK = {0: 0, 1: 1}
        wqk_guard = [[], []]
        qk_guard = []
        pv_guard = [[[], [], []], [[], [], []]]
        sa_guard = [[], []]
        norm_pending = []
        oun_next = [0]
        ev_ctr = [0]
        B.gstep = 0

        def emit_norm(maxn=10 ** 6, due=None):
            n = 0
            while norm_pending and n < maxn:
                if due is not None and norm_pending[0][0] > due:
                    return
                norm_pending.pop(0)[2]()
                n += 1

        def load_wqk(hp2):
            par2 = hp2 % 2
            B.waitg("sync", wqk_guard[par2])
            out = []
            for qk, c0 in ((0, 128 * hp2), (1, D + 128 * hp2)):
                sl = qk
                wvw = B.w_in[:, c0:c0 + 128].rearrange("(c p) f -> p c f", p=128)
                dv = B.dma("sync", wstage2[sl].rearrange("p (c f) -> p c f", f=128), wvw, "WST%d" % sl,
                           gate=[stg_cons[sl]] if stg_cons[sl] else [])
                B.waitg("gpsimd", [dv])
                stg_cons[sl] = B.op("gpsimd", lambda e, p=par2, q=qk, s2=sl: e.tensor_copy(
                    out=wqk[p][:, 1024 * q:1024 * q + 1024], in_=wstage2[s2]))
                out.append(stg_cons[sl])
            return out

        B.waitg("tensor", tv)
        B.waitg("scalar", B.ln_final)
        B.waitg("vector", B.ln_final)
        B.ln_final = None
        vjunk = fa.alloc(4, parts=1)
        B.op("vector", lambda e: e.memset(vjunk[:], 0.0))
        wq_cast_next = load_wqk(0)
        qk_guard2 = [[], []]
        for hp in range(8):
            par = hp % 2
            q_fm, k_fm = q_fm2[par], k_fm2[par]
            wq_cast = wq_cast_next
            B.waitg("tensor", [wq_cast[-1]])
            qk_ready = []
            for qk in range(2):
                for tc in range(4):
                    bank = (qk * 4 + tc) % 4
                    Pb = B.P[bank]
                    B.waitg("tensor", B.bguard[bank])
                    for c in range(ND - 1):
                        B.em("tensor", lambda e, b=Pb, cc=c, q=qk, t=tc, p=par: e.matmul(
                            b[:], wqk[p][:, 1024 * q + 128 * cc:1024 * q + 128 * cc + 128],
                            xn_fm[cc][:, 512 * t:512 * t + 512],
                            start=(cc == 0), stop=False))
                    tg = B.op("tensor", lambda e, b=Pb, q=qk, t=tc, p=par: e.matmul(
                        b[:], wqk[p][:, 1024 * q + 128 * (ND - 1):1024 * q + 128 * ND],
                        xn_fm[ND - 1][:, 512 * t:512 * t + 512],
                        start=False, stop=True))
                    B.waitg("scalar", [tg])
                    B.waitg("scalar", qk_guard2[par])
                    qk_guard2[par] = []
                    dst = q_fm if qk == 0 else k_fm
                    bias = B.b_qk[:, 8 * qk + hp:8 * qk + hp + 1]
                    if qk == 0:
                        g = B.op("scalar", lambda e, b=Pb, t=tc, dd=dst, bi=bias: e.activation(
                            out=dd[:, 512 * t:512 * t + 512], in_=b[:], func=AF.Identity,
                            scale=B.cpool[:, 2:3], bias=bi))
                    else:
                        g = B.op("scalar", lambda e, b=Pb, t=tc, dd=dst, bi=bias: e.activation(
                            out=dd[:, 512 * t:512 * t + 512], in_=b[:], func=AF.Identity, bias=bi))
                    B.bguard[bank] = [g]
                    qk_ready.append(g)
            B.waitg("tensor", [qk_ready[-1]])
            if hp < 7:
                wq_cast_next = load_wqk(hp + 1)
            if hp >= 2:
                c2 = hp - 2
                B.dma("sync", B.afm_t[c2], B.s_attn[128 * c2:128 * c2 + 128, :], "AFM",
                      gate=B.store_done[2 * c2] + B.store_done[2 * c2 + 1])
            qk_guard = []
            for qh in range(2):
                locs = [0, 1]
                plans = {loc: ATTN_PLANS[(2 * hp + loc, qh)] for loc in locs}
                keys = sorted({(st["j"], st["ki"]) for loc in locs for st in plans[loc][0]})
                stepmap = {loc: {(st["j"], st["ki"]): st for st in plans[loc][0]} for loc in locs}
                ready = {}

                def emit_v(si2, keys=keys, stepmap=stepmap, plans=plans, ready=ready, hp=hp, qh=qh):
                    sp2 = si2 % 3
                    infos = []
                    for loc2 in locs:
                        info = ready.pop((loc2, si2), None)
                        if info is None:
                            continue
                        ki2, j2, eg2, vg2, pbuf2, h2 = info
                        kivj = plans[loc2][1][j2]
                        first = (ki2 == kivj[0])
                        lastk = (ki2 == kivj[-1])
                        B.waitg("tensor", eg2)
                        B.waitg("tensor", vg2)
                        if first:
                            B.waitg("tensor", B.bguard[OBANK[loc2]])
                        infos.append((loc2, ki2, j2, pbuf2, h2, first, lastk))
                    for loc2, ki2, j2, pbuf2, h2, first, lastk in infos:
                        obank = OBANK[loc2]
                        g = B.op("tensor", lambda e, ob=B.P[obank], k=ki2, hh=h2, pb=pbuf2, f1=first, l1=lastk: e.matmul(
                            ob[0:65, :],
                            v_sb[:, 1040 * k + 65 * hh:1040 * k + 65 * hh + 65],
                            pb[:], start=f1, stop=l1))
                        pv_guard[loc2][sp2] = [g]
                        if lastk:
                            _norm_part1(loc2, j2, hp, qh, [g])

                def _norm_part1(loc2, j2, hp2, qh2, vlast):
                    evid = ev_ctr[0]
                    ev_ctr[0] += 1
                    while norm_pending and norm_pending[0][1] <= evid - 4:
                        emit_norm(1)
                    h2 = 2 * hp2 + loc2
                    jj = j2 % 2
                    obank = OBANK[loc2]
                    Ob = B.P[obank]
                    pu = oun_next[0]
                    oun_next[0] = (pu + 1) % 8
                    B.waitg("scalar", vlast)
                    B.waitg("scalar", oun_guard[pu])
                    ru = pu % 4
                    B.waitg("scalar", g_srow[ru])
                    gs0 = B.op("scalar", lambda e, b=Ob, r2=ru: e.copy(
                        out=srow[0:1, 512 * r2:512 * r2 + 512], in_=b[64:65, :]))
                    gs_ = B.op("scalar", lambda e, b=Ob, p2=pu: e.copy(out=o_un[p2][0:64, :], in_=b[0:64, :]))
                    B.bguard[obank] = [gs_]
                    st0 = B.gstep
                    ev = {}

                    def part2a(ru3=ru, g0=[gs0]):
                        B.waitg("vector", g0)
                        B.waitg("vector", rr_guard[ru3])
                        for ch in (0, 1):
                            B.op("vector", lambda e, r3=ru3, c0=128 * ch: e.reciprocal(
                                out=B.r_row[0:1, 512 * r3 + c0:512 * r3 + c0 + 128],
                                in_=srow[0:1, 512 * r3 + c0:512 * r3 + c0 + 128]))

                    def part2b(ru3=ru):
                        for ch in (2, 3):
                            B.op("vector", lambda e, r3=ru3, c0=128 * ch: e.reciprocal(
                                out=B.r_row[0:1, 512 * r3 + c0:512 * r3 + c0 + 128],
                                in_=srow[0:1, 512 * r3 + c0:512 * r3 + c0 + 128]))
                        B.op("vector", lambda e: e.memset(vjunk[:], 0.0))
                        gvb = B.op("vector", lambda e: e.memset(vjunk[:], 0.0))
                        ev["gvb"] = gvb
                        g_srow[ru3] = [gvb]

                    def part2c(loc3=loc2, h3=h2, jj3=jj, qh3=qh2, pu3=pu, ru3=ru, gs3=gs_):
                        rbank = SPOOL[loc3][0]
                        Rb = B.P[rbank]
                        B.waitg("tensor", [ev["gvb"]])
                        B.waitg("tensor", B.bguard[rbank])
                        gt = B.op("tensor", lambda e, b=Rb, r3=ru3: e.matmul(
                            b[0:64, :], B.ones_row64[0:1, :],
                            B.r_row[0:1, 512 * r3:512 * r3 + 512], start=True, stop=True))
                        rr_guard[ru3] = [gt]
                        B.waitg("vector", [gt, gs3])
                        B.waitg("vector", sa_guard[jj3])
                        gv = B.op("vector", lambda e, b=Rb, p2=pu3, j2=jj3: e.tensor_tensor(
                            out=o_attn[0:64, 512 * j2:512 * j2 + 512],
                            in0=o_un[p2][0:64, :], in1=b[0:64, :], op=ALU.mult))
                        oun_guard[pu3] = [gv]
                        B.bguard[rbank] = [gv]
                        B.waitg("gpsimd", [gv])
                        sa_guard[jj3] = [B.dma("gpsimd",
                            B.s_attn[64 * h3:64 * h3 + 64, 1024 * qh3 + 512 * jj3:1024 * qh3 + 512 * jj3 + 512],
                            o_attn[0:64, 512 * jj3:512 * jj3 + 512], "SA%d" % jj3)]
                        B.store_done[h3] = B.store_done[h3] + sa_guard[jj3]
                    norm_pending.append((st0 + 2, evid, part2a))
                    norm_pending.append((st0 + 4, evid, part2b))
                    norm_pending.append((st0 + 6, evid, part2c))

                final_blk = (hp == 7 and qh == 1)
                for si, (j, ki) in enumerate(keys):
                    sp = si % 3
                    B.gstep += 1
                    if final_blk:
                        emit_norm(10 ** 6)
                    else:
                        emit_norm(2, due=B.gstep)
                    for loc in locs:
                        st_ = stepmap[loc].get((j, ki))
                        if st_ is None:
                            continue
                        h = 2 * hp + loc
                        sbank = SPOOL[loc][sp]
                        Sb = B.P[sbank]
                        wlo, whi = st_["wlo"], st_["whi"]
                        B.waitg("tensor", B.bguard[sbank])
                        tg = B.op("tensor", lambda e, b=Sb, l=loc, k=ki, a=wlo, z=whi, kf=k_fm, qf=q_fm: e.matmul(
                            b[0:128, 0:z - a],
                            kf[64 * l:64 * l + 64, 128 * k:128 * k + 128],
                            qf[64 * l:64 * l + 64, a:z],
                            start=True, stop=True, tile_position=(64 * l, 0)))
                        qk_guard.append(tg)
                        B.waitg("scalar", [tg])
                        B.waitg("scalar", pv_guard[loc][sp])
                        pbuf = p_sb[loc][sp]
                        eg = []
                        for qg0, delta in st_["exps"]:
                            gw = GWH[h]
                            col = _EB_COLS[(h, delta)]
                            eg.append(B.op("scalar", lambda e, b=Sb, q=qg0, g0=gw, a=wlo, c=col, pb=pbuf, jj=j: e.activation(
                                out=pb[:, q - 512 * jj:q - 512 * jj + g0],
                                in_=b[0:128, q - a:q - a + g0],
                                func=AF.Exp, bias=B.eb[:, c:c + 1])))
                        B.bguard[sbank] = [eg[-1]]
                        vg = []
                        if st_["zeros"] or st_["mask"] is not None:
                            B.waitg("vector", [eg[-1]])
                            B.waitg("vector", pv_guard[loc][sp])
                        for z0, z1 in st_["zeros"]:
                            vg.append(B.op("vector", lambda e, a=z0, b2=z1, pb=pbuf, jj=j: e.memset(
                                pb[:, a - 512 * jj:b2 - 512 * jj], 0.0)))
                        if st_["mask"] is not None:
                            mk = st_["mask"]
                            vg.append(B.op("vector", lambda e, m=mk, pb=pbuf, jj=j: e.tensor_tensor(
                                out=pb[:, m - 512 * jj:m - 512 * jj + 128],
                                in0=pb[:, m - 512 * jj:m - 512 * jj + 128],
                                in1=B.m01[:], op=ALU.mult)))
                        ready[(loc, si)] = (ki, j, eg[-1:], vg[-1:], pbuf, h)
                    if si % 2 == 1:
                        B.pump(1)
                    if si >= 2:
                        emit_v(si - 2)
                if len(keys) >= 2:
                    emit_v(len(keys) - 2)
                emit_v(len(keys) - 1)
                B.pump(2)
            wqk_guard[par] = list(qk_guard[-1:])
            qk_guard2[par] = list(qk_guard[-1:])
            qk_guard = []
        emit_norm(10 ** 6)
        B.pump(10 ** 6)
        ba.reset(); fa.reset()
        B.ba.reset()
        B.fa.reset()

    def ph_outproj(self):
        B = self
        ba, fa = B.ba, B.fa
        ba.mark()
        B.nfm_all = ba.alloc(ND * 2048)
        B.n_fm_t = [B.nfm_all[:, 2048 * c:2048 * c + 2048] for c in range(ND)]
        B.mg_all = ba.alloc(ND * 4096)
        B.mg_t = [B.mg_all[:, 4096 * c:4096 * c + 4096] for c in range(ND)]
        ba.mark(); fa.mark()
        attn_fm = B.afm_t
        wo_sb = B.wo_sb
        xt = [fa.alloc(1024) for _ in range(2)]
        acc = [fa.alloc(1024) for _ in range(2)]
        for c in (6, 7):
            B.dma("sync", attn_fm[c], B.s_attn[128 * c:128 * c + 128, :], "AFM",
                  gate=B.store_done[2 * c] + B.store_done[2 * c + 1])
        B.barrier()
        B.tr_n = B.dma("sync", B.nfm_all.rearrange("p (c t) -> p c t", c=ND),
                       B.s_n.rearrange("t (c p) -> t c p", c=ND), "TRN", transpose=True)
        B.tr_g = B.dma("sync", B.mg_all.rearrange("p (c t) -> p c t", c=ND),
                       B.s_mg.rearrange("t (c p) -> t c p", c=ND), "TRG", transpose=True,
                       gate=B.mdone.get("g", []))
        B.barrier()
        xv = B.x.rearrange("(t p) d -> t p d", p=128)
        av = B.s_acc.rearrange("(t p) d -> t p d", p=128)
        ld_guard = [[], []]
        stg = [[], []]
        for t in range(NT):
            par = t % 2
            dv = B.dma("sync", xt[par], xv[t], "LX%d" % par, gate=ld_guard[par])
            evs = []
            for half in range(2):
                bank = (2 * t + half) % 4
                Pb = B.P[bank]
                B.waitg("tensor", B.bguard[bank])
                for c in range(ND):
                    B.em("tensor", lambda e, b=Pb, cc=c, tt=t, hh=half: e.matmul(
                        b[:], attn_fm[cc][:, 128 * tt:128 * tt + 128],
                        wo_sb[:, 1024 * cc + 512 * hh:1024 * cc + 512 * hh + 512],
                        start=(cc == 0), stop=False))
                tg = B.op("tensor", lambda e, b=Pb, tt=t, hh=half: e.matmul(
                    b[:], B.ones_row[0:1, 128 * tt:128 * tt + 128],
                    B.bo_row[0:1, 512 * hh:512 * hh + 512], start=False, stop=True))
                B.waitg("vector", [tg, dv])
                B.waitg("vector", stg[par])
                g = B.op("vector", lambda e, b=Pb, hh=half, p=par: e.tensor_tensor(
                    out=acc[p][:, 512 * hh:512 * hh + 512],
                    in0=xt[p][:, 512 * hh:512 * hh + 512], in1=b[:], op=ALU.add))
                B.bguard[bank] = [g]
                evs.append(g)
            ld_guard[par] = [evs[-1]]
            B.waitg("gpsimd", evs)
            stg[par] = [B.dma("gpsimd", av[t], acc[par], "LS%d" % par)]
        ba.reset(); fa.reset()
        B.outproj_done = [evs[-1]]

    def wquant_gen(self, wb, m2, sg):
        B = self
        accs = B.wqacc
        specs = [("g", B.w_gate, B.s_mg, 32, 1024), ("u", B.w_up, B.s_mu, 32, 1024),
                 ("d", B.w_down, B.s_md, 8, 4096)]
        cons_guard = [[], []]
        mstore = [[], []]
        sg_guard = []
        for wname, wsrc, mdst, ntile, fdim in specs:
            nsub = fdim // 1024
            wv = wsrc.rearrange("(t p) f -> t p f", p=128)
            gv = None
            ncols = ntile * nsub
            B.waitg("scalar", sg_guard)
            for t in range(ntile):
                for s_ in range(nsub):
                    idx = (t * nsub + s_) % 2
                    dv = B.dma("sync", wb[idx], wv[t, :, 1024 * s_:1024 * s_ + 1024], "WQ%d" % idx, gate=cons_guard[idx])
                    B.waitg("scalar", [dv])
                    col = t * nsub + s_
                    gv = B.op("scalar", lambda e, i=idx, c0=col: e.activation(
                        out=sg[0], in_=wb[i], func=AF.Abs, accum_out=accs[:, c0:c0 + 1]))
                    cons_guard[idx] = [gv]
                    yield
            B.waitg("vector", [gv])
            gv = B.op("vector", lambda e, nc2=ncols: e.reduce_sum(out=accs[:, 32:33], in_=accs[:, 0:nc2], axis=AX.X))
            B.waitg("tensor", [gv])
            B.waitg("tensor", B.bguard[4])
            gt = B.op("tensor", lambda e: e.matmul(B.P[4][0:1, 0:1], accs[:, 32:33], B.ones_col_f, start=True, stop=True))
            B.waitg("vector", [gt])
            B.op("vector", lambda e, n=float(ntile * fdim * 128): e.tensor_scalar_mul(out=accs[0:1, 33:34], in0=B.P[4][0:1, 0:1], scalar1=1.0 / n))
            B.op("vector", lambda e: e.tensor_scalar_max(out=accs[0:1, 33:34], in0=accs[0:1, 33:34], scalar1=1e-5))
            gv = B.op("vector", lambda e: e.reciprocal(out=accs[0:1, 34:35], in_=accs[0:1, 33:34]))
            B.bguard[4] = [gv]
            B.waitg("tensor", [gv])
            B.waitg("tensor", B.bguard[5])
            gt = B.op("tensor", lambda e: e.matmul(B.P[5][0:128, 0:2], B.ones_row128[0:1, :], accs[0:1, 33:35], start=True, stop=True))
            B.waitg("vector", [gt])
            B.op("vector", lambda e, w=wname: e.tensor_copy(out=B.mav[w], in_=B.P[5][0:128, 0:1]))
            gv = B.op("vector", lambda e, w=wname: e.tensor_copy(out=B.wsv[w], in_=B.P[5][0:128, 1:2]))
            B.bguard[5] = [gv]
            B.waitg("scalar", [gv])
            yield
            mv = mdst.rearrange("(t p) f -> t p f", p=128)
            for t in range(ntile):
                for s_ in range(nsub):
                    idx = (t * nsub + s_) % 2
                    dv = B.dma("sync", wb[idx], wv[t, :, 1024 * s_:1024 * s_ + 1024], "WQ%d" % idx, gate=cons_guard[idx])
                    B.waitg("vector", [dv])
                    B.waitg("vector", sg_guard)
                    B.op("vector", lambda e, i=idx, w=wname: e.tensor_scalar(
                        out=sg[0], in0=wb[i], scalar1=B.wsv[w], scalar2=0.5, op0=ALU.mult, op1=ALU.is_ge))
                    g2 = B.op("vector", lambda e, i=idx, w=wname: e.tensor_scalar(
                        out=sg[1], in0=wb[i], scalar1=B.wsv[w], scalar2=-0.5, op0=ALU.mult, op1=ALU.is_le))
                    B.waitg("vector", mstore[idx])
                    gv = B.op("vector", lambda e, i=idx: e.tensor_sub(out=m2[i], in0=sg[0], in1=sg[1]))
                    sg_guard = [gv]
                    cons_guard[idx] = [gv]
                    B.waitg("gpsimd", [gv])
                    mstore[idx] = [B.dma("gpsimd", mv[t, :, 1024 * s_:1024 * s_ + 1024], m2[idx], "MS%d" % idx)]
                    yield
            self.mdone[wname] = mstore[0] + mstore[1]

    def pump(self, n=1):
        for _ in range(n):
            if next(self.wq_gen, "done") == "done":
                break

    def emit_v_unit(self, ts_):
        B = self
        xn_fm, wv_sb, v_sb = B.xn_fm_t, B.att_wv, B.att_v
        B.waitg("tensor", B.tx_done[ts_])
        if ts_ == 0:
            B.waitg("tensor", [B.att_wcast[7], B.att_vs_ones])
        for half in range(2):
            bank = (ts_ * 2 + half) % 4
            Pb = B.P[bank]
            B.waitg("tensor", B.bguard[bank])
            for c in range(ND):
                B.em("tensor", lambda e, b=Pb, cc=c, t=ts_, hh=half: e.matmul(
                    b[:], xn_fm[cc][:, 128 * t:128 * t + 128],
                    wv_sb[:, 1024 * cc + 512 * hh:1024 * cc + 512 * hh + 512],
                    start=(cc == 0), stop=False))
            tg = B.op("tensor", lambda e, b=Pb, t=ts_, hh=half: e.matmul(
                b[:], B.ones_row[0:1, 128 * t:128 * t + 128],
                B.bv_row[0:1, 512 * hh:512 * hh + 512], start=False, stop=True))
            B.waitg("vector", [tg])
            last = None
            for hs in range(8):
                hg = half * 8 + hs
                last = B.op("vector", lambda e, b=Pb, t=ts_, h1=hs, h2=hg: e.tensor_copy(
                    out=v_sb[:, 1040 * t + 65 * h2:1040 * t + 65 * h2 + 64],
                    in_=b[:, 64 * h1:64 * h1 + 64]))
            B.bguard[bank] = [last]

    def ph_ffn_gateup(self):
        B = self
        ba, fa = B.ba, B.fa
        ba.mark(); fa.mark()
        n_fm, mg = B.n_fm_t, B.mg_t
        B.mu67_off = ba.off
        mu = [ba.alloc_at(B.op_region + 4096 * c, 4096) for c in range(6)] + [ba.alloc(4096) for _ in range(2)]
        n2b = ba.alloc(2048)
        h2 = [fa.alloc(4096) for _ in range(2)]
        st = B.stat
        B.barrier()
        mu_a = ba.alloc_at(B.op_region, 6 * 4096)
        tru = [B.dma("sync", mu_a.rearrange("p (c t) -> p c t", c=6),
                     B.s_mu[:, 0:768].rearrange("t (c p) -> t c p", c=6), "TRU", transpose=True,
                     gate=B.mdone.get("u", []) + B.outproj_done)]
        mu_b = mu[6]
        tru.append(B.dma("sync", ba.t[0:128, mu_b.offset - 0:mu_b.offset + 2 * 4096].rearrange("p (c t) -> p c t", c=2)
                         if False else None, None, "TRU", transpose=True) if False else
                   B.dma("sync", ba.alloc_at(B.mu67_off, 2 * 4096).rearrange("p (c t) -> p c t", c=2),
                         B.s_mu[:, 768:1024].rearrange("t (c p) -> t c p", c=2), "TRU", transpose=True))
        B.op("vector", lambda e: e.tensor_scalar(out=B.dqg_all[:], in0=B.dq1_all[:], scalar1=B.mav["g"], scalar2=None, op0=ALU.mult))
        gdq = B.op("vector", lambda e: e.tensor_scalar(out=B.dqu_all[:], in0=B.dq1_all[:], scalar1=B.mav["u"], scalar2=None, op0=ALU.mult))
        B.waitg("scalar", [gdq])
        n2v = B.s_n2.rearrange("(t p) f -> t p f", p=128)
        hguard = [[], []]
        n2guard = []
        mu_waited = False
        for t in range(NT):
            h_sb = h2[t % 2]
            for oc in range(8):
                bank = oc % 4
                bg_ = B.P[bank]
                B.waitg("tensor", B.bguard[bank])
                for c in range(ND - 1):
                    B.em("tensor", lambda e, b=bg_, cc=c, tt=t, o=oc: e.matmul(
                        b[:], n_fm[cc][:, 128 * tt:128 * tt + 128],
                        mg[cc][:, 512 * o:512 * o + 512],
                        start=(cc == 0), stop=False))
                tg = B.op("tensor", lambda e, b=bg_, tt=t, o=oc: e.matmul(
                    b[:], n_fm[ND - 1][:, 128 * tt:128 * tt + 128],
                    mg[ND - 1][:, 512 * o:512 * o + 512],
                    start=False, stop=True))
                B.waitg("scalar", [tg])
                B.waitg("scalar", hguard[t % 2])
                g1 = B.op("scalar", lambda e, b=bg_, tt=t, o=oc, hs=h_sb: e.activation(
                    out=hs[:, 512 * o:512 * o + 512], in_=b[:], func=AF.Silu,
                    scale=B.dqg_all[:, tt:tt + 1]))
                B.bguard[bank] = [g1]
            if not mu_waited:
                B.waitg("tensor", tru)
                mu_waited = True
            for oc in range(8):
                bank = oc % 4
                bu_ = B.P[bank]
                B.waitg("tensor", B.bguard[bank])
                for c in range(ND - 1):
                    B.em("tensor", lambda e, b=bu_, cc=c, tt=t, o=oc: e.matmul(
                        b[:], n_fm[cc][:, 128 * tt:128 * tt + 128],
                        mu[cc][:, 512 * o:512 * o + 512],
                        start=(cc == 0), stop=False))
                tg = B.op("tensor", lambda e, b=bu_, tt=t, o=oc: e.matmul(
                    b[:], n_fm[ND - 1][:, 128 * tt:128 * tt + 128],
                    mu[ND - 1][:, 512 * o:512 * o + 512],
                    start=False, stop=True))
                B.waitg("vector", [tg])
                B.waitg("vector", hguard[t % 2] if oc == 0 else [])
                gh = B.op("vector", lambda e, b=bu_, tt=t, o=oc, hs=h_sb: e.scalar_tensor_tensor(
                    out=hs[:, 512 * o:512 * o + 512], in0=b[:],
                    scalar=B.dqu_all[:, tt:tt + 1], in1=hs[:, 512 * o:512 * o + 512],
                    op0=ALU.mult, op1=ALU.mult))
                B.bguard[bank] = [gh]
            sc = 2 + (t % 2)
            B.op("vector", lambda e, hs=h_sb, c0=sc: e.reduce_max(out=st[:, c0:c0 + 1], in_=hs, axis=AX.X, apply_absolute_value=True))
            B.op("vector", lambda e, c0=sc: e.tensor_scalar_max(out=st[:, c0:c0 + 1], in0=st[:, c0:c0 + 1], scalar1=1e-5))
            B.op("vector", lambda e, tt=t, c0=sc: e.tensor_scalar_mul(out=B.dq2_all[:, tt:tt + 1], in0=st[:, c0:c0 + 1], scalar1=1.0 / 127.0))
            B.op("vector", lambda e, c0=sc: e.reciprocal(out=st[:, c0:c0 + 1], in_=st[:, c0:c0 + 1]))
            B.op("vector", lambda e, c0=sc: e.tensor_scalar_mul(out=st[:, c0:c0 + 1], in0=st[:, c0:c0 + 1], scalar1=127.0))
            B.op("vector", lambda e, hs=h_sb, c0=sc: e.tensor_scalar(out=hs, in0=hs, scalar1=st[:, c0:c0 + 1], scalar2=MAGIC, op0=ALU.mult, op1=ALU.add))
            hguard[t % 2] = []
            for hf in range(2):
                B.waitg("vector", n2guard)
                gq = B.op("vector", lambda e, h3=hf, hs=h_sb: e.tensor_scalar(
                    out=n2b, in0=hs[:, 2048 * h3:2048 * h3 + 2048], scalar1=MAGIC, scalar2=None, op0=ALU.subtract))
                hguard[t % 2].append(gq)
                B.waitg("gpsimd", [gq])
                n2guard = [B.dma("gpsimd", n2v[t, :, 2048 * hf:2048 * hf + 2048], n2b, "SN2")]
        B.gateup_tail = [tg]
        ba.reset(); fa.reset()
        B.ba.reset()

    def ph_ffn_down(self):
        B = self
        ba, fa = B.ba, B.fa
        ba.reset()
        ba.mark(); fa.mark()
        md_all = ba.alloc(NO * 1024)
        md = [md_all[:, 1024 * c:1024 * c + 1024] for c in range(NO)]
        n2f_all = [ba.alloc(NO * 512) for _ in range(2)]
        n2f2 = [[n2f_all[p][:, 512 * c:512 * c + 512] for c in range(NO)] for p in range(2)]
        acc = [fa.alloc(1024) for _ in range(2)]
        ys = [fa.alloc(1024) for _ in range(2)]
        B.dma("sync", md_all.rearrange("p (c t) -> p c t", c=NO),
              B.s_md.rearrange("t (c p) -> t c p", c=NO), "DTR", transpose=True,
              gate=B.gateup_tail + B.mdone.get("d", []))
        B.op("vector", lambda e: e.tensor_scalar(out=B.ddq_all[:], in0=B.dq2_all[:], scalar1=B.mav["d"], scalar2=None, op0=ALU.mult))
        B.barrier()
        av = B.s_acc.rearrange("(t p) d -> t p d", p=128)
        yv = B.y.rearrange("(t p) d -> t p d", p=128)
        ld_guard = [[], []]
        st_guard = [[], []]
        tr_guard = [[], []]
        trs_done = [[], []]

        def load_n2(tg2):
            pp = tg2 % 2
            trs_done[pp] = [B.dma("sync", n2f_all[pp].rearrange("p (c t) -> p c t", c=NO),
                                  B.s_n2[512 * tg2:512 * tg2 + 512, :].rearrange("t (c p) -> t c p", c=NO),
                                  "TN%d" % pp, transpose=True, gate=tr_guard[pp])]
        load_n2(0)
        for tg_ in range(4):
            pp = tg_ % 2
            n2f = n2f2[pp]
            B.waitg("tensor", trs_done[pp])
            if tg_ < 3:
                load_n2(tg_ + 1)
            tr_guard[pp] = []
            for tl in range(4):
                t = 4 * tg_ + tl
                par = t % 2
                dv = B.dma("sync", acc[par], av[t], "LA%d" % par, gate=ld_guard[par])
                evs = []
                for half in range(2):
                    bank = (2 * t + half) % 4
                    Pb = B.P[bank]
                    B.waitg("tensor", B.bguard[bank])
                    for c in range(NO - 1):
                        B.em("tensor", lambda e, b=Pb, cc=c, lt=tl, hh=half, nf=n2f: e.matmul(
                            b[:], nf[cc][:, 128 * lt:128 * lt + 128],
                            md[cc][:, 512 * hh:512 * hh + 512],
                            start=(cc == 0), stop=False))
                    tgm = B.op("tensor", lambda e, b=Pb, lt=tl, hh=half, nf=n2f: e.matmul(
                        b[:], nf[NO - 1][:, 128 * lt:128 * lt + 128],
                        md[NO - 1][:, 512 * hh:512 * hh + 512],
                        start=False, stop=True))
                    tr_guard[pp].append(tgm)
                    B.waitg("vector", [tgm, dv])
                    B.waitg("vector", st_guard[par])
                    g = B.op("vector", lambda e, b=Pb, tt=t, hh=half, p=par: e.scalar_tensor_tensor(
                        out=ys[p][:, 512 * hh:512 * hh + 512], in0=b[:],
                        scalar=B.ddq_all[:, tt:tt + 1], in1=acc[p][:, 512 * hh:512 * hh + 512],
                        op0=ALU.mult, op1=ALU.add))
                    B.bguard[bank] = [g]
                    evs.append(g)
                ld_guard[par] = [evs[-1]]
                B.waitg("gpsimd", evs)
                st_guard[par] = [B.dma("gpsimd", yv[t], ys[par], "YS%d" % par)]
        ba.reset(); fa.reset()

    def finish(self):
        B = self
        for sname in list(self.C):
            if sname not in ENG_SEM.values() and self.C[sname] > 0:
                B.wait("gpsimd", sname, self.C[sname])

    def materialize(self):
        nc = self.nc
        with self.stack:
            with nc.Block() as block:
                def run(engine_name):
                    def body(eng):
                        for e, fn in self.sched:
                            if e == engine_name:
                                fn(eng)
                    return body
                block.sync(run("sync"))
                block.tensor(run("tensor"))
                block.vector(run("vector"))
                block.scalar(run("scalar"))
                block.gpsimd(run("gpsimd"))
        return nc


_BUILD_CACHE = {}


def _get_nc():
    if "nc" not in _BUILD_CACHE:
        _BUILD_CACHE["nc"] = Builder().build()
    return _BUILD_CACHE["nc"]


def kernel(L0_attn_x, L0_attn_w_in, L0_attn_b_in, L0_attn_w_out, L0_attn_b_out,
           L0_ffn_w_gate, L0_ffn_w_up, L0_ffn_w_down):
    x = np.ascontiguousarray(np.asarray(L0_attn_x, dtype=np.float32))
    w_in = np.ascontiguousarray(np.asarray(L0_attn_w_in, dtype=np.float32))
    b_in = np.ascontiguousarray(np.asarray(L0_attn_b_in, dtype=np.float32))
    w_out = np.ascontiguousarray(np.asarray(L0_attn_w_out, dtype=np.float32))
    b_out = np.ascontiguousarray(np.asarray(L0_attn_b_out, dtype=np.float32))
    w_gate = np.ascontiguousarray(np.asarray(L0_ffn_w_gate, dtype=np.float32))
    w_up = np.ascontiguousarray(np.asarray(L0_ffn_w_up, dtype=np.float32))
    w_down = np.ascontiguousarray(np.asarray(L0_ffn_w_down, dtype=np.float32))

    nc = _get_nc()
    in_maps = [dict(x=x[c], w_in=w_in, b_in=b_in, w_out=w_out, b_out=b_out,
                    w_gate=w_gate, w_up=w_up, w_down=w_down) for c in range(8)]
    res = run_bass_kernel_spmd(nc, in_maps, core_ids=list(range(8)))
    return np.stack([res.results[c]["y"] for c in range(8)], axis=0)


# version 23
# speedup vs baseline: 1.1230x; 1.0695x over previous
import contextlib
from collections import defaultdict

import numpy as np
import ml_dtypes

import concourse.bass as bass
import concourse.mybir as mybir
from concourse.bass_utils import run_bass_kernel_spmd

BFNP = ml_dtypes.bfloat16
F32 = mybir.dt.float32
BF16 = mybir.dt.bfloat16
AF = mybir.ActivationFunctionType
ALU = mybir.AluOpType
AX = mybir.AxisListType

L, D, H, DH, INNER = 2048, 1024, 16, 64, 4096
NT, ND, NO = L // 128, D // 128, INNER // 128
MAGIC = float(np.float32(1.5 * 2 ** 23))
EPS = 1e-5
SLOPES = [2.0 ** (-(h + 1) * 0.5) for h in range(H)]
GWH = [128, 256, 256, 512, 512] + [512] * 11
WTH = [127.0 + 45.0 / s for s in SLOPES]
ENG_SEM = {"tensor": "TS", "vector": "VS", "scalar": "SS", "gpsimd": "GS", "sync": "SY"}


def _attn_plan():
    plans = {}
    for h in range(H):
        gw, W = GWH[h], WTH[h]
        for qh in range(2):
            steps = []
            kiv = {j: [] for j in (2 * qh, 2 * qh + 1)}
            for ki in range(8 * qh + 8):
                k0 = 128 * ki
                for j in (2 * qh, 2 * qh + 1):
                    c0 = 512 * j
                    exps, zeros, mask = [], [], None
                    for qg0 in range(c0, c0 + 512, gw):
                        if qg0 + gw <= k0 or qg0 - k0 > W:
                            zeros.append((qg0, qg0 + gw))
                        else:
                            exps.append((qg0, qg0 - k0))
                            if k0 >= qg0:
                                if k0 > qg0:
                                    zeros.append((qg0, k0))
                                mask = k0
                    if exps:
                        wlo = min(q for q, _ in exps)
                        whi = max(q for q, _ in exps) + gw
                        steps.append(dict(ki=ki, j=j, wlo=wlo, whi=whi,
                                          exps=exps, zeros=zeros, mask=mask))
                        kiv[j].append(ki)
            plans[(h, qh)] = (steps, kiv)
    return plans


ATTN_PLANS = _attn_plan()
_EB_COLS = {}
for (_h, _qh), (_steps, _kiv) in ATTN_PLANS.items():
    for _st in _steps:
        for _qg0, _delta in _st["exps"]:
            _EB_COLS.setdefault((_h, _delta), len(_EB_COLS))


def _make_consts():
    kl = np.arange(128, dtype=np.float64)
    eb = np.zeros((128, max(8, len(_EB_COLS))), dtype=np.float32)
    for (h, delta), c in _EB_COLS.items():
        eb[:, c] = (SLOPES[h] * (kl - delta - GWH[h] / 2)).astype(np.float32)
    m01 = (np.arange(128)[:, None] <= np.arange(128)[None, :]).astype(BFNP)
    return eb, m01


class Arena:
    def __init__(self, nc, name, dtype, ncols):
        self.t = nc.alloc_sbuf_tensor(name, [128, ncols], dtype)
        self.n = ncols
        self.off = 0
        self.stack = []

    def alloc(self, cols, parts=128):
        assert self.off + cols <= self.n, f"arena overflow {self.off}+{cols}>{self.n}"
        ap = self.t[0:parts, self.off:self.off + cols]
        self.off += cols
        return ap

    def alloc_at(self, off, cols, parts=128):
        assert off + cols <= self.n
        return self.t[0:parts, off:off + cols]

    def mark(self):
        self.stack.append(self.off)

    def reset(self):
        self.off = self.stack.pop()


class Builder:
    def __init__(self):
        nc = bass.Bass("TRN2", target_bir_lowering=False)
        self.nc = nc
        self.sched = []
        self.C = defaultdict(int)
        self.stack = contextlib.ExitStack()
        self.sem = {s: self.stack.enter_context(nc.semaphore(s))
                    for s in ("TS", "VS", "SS", "GS", "SY")}
        self.lastg = {e: [] for e in ("tensor", "vector", "scalar", "gpsimd", "sync")}

        d = nc.dram_tensor
        self.x = d("x", [L, D], F32, kind="ExternalInput").ap()
        self.w_in = d("w_in", [D, 3 * D], F32, kind="ExternalInput").ap()
        self.b_in = d("b_in", [3 * D], F32, kind="ExternalInput").ap()
        self.w_out = d("w_out", [D, D], F32, kind="ExternalInput").ap()
        self.b_out = d("b_out", [D], F32, kind="ExternalInput").ap()
        self.w_gate = d("w_gate", [INNER, D], F32, kind="ExternalInput").ap()
        self.w_up = d("w_up", [INNER, D], F32, kind="ExternalInput").ap()
        self.w_down = d("w_down", [D, INNER], F32, kind="ExternalInput").ap()
        self.y = d("y", [L, D], F32, kind="ExternalOutput").ap()
        self.s_xnb = d("s_xnb", [L, D], BF16).ap()
        self.s_n = d("s_n", [L, D], BF16).ap()
        self.s_attn = d("s_attn", [D, L], BF16).ap()
        self.s_acc = d("s_acc", [L, D], F32).ap()
        self.s_mg = d("s_mg", [INNER, D], BF16).ap()
        self.s_mu = d("s_mu", [INNER, D], BF16).ap()
        self.s_md = d("s_md", [D, INNER], BF16).ap()
        self.s_n2 = d("s_n2", [L, INNER], BF16).ap()

        eb_np, m01_np = _make_consts()
        self.c_eb = nc.inline_tensor(eb_np, name="c_eb")
        self.c_m01 = nc.inline_tensor(m01_np, name="c_m01")
        self.n_eb = eb_np.shape[1]

        self.fa = Arena(nc, "fa", F32, 8832)
        self.ba = Arena(nc, "ba", BF16, 88320)

        self.eb = self.fa.alloc(self.n_eb)
        self.cpool = self.fa.alloc(4)
        self.dq1_all = self.fa.alloc(NT)
        self.dq2_all = self.fa.alloc(NT)
        self.dqg_all = self.fa.alloc(NT)
        self.dqu_all = self.fa.alloc(NT)
        self.ddq_all = self.fa.alloc(NT)
        self.b_qk = self.fa.alloc(16)
        self.wsv = {w: self.fa.alloc(1) for w in "gud"}
        self.mav = {w: self.fa.alloc(1) for w in "gud"}
        self.stat = self.fa.alloc(8)
        self.ones_col_f = self.fa.alloc(1)
        self.ones_row128 = self.fa.alloc(128, parts=1)
        self.ones_row64 = self.ones_row128[0:1, 0:64]

        self.m01 = self.ba.alloc(128)
        self.ones_row = self.ba.alloc(2048, parts=1)
        self.bv_row = self.ba.alloc(1024, parts=1)
        self.bo_row = self.ba.alloc(1024, parts=1)

        self.mdone = {}
        self.wqacc = self.fa.alloc(36)

        self.P = [nc.alloc_psum_tensor(f"pb{i}", [128, 512], F32) for i in range(8)]
        self.bguard = {i: [] for i in range(8)}

    def em(self, engine, fn):
        self.sched.append((engine, fn))

    def _sem(self, name):
        if name not in self.sem:
            self.sem[name] = self.stack.enter_context(self.nc.semaphore(name))
        return self.sem[name]

    def wait(self, engine, sem, val):
        if val > 0:
            self.em(engine, lambda e, s=self._sem(sem), v=val: e.wait_ge(s, v))

    def waitg(self, engine, guard):
        best = {}
        for s, v in guard:
            best[s] = max(best.get(s, 0), v)
        for s, v in best.items():
            self.wait(engine, s, v)

    def dma(self, queue, out, in_, sem, transpose=False, slow=False, gate=()):
        sh = self._sem(sem)
        self.waitg(queue, gate)
        self.C[sem] += 16
        v = self.C[sem]
        if transpose:
            self.em(queue, lambda e, o=out, i=in_, s=sh: e.dma_start_transpose(out=o, in_=i).then_inc(s, 16))
        else:
            self.em(queue, lambda e, o=out, i=in_, s=sh, sl=slow: e.dma_start(out=o, in_=i, allow_slow_non_contiguous=sl).then_inc(s, 16))
        return (sem, v)

    def op(self, engine, fn):
        s = ENG_SEM[engine]
        prev = 0 if engine == "tensor" else self.C[s]
        self.C[s] += 1
        v = self.C[s]

        def run(e, f=fn, sh=self.sem[s], p=prev, vv=v):
            if p > 0:
                e.wait_ge(sh, p)
            return f(e).then_inc(sh, 1)
        self.em(engine, run)
        self.lastg[engine] = [(s, v)]
        return (s, v)

    def barrier(self):
        vals = [(s, self.C[s]) for s in list(self.C) if self.C[s] > 0]
        for eng in ("sync", "tensor", "vector", "scalar", "gpsimd"):
            for s, v in vals:
                self.wait(eng, s, v)

    def build(self):
        self.ph_init()
        self.ph_ln_quant()
        self.ph_attention()
        self.ph_outproj()
        self.ph_ffn_gateup()
        self.ph_ffn_down()
        self.finish()
        return self.materialize()

    def ph_init(self):
        B = self
        B.dma("sync", B.eb, B.c_eb[:], "DIN")
        B.dma("sync", B.m01, B.c_m01[:], "DIN")
        bq = B.b_in[0:D].rearrange("(j p) -> p j", p=128)
        bk = B.b_in[D:2 * D].rearrange("(j p) -> p j", p=128)
        B.dma("sync", B.b_qk[:, 0:8], bq, "DIN", slow=True)
        B.dma("sync", B.b_qk[:, 8:16], bk, "DIN", slow=True)
        B.fa.mark()
        bvf = B.fa.alloc(1024, parts=1)
        bof = B.fa.alloc(1024, parts=1)
        B.dma("sync", bvf, B.b_in[2 * D:3 * D][None, :], "DIN")
        g = B.dma("sync", bof, B.b_out[None, :], "DIN")
        B.wait("vector", "DIN", g[1])
        B.op("vector", lambda e: e.memset(B.cpool[:, 0:1], -0.5))
        B.op("vector", lambda e: e.memset(B.cpool[:, 1:2], 0.5))
        B.op("vector", lambda e: e.memset(B.cpool[:, 2:3], 0.125))
        B.op("vector", lambda e: e.memset(B.cpool[:, 3:4], MAGIC))
        B.op("vector", lambda e: e.memset(B.ones_col_f, 1.0))
        B.op("vector", lambda e: e.memset(B.ones_row128, 1.0))
        B.op("vector", lambda e: e.memset(B.ones_row, 1.0))
        B.op("vector", lambda e, s=bvf: e.tensor_copy(out=B.bv_row, in_=s))
        B.op("vector", lambda e, s=bof: e.tensor_copy(out=B.bo_row, in_=s))
        B.op("vector", lambda e: e.tensor_scalar_mul(out=B.b_qk[:, 0:8], in0=B.b_qk[:, 0:8], scalar1=0.125))
        B.fa.reset()

    def ph_ln_quant(self):
        B = self
        B.barrier()
        fa, ba = B.fa, B.ba
        ba.mark()
        B.op_region = ba.off
        B.afm_t = [ba.alloc(2048) for _ in range(ND)]
        B.wo_sb = ba.alloc(ND * 1024)
        ba.mark()
        B.xnfm_all = ba.alloc(ND * 2048)
        B.xn_fm_t = [B.xnfm_all[:, 2048 * c:2048 * c + 2048] for c in range(ND)]
        B.wqm2 = [ba.alloc(1024) for _ in range(2)]
        B.wqsg = [ba.alloc(1024) for _ in range(2)]
        fa.mark()
        B.wqb = [fa.alloc(1024) for _ in range(2)]
        B.wq_gen = B.wquant_gen(B.wqb, B.wqm2, B.wqsg)
        B.att_wv_region = ba.off
        B.att_wv = ba.alloc(8 * 1024)
        B.att_v = ba.alloc(NT * 1040)
        B.att_wqk = [ba.alloc(2 * 1024) for _ in range(2)]
        B.att_wstage = [fa.alloc(1024) for _ in range(2)]
        B.att_stg_cons = [None, None]
        wvv0 = B.w_in[:, 2 * D:3 * D].rearrange("(c p) f -> c p f", p=128)
        wov0 = B.w_out.rearrange("(c p) f -> c p f", p=128)
        B.att_wcast = []
        for c in range(ND):
            sl = c % 2
            dv = B.dma("sync", B.att_wstage[sl], wvv0[c], "WST%d" % sl,
                       gate=[B.att_stg_cons[sl]] if B.att_stg_cons[sl] else [])
            B.waitg("gpsimd", [dv])
            B.att_stg_cons[sl] = B.op("gpsimd", lambda e, cc=c, s2=sl: e.tensor_copy(
                out=B.att_wv[:, 1024 * cc:1024 * cc + 1024], in_=B.att_wstage[s2]))
            B.att_wcast.append(B.att_stg_cons[sl])
        for c in range(ND):
            sl = c % 2
            dv = B.dma("sync", B.att_wstage[sl], wov0[c], "WST%d" % sl,
                       gate=[B.att_stg_cons[sl]] if B.att_stg_cons[sl] else [])
            B.waitg("gpsimd", [dv])
            B.att_stg_cons[sl] = B.op("gpsimd", lambda e, cc=c, s2=sl: e.tensor_copy(
                out=B.wo_sb[:, 1024 * cc:1024 * cc + 1024], in_=B.att_wstage[s2]))
        B.att_vs_ones = B.op("vector", lambda e: e.memset(B.att_v, 1.0))
        fa.mark(); ba.mark()
        xt = [fa.alloc(1024) for _ in range(2)]
        xn = [fa.alloc(1024) for _ in range(2)]
        dump = xn[0]
        ls = fa.alloc(8 * 16)
        nb = [ba.alloc(1024) for _ in range(2)]
        xb = [ba.alloc(1024) for _ in range(2)]
        xv = B.x.rearrange("(t p) d -> t p d", p=128)
        nv = B.s_n.rearrange("(t p) d -> t p d", p=128)

        def C(i):
            return ls[:, 16 * i:16 * i + 16]

        def Ct(i, t):
            return ls[:, 16 * i + t:16 * i + t + 1]

        sq_cons = [[], []]
        gq = None
        for t in range(NT):
            par = t % 2
            dv = B.dma("sync", xt[par], xv[t], "LX%d" % par, gate=sq_cons[par])
            B.waitg("scalar", [dv])
            B.op("scalar", lambda e, p=par, tt=t: e.activation(
                out=dump, in_=xt[p], func=AF.Identity, accum_out=Ct(0, tt)))
            gq = B.op("scalar", lambda e, p=par, tt=t: e.activation(
                out=dump, in_=xt[p], func=AF.Square, accum_out=Ct(1, tt)))
            sq_cons[par] = [gq]
            B.pump(2)
        B.waitg("vector", [gq])
        B.op("vector", lambda e: e.tensor_scalar_mul(out=C(2), in0=C(0), scalar1=1.0 / 1024))
        B.op("vector", lambda e: e.tensor_scalar_mul(out=C(3), in0=C(1), scalar1=1.0 / 1024))
        B.op("vector", lambda e: e.tensor_mul(out=C(5), in0=C(2), in1=C(2)))
        B.op("vector", lambda e: e.tensor_sub(out=C(3), in0=C(3), in1=C(5)))
        B.op("vector", lambda e: e.tensor_scalar_add(out=C(3), in0=C(3), scalar1=EPS))
        g = B.op("vector", lambda e: e.reciprocal(out=C(4), in_=C(3)))
        B.waitg("scalar", [g])
        g = B.op("scalar", lambda e: e.activation(out=C(4), in_=C(4), func=AF.Sqrt))
        B.waitg("vector", [g])
        for _ in range(2):
            B.op("vector", lambda e: e.tensor_mul(out=C(5), in0=C(4), in1=C(4)))
            B.op("vector", lambda e: e.tensor_mul(out=C(5), in0=C(5), in1=C(3)))
            B.op("vector", lambda e: e.tensor_scalar(out=C(5), in0=C(5), scalar1=-0.5, scalar2=1.5, op0=ALU.mult, op1=ALU.add))
            B.op("vector", lambda e: e.tensor_mul(out=C(4), in0=C(4), in1=C(5)))
        B.op("vector", lambda e: e.tensor_mul(out=C(5), in0=C(2), in1=C(4)))
        gmr = B.op("vector", lambda e: e.tensor_scalar_mul(out=C(5), in0=C(5), scalar1=-1.0))
        B.waitg("scalar", [gmr])
        B.tx_done = {}
        sguard = [[], []]
        xt_cons = [list(sq_cons[0]), list(sq_cons[1])]
        xn_cons = [[], []]
        for t in range(NT):
            par = t % 2
            dv = B.dma("sync", xt[par], xv[t], "LX%d" % par, gate=xt_cons[par])
            B.waitg("scalar", [dv])
            B.waitg("scalar", xn_cons[par])
            gap = B.op("scalar", lambda e, p=par, tt=t: e.activation(
                out=xn[p], in_=xt[p], func=AF.Identity, scale=Ct(4, tt), bias=Ct(5, tt)))
            B.waitg("scalar", sguard[par])
            gx = B.op("scalar", lambda e, p=par: e.copy(out=xb[p], in_=xn[p]))
            B.waitg("vector", [gap])
            gab = B.op("vector", lambda e, p=par, tt=t: e.reduce_max(out=Ct(6, tt), in_=xn[p], axis=AX.X, apply_absolute_value=True))
            B.op("vector", lambda e, tt=t: e.tensor_scalar_max(out=Ct(6, tt), in0=Ct(6, tt), scalar1=1e-5))
            B.op("vector", lambda e, tt=t: e.tensor_scalar_mul(out=B.dq1_all[:, tt:tt + 1], in0=Ct(6, tt), scalar1=1.0 / 127.0))
            B.op("vector", lambda e, tt=t: e.reciprocal(out=Ct(7, tt), in_=Ct(6, tt)))
            gs1 = B.op("vector", lambda e, tt=t: e.tensor_scalar_mul(out=Ct(7, tt), in0=Ct(7, tt), scalar1=127.0))
            B.waitg("scalar", [gs1])
            gy = B.op("scalar", lambda e, p=par, tt=t: e.activation(
                out=xt[p], in_=xn[p], func=AF.Identity, scale=Ct(7, tt), bias=B.cpool[:, 3:4]))
            B.waitg("vector", [gy])
            gn = B.op("vector", lambda e, p=par: e.tensor_scalar(out=nb[p], in0=xt[p], scalar1=MAGIC, scalar2=None, op0=ALU.subtract))
            xt_cons[par] = [gn]
            xn_cons[par] = [gy, gab]
            B.waitg("gpsimd", [gn])
            s2 = B.dma("gpsimd", nv[t], nb[par], "LS%d" % par)
            txg = [B.dma("sync", B.xnfm_all.rearrange("p (c t) -> p c t", c=ND)[:, :, 128 * t:128 * t + 128],
                         xb[par].rearrange("t (c p) -> t c p", c=ND),
                         "TX%d" % par, transpose=True, gate=[gx])]
            sguard[par] = [s2] + txg
            B.tx_done[t] = txg
            B.emit_v_unit(t)
            B.pump(2)
        B.ln_final = sguard[0] + sguard[1]
        fa.reset(); ba.reset()

    def ph_attention(self):
        B = self
        ba, fa = B.ba, B.fa
        ba.mark(); fa.mark()
        xn_fm = B.xn_fm_t
        wv_region = B.att_wv_region
        wv_sb = B.att_wv
        v_sb = B.att_v
        wqk = B.att_wqk
        q_fm2 = [ba.alloc_at(wv_region, 2048), ba.alloc_at(wv_region + 4096, 2048)]
        k_fm2 = [ba.alloc_at(wv_region + 2048, 2048), ba.alloc_at(wv_region + 6144, 2048)]
        p_sb = [[ba.alloc(512) for i in range(3)] for l in range(2)]
        o_attn = ba.alloc(1024)
        o_un = [ba.alloc(512, parts=65) for _ in range(8)]
        srow = fa.alloc(2048, parts=1)
        oun_guard = [[] for _ in range(8)]
        B.r_row = fa.alloc(2048, parts=1)
        rr_guard = [[] for _ in range(4)]
        wstage2 = B.att_wstage
        stg_cons = B.att_stg_cons
        tv = [("TX%d" % p, B.C["TX%d" % p]) for p in range(2)]
        wcast = B.att_wcast
        B.store_done = {h: [] for h in range(H)}
        vs_ones = B.att_vs_ones
        B.waitg("tensor", [wcast[-1], vs_ones])
        SPOOL = {0: (2, 3, 4), 1: (5, 6, 7)}
        OBANK = {0: 0, 1: 1}
        wqk_guard = [[], []]
        qk_guard = []
        pv_guard = [[[], [], []], [[], [], []]]
        sa_guard = [[], []]
        norm_pending = []
        oun_next = [0]

        def emit_norm(maxn=1):
            for _ in range(maxn):
                if not norm_pending:
                    return
                norm_pending.pop(0)()

        def load_wqk(hp2):
            par2 = hp2 % 2
            B.waitg("sync", wqk_guard[par2])
            out = []
            for qk, c0 in ((0, 128 * hp2), (1, D + 128 * hp2)):
                sl = qk
                wvw = B.w_in[:, c0:c0 + 128].rearrange("(c p) f -> p c f", p=128)
                dv = B.dma("sync", wstage2[sl].rearrange("p (c f) -> p c f", f=128), wvw, "WST%d" % sl,
                           gate=[stg_cons[sl]] if stg_cons[sl] else [])
                B.waitg("gpsimd", [dv])
                stg_cons[sl] = B.op("gpsimd", lambda e, p=par2, q=qk, s2=sl: e.tensor_copy(
                    out=wqk[p][:, 1024 * q:1024 * q + 1024], in_=wstage2[s2]))
                out.append(stg_cons[sl])
            return out

        B.waitg("tensor", tv)
        B.waitg("scalar", B.ln_final)
        B.waitg("vector", B.ln_final)
        B.ln_final = None
        wq_cast_next = load_wqk(0)
        qk_guard2 = [[], []]
        for hp in range(8):
            par = hp % 2
            q_fm, k_fm = q_fm2[par], k_fm2[par]
            wq_cast = wq_cast_next
            B.waitg("tensor", [wq_cast[-1]])
            qk_ready = []
            for qk in range(2):
                for tc in range(4):
                    bank = (qk * 4 + tc) % 4
                    Pb = B.P[bank]
                    B.waitg("tensor", B.bguard[bank])
                    for c in range(ND - 1):
                        B.em("tensor", lambda e, b=Pb, cc=c, q=qk, t=tc, p=par: e.matmul(
                            b[:], wqk[p][:, 1024 * q + 128 * cc:1024 * q + 128 * cc + 128],
                            xn_fm[cc][:, 512 * t:512 * t + 512],
                            start=(cc == 0), stop=False))
                    tg = B.op("tensor", lambda e, b=Pb, q=qk, t=tc, p=par: e.matmul(
                        b[:], wqk[p][:, 1024 * q + 128 * (ND - 1):1024 * q + 128 * ND],
                        xn_fm[ND - 1][:, 512 * t:512 * t + 512],
                        start=False, stop=True))
                    B.waitg("scalar", [tg])
                    B.waitg("scalar", qk_guard2[par])
                    qk_guard2[par] = []
                    dst = q_fm if qk == 0 else k_fm
                    bias = B.b_qk[:, 8 * qk + hp:8 * qk + hp + 1]
                    if qk == 0:
                        g = B.op("scalar", lambda e, b=Pb, t=tc, dd=dst, bi=bias: e.activation(
                            out=dd[:, 512 * t:512 * t + 512], in_=b[:], func=AF.Identity,
                            scale=B.cpool[:, 2:3], bias=bi))
                    else:
                        g = B.op("scalar", lambda e, b=Pb, t=tc, dd=dst, bi=bias: e.activation(
                            out=dd[:, 512 * t:512 * t + 512], in_=b[:], func=AF.Identity, bias=bi))
                    B.bguard[bank] = [g]
                    qk_ready.append(g)
            B.waitg("tensor", [qk_ready[-1]])
            if hp < 7:
                wq_cast_next = load_wqk(hp + 1)
            if hp >= 2:
                c2 = hp - 2
                B.dma("sync", B.afm_t[c2], B.s_attn[128 * c2:128 * c2 + 128, :], "AFM",
                      gate=B.store_done[2 * c2] + B.store_done[2 * c2 + 1])
            qk_guard = []
            for qh in range(2):
                locs = [0, 1]
                plans = {loc: ATTN_PLANS[(2 * hp + loc, qh)] for loc in locs}
                keys = sorted({(st["j"], st["ki"]) for loc in locs for st in plans[loc][0]})
                stepmap = {loc: {(st["j"], st["ki"]): st for st in plans[loc][0]} for loc in locs}
                ready = {}

                def emit_v(si2, keys=keys, stepmap=stepmap, plans=plans, ready=ready, hp=hp, qh=qh):
                    sp2 = si2 % 3
                    infos = []
                    for loc2 in locs:
                        info = ready.pop((loc2, si2), None)
                        if info is None:
                            continue
                        ki2, j2, eg2, vg2, pbuf2, h2 = info
                        kivj = plans[loc2][1][j2]
                        first = (ki2 == kivj[0])
                        lastk = (ki2 == kivj[-1])
                        B.waitg("tensor", eg2)
                        B.waitg("tensor", vg2)
                        if first:
                            B.waitg("tensor", B.bguard[OBANK[loc2]])
                        infos.append((loc2, ki2, j2, pbuf2, h2, first, lastk))
                    for loc2, ki2, j2, pbuf2, h2, first, lastk in infos:
                        obank = OBANK[loc2]
                        g = B.op("tensor", lambda e, ob=B.P[obank], k=ki2, hh=h2, pb=pbuf2, f1=first, l1=lastk: e.matmul(
                            ob[0:65, :],
                            v_sb[:, 1040 * k + 65 * hh:1040 * k + 65 * hh + 65],
                            pb[:], start=f1, stop=l1))
                        pv_guard[loc2][sp2] = [g]
                        if lastk:
                            _norm_part1(loc2, j2, hp, qh, [g])

                def _norm_part1(loc2, j2, hp2, qh2, vlast):
                    while len(norm_pending) > 3:
                        emit_norm(1)
                    h2 = 2 * hp2 + loc2
                    jj = j2 % 2
                    obank = OBANK[loc2]
                    Ob = B.P[obank]
                    pu = oun_next[0]
                    oun_next[0] = (pu + 1) % 8
                    B.waitg("scalar", vlast)
                    B.waitg("scalar", oun_guard[pu])
                    ru = pu % 4
                    B.waitg("scalar", rr_guard[ru])
                    gs0 = B.op("scalar", lambda e, b=Ob, r2=ru: e.copy(
                        out=srow[0:1, 512 * r2:512 * r2 + 512], in_=b[64:65, :]))
                    gs_ = B.op("scalar", lambda e, b=Ob, p2=pu: e.copy(out=o_un[p2][0:64, :], in_=b[0:64, :]))
                    B.bguard[obank] = [gs_]
                    B.waitg("vector", [gs0])
                    gr = B.op("vector", lambda e, r2=ru: e.reciprocal(
                        out=B.r_row[0:1, 512 * r2:512 * r2 + 512],
                        in_=srow[0:1, 512 * r2:512 * r2 + 512]))

                    def part2(loc3=loc2, h3=h2, jj3=jj, qh3=qh2, pu3=pu, ru3=ru, gr3=gr, gs3=gs_):
                        rbank = SPOOL[loc3][0]
                        Rb = B.P[rbank]
                        B.waitg("tensor", [gr3])
                        B.waitg("tensor", B.bguard[rbank])
                        gt = B.op("tensor", lambda e, b=Rb, r2=ru3: e.matmul(
                            b[0:64, :], B.ones_row64[0:1, :],
                            B.r_row[0:1, 512 * r2:512 * r2 + 512], start=True, stop=True))
                        rr_guard[ru3] = [gt]
                        B.waitg("vector", [gt, gs3])
                        B.waitg("vector", sa_guard[jj3])
                        gv = B.op("vector", lambda e, b=Rb, p2=pu3, j2=jj3: e.tensor_tensor(
                            out=o_attn[0:64, 512 * j2:512 * j2 + 512],
                            in0=o_un[p2][0:64, :], in1=b[0:64, :], op=ALU.mult))
                        oun_guard[pu3] = [gv]
                        B.bguard[rbank] = [gv]
                        B.waitg("gpsimd", [gv])
                        sa_guard[jj3] = [B.dma("gpsimd",
                            B.s_attn[64 * h3:64 * h3 + 64, 1024 * qh3 + 512 * jj3:1024 * qh3 + 512 * jj3 + 512],
                            o_attn[0:64, 512 * jj3:512 * jj3 + 512], "SA%d" % jj3)]
                        B.store_done[h3] = B.store_done[h3] + sa_guard[jj3]
                    norm_pending.append(part2)

                final_blk = (hp == 7 and qh == 1)
                for si, (j, ki) in enumerate(keys):
                    sp = si % 3
                    if len(norm_pending) > (0 if final_blk else 3):
                        emit_norm(1)
                    for loc in locs:
                        st_ = stepmap[loc].get((j, ki))
                        if st_ is None:
                            continue
                        h = 2 * hp + loc
                        sbank = SPOOL[loc][sp]
                        Sb = B.P[sbank]
                        wlo, whi = st_["wlo"], st_["whi"]
                        B.waitg("tensor", B.bguard[sbank])
                        tg = B.op("tensor", lambda e, b=Sb, l=loc, k=ki, a=wlo, z=whi, kf=k_fm, qf=q_fm: e.matmul(
                            b[0:128, 0:z - a],
                            kf[64 * l:64 * l + 64, 128 * k:128 * k + 128],
                            qf[64 * l:64 * l + 64, a:z],
                            start=True, stop=True, tile_position=(64 * l, 0)))
                        qk_guard.append(tg)
                        B.waitg("scalar", [tg])
                        B.waitg("scalar", pv_guard[loc][sp])
                        pbuf = p_sb[loc][sp]
                        eg = []
                        for qg0, delta in st_["exps"]:
                            gw = GWH[h]
                            col = _EB_COLS[(h, delta)]
                            eg.append(B.op("scalar", lambda e, b=Sb, q=qg0, g0=gw, a=wlo, c=col, pb=pbuf, jj=j: e.activation(
                                out=pb[:, q - 512 * jj:q - 512 * jj + g0],
                                in_=b[0:128, q - a:q - a + g0],
                                func=AF.Exp, bias=B.eb[:, c:c + 1])))
                        B.bguard[sbank] = [eg[-1]]
                        vg = []
                        if st_["zeros"] or st_["mask"] is not None:
                            B.waitg("vector", [eg[-1]])
                            B.waitg("vector", pv_guard[loc][sp])
                        for z0, z1 in st_["zeros"]:
                            vg.append(B.op("vector", lambda e, a=z0, b2=z1, pb=pbuf, jj=j: e.memset(
                                pb[:, a - 512 * jj:b2 - 512 * jj], 0.0)))
                        if st_["mask"] is not None:
                            mk = st_["mask"]
                            vg.append(B.op("vector", lambda e, m=mk, pb=pbuf, jj=j: e.tensor_tensor(
                                out=pb[:, m - 512 * jj:m - 512 * jj + 128],
                                in0=pb[:, m - 512 * jj:m - 512 * jj + 128],
                                in1=B.m01[:], op=ALU.mult)))
                        ready[(loc, si)] = (ki, j, eg[-1:], vg[-1:], pbuf, h)
                    if si % 2 == 1:
                        B.pump(1)
                    if si >= 2:
                        emit_v(si - 2)
                if len(keys) >= 2:
                    emit_v(len(keys) - 2)
                emit_v(len(keys) - 1)
                B.pump(2)
            wqk_guard[par] = list(qk_guard[-1:])
            qk_guard2[par] = list(qk_guard[-1:])
            qk_guard = []
        emit_norm(100)
        B.pump(10 ** 6)
        ba.reset(); fa.reset()
        B.ba.reset()
        B.fa.reset()

    def ph_outproj(self):
        B = self
        ba, fa = B.ba, B.fa
        ba.mark()
        B.nfm_all = ba.alloc(ND * 2048)
        B.n_fm_t = [B.nfm_all[:, 2048 * c:2048 * c + 2048] for c in range(ND)]
        B.mg_all = ba.alloc(ND * 4096)
        B.mg_t = [B.mg_all[:, 4096 * c:4096 * c + 4096] for c in range(ND)]
        ba.mark(); fa.mark()
        attn_fm = B.afm_t
        wo_sb = B.wo_sb
        xt = [fa.alloc(1024) for _ in range(2)]
        acc = [fa.alloc(1024) for _ in range(2)]
        afm67 = []
        for c in (6, 7):
            afm67.append(B.dma("sync", attn_fm[c], B.s_attn[128 * c:128 * c + 128, :], "AFM",
                  gate=B.store_done[2 * c] + B.store_done[2 * c + 1]))
        t_f, v_f, s_f = B.lastg["tensor"], B.lastg["vector"], B.lastg["scalar"]
        sa_f = B.store_done[14] + B.store_done[15]
        B.tr_n = B.dma("sync", B.nfm_all.rearrange("p (c t) -> p c t", c=ND),
                       B.s_n.rearrange("t (c p) -> t c p", c=ND), "TRN", transpose=True,
                       gate=t_f + v_f + sa_f)
        B.tr_g = B.dma("sync", B.mg_all.rearrange("p (c t) -> p c t", c=ND),
                       B.s_mg.rearrange("t (c p) -> t c p", c=ND), "TRG", transpose=True,
                       gate=B.mdone.get("g", []) + B.mdone.get("u", []) + B.mdone.get("d", [])
                            + t_f + v_f + sa_f)
        B.waitg("vector", t_f + s_f)
        B.waitg("tensor", afm67 + v_f + s_f)
        xv = B.x.rearrange("(t p) d -> t p d", p=128)
        av = B.s_acc.rearrange("(t p) d -> t p d", p=128)
        ld_guard = [t_f + v_f + s_f, t_f + v_f + s_f]
        stg = [[], []]
        for t in range(NT):
            par = t % 2
            dv = B.dma("sync", xt[par], xv[t], "LX%d" % par, gate=ld_guard[par])
            evs = []
            for half in range(2):
                bank = (2 * t + half) % 4
                Pb = B.P[bank]
                B.waitg("tensor", B.bguard[bank])
                for c in range(ND):
                    B.em("tensor", lambda e, b=Pb, cc=c, tt=t, hh=half: e.matmul(
                        b[:], attn_fm[cc][:, 128 * tt:128 * tt + 128],
                        wo_sb[:, 1024 * cc + 512 * hh:1024 * cc + 512 * hh + 512],
                        start=(cc == 0), stop=False))
                tg = B.op("tensor", lambda e, b=Pb, tt=t, hh=half: e.matmul(
                    b[:], B.ones_row[0:1, 128 * tt:128 * tt + 128],
                    B.bo_row[0:1, 512 * hh:512 * hh + 512], start=False, stop=True))
                B.waitg("vector", [tg, dv])
                B.waitg("vector", stg[par])
                g = B.op("vector", lambda e, b=Pb, hh=half, p=par: e.tensor_tensor(
                    out=acc[p][:, 512 * hh:512 * hh + 512],
                    in0=xt[p][:, 512 * hh:512 * hh + 512], in1=b[:], op=ALU.add))
                B.bguard[bank] = [g]
                evs.append(g)
            ld_guard[par] = [evs[-1]]
            B.waitg("gpsimd", evs)
            stg[par] = [B.dma("gpsimd", av[t], acc[par], "LS%d" % par)]
        ba.reset(); fa.reset()
        B.outproj_done = [evs[-1]]

    def wquant_gen(self, wb, m2, sg):
        B = self
        accs = B.wqacc
        specs = [("g", B.w_gate, B.s_mg, 32, 1024), ("u", B.w_up, B.s_mu, 32, 1024),
                 ("d", B.w_down, B.s_md, 8, 4096)]
        cons_guard = [[], []]
        mstore = [[], []]
        sg_guard = []
        for wname, wsrc, mdst, ntile, fdim in specs:
            nsub = fdim // 1024
            wv = wsrc.rearrange("(t p) f -> t p f", p=128)
            gv = None
            ncols = ntile * nsub
            B.waitg("scalar", sg_guard)
            for t in range(ntile):
                for s_ in range(nsub):
                    idx = (t * nsub + s_) % 2
                    dv = B.dma("sync", wb[idx], wv[t, :, 1024 * s_:1024 * s_ + 1024], "WQ%d" % idx, gate=cons_guard[idx])
                    B.waitg("scalar", [dv])
                    col = t * nsub + s_
                    gv = B.op("scalar", lambda e, i=idx, c0=col: e.activation(
                        out=sg[0], in_=wb[i], func=AF.Abs, accum_out=accs[:, c0:c0 + 1]))
                    cons_guard[idx] = [gv]
                    yield
            B.waitg("vector", [gv])
            gv = B.op("vector", lambda e, nc2=ncols: e.reduce_sum(out=accs[:, 32:33], in_=accs[:, 0:nc2], axis=AX.X))
            B.waitg("tensor", [gv])
            B.waitg("tensor", B.bguard[4])
            gt = B.op("tensor", lambda e: e.matmul(B.P[4][0:1, 0:1], accs[:, 32:33], B.ones_col_f, start=True, stop=True))
            B.waitg("vector", [gt])
            B.op("vector", lambda e, n=float(ntile * fdim * 128): e.tensor_scalar_mul(out=accs[0:1, 33:34], in0=B.P[4][0:1, 0:1], scalar1=1.0 / n))
            B.op("vector", lambda e: e.tensor_scalar_max(out=accs[0:1, 33:34], in0=accs[0:1, 33:34], scalar1=1e-5))
            gv = B.op("vector", lambda e: e.reciprocal(out=accs[0:1, 34:35], in_=accs[0:1, 33:34]))
            B.bguard[4] = [gv]
            B.waitg("tensor", [gv])
            B.waitg("tensor", B.bguard[5])
            gt = B.op("tensor", lambda e: e.matmul(B.P[5][0:128, 0:2], B.ones_row128[0:1, :], accs[0:1, 33:35], start=True, stop=True))
            B.waitg("vector", [gt])
            B.op("vector", lambda e, w=wname: e.tensor_copy(out=B.mav[w], in_=B.P[5][0:128, 0:1]))
            gv = B.op("vector", lambda e, w=wname: e.tensor_copy(out=B.wsv[w], in_=B.P[5][0:128, 1:2]))
            B.bguard[5] = [gv]
            B.waitg("scalar", [gv])
            yield
            mv = mdst.rearrange("(t p) f -> t p f", p=128)
            for t in range(ntile):
                for s_ in range(nsub):
                    idx = (t * nsub + s_) % 2
                    dv = B.dma("sync", wb[idx], wv[t, :, 1024 * s_:1024 * s_ + 1024], "WQ%d" % idx, gate=cons_guard[idx])
                    B.waitg("vector", [dv])
                    B.waitg("vector", sg_guard)
                    B.op("vector", lambda e, i=idx, w=wname: e.tensor_scalar(
                        out=sg[0], in0=wb[i], scalar1=B.wsv[w], scalar2=0.5, op0=ALU.mult, op1=ALU.is_ge))
                    g2 = B.op("vector", lambda e, i=idx, w=wname: e.tensor_scalar(
                        out=sg[1], in0=wb[i], scalar1=B.wsv[w], scalar2=-0.5, op0=ALU.mult, op1=ALU.is_le))
                    B.waitg("vector", mstore[idx])
                    gv = B.op("vector", lambda e, i=idx: e.tensor_sub(out=m2[i], in0=sg[0], in1=sg[1]))
                    sg_guard = [gv]
                    cons_guard[idx] = [gv]
                    B.waitg("gpsimd", [gv])
                    mstore[idx] = [B.dma("gpsimd", mv[t, :, 1024 * s_:1024 * s_ + 1024], m2[idx], "MS%d" % idx)]
                    yield
            self.mdone[wname] = mstore[0] + mstore[1]

    def pump(self, n=1):
        for _ in range(n):
            if next(self.wq_gen, "done") == "done":
                break

    def emit_v_unit(self, ts_):
        B = self
        xn_fm, wv_sb, v_sb = B.xn_fm_t, B.att_wv, B.att_v
        B.waitg("tensor", B.tx_done[ts_])
        if ts_ == 0:
            B.waitg("tensor", [B.att_wcast[7], B.att_vs_ones])
        for half in range(2):
            bank = (ts_ * 2 + half) % 4
            Pb = B.P[bank]
            B.waitg("tensor", B.bguard[bank])
            for c in range(ND):
                B.em("tensor", lambda e, b=Pb, cc=c, t=ts_, hh=half: e.matmul(
                    b[:], xn_fm[cc][:, 128 * t:128 * t + 128],
                    wv_sb[:, 1024 * cc + 512 * hh:1024 * cc + 512 * hh + 512],
                    start=(cc == 0), stop=False))
            tg = B.op("tensor", lambda e, b=Pb, t=ts_, hh=half: e.matmul(
                b[:], B.ones_row[0:1, 128 * t:128 * t + 128],
                B.bv_row[0:1, 512 * hh:512 * hh + 512], start=False, stop=True))
            B.waitg("vector", [tg])
            last = None
            for hs in range(8):
                hg = half * 8 + hs
                last = B.op("vector", lambda e, b=Pb, t=ts_, h1=hs, h2=hg: e.tensor_copy(
                    out=v_sb[:, 1040 * t + 65 * h2:1040 * t + 65 * h2 + 64],
                    in_=b[:, 64 * h1:64 * h1 + 64]))
            B.bguard[bank] = [last]

    def ph_ffn_gateup(self):
        B = self
        ba, fa = B.ba, B.fa
        ba.mark(); fa.mark()
        n_fm, mg = B.n_fm_t, B.mg_t
        B.mu67_off = ba.off
        mu = [ba.alloc_at(B.op_region + 4096 * c, 4096) for c in range(6)] + [ba.alloc(4096) for _ in range(2)]
        n2b = ba.alloc(2048)
        h2 = [fa.alloc(4096) for _ in range(2)]
        st = B.stat
        B.barrier()
        mu_a = ba.alloc_at(B.op_region, 6 * 4096)
        tru = [B.dma("sync", mu_a.rearrange("p (c t) -> p c t", c=6),
                     B.s_mu[:, 0:768].rearrange("t (c p) -> t c p", c=6), "TRU", transpose=True,
                     gate=B.mdone.get("u", []) + B.outproj_done)]
        mu_b = mu[6]
        tru.append(B.dma("sync", ba.t[0:128, mu_b.offset - 0:mu_b.offset + 2 * 4096].rearrange("p (c t) -> p c t", c=2)
                         if False else None, None, "TRU", transpose=True) if False else
                   B.dma("sync", ba.alloc_at(B.mu67_off, 2 * 4096).rearrange("p (c t) -> p c t", c=2),
                         B.s_mu[:, 768:1024].rearrange("t (c p) -> t c p", c=2), "TRU", transpose=True))
        B.op("vector", lambda e: e.tensor_scalar(out=B.dqg_all[:], in0=B.dq1_all[:], scalar1=B.mav["g"], scalar2=None, op0=ALU.mult))
        gdq = B.op("vector", lambda e: e.tensor_scalar(out=B.dqu_all[:], in0=B.dq1_all[:], scalar1=B.mav["u"], scalar2=None, op0=ALU.mult))
        B.waitg("scalar", [gdq])
        n2v = B.s_n2.rearrange("(t p) f -> t p f", p=128)
        hguard = [[], []]
        n2guard = []
        mu_waited = False
        for t in range(NT):
            h_sb = h2[t % 2]
            for oc in range(8):
                bank = oc % 4
                bg_ = B.P[bank]
                B.waitg("tensor", B.bguard[bank])
                for c in range(ND - 1):
                    B.em("tensor", lambda e, b=bg_, cc=c, tt=t, o=oc: e.matmul(
                        b[:], n_fm[cc][:, 128 * tt:128 * tt + 128],
                        mg[cc][:, 512 * o:512 * o + 512],
                        start=(cc == 0), stop=False))
                tg = B.op("tensor", lambda e, b=bg_, tt=t, o=oc: e.matmul(
                    b[:], n_fm[ND - 1][:, 128 * tt:128 * tt + 128],
                    mg[ND - 1][:, 512 * o:512 * o + 512],
                    start=False, stop=True))
                B.waitg("scalar", [tg])
                B.waitg("scalar", hguard[t % 2])
                g1 = B.op("scalar", lambda e, b=bg_, tt=t, o=oc, hs=h_sb: e.activation(
                    out=hs[:, 512 * o:512 * o + 512], in_=b[:], func=AF.Silu,
                    scale=B.dqg_all[:, tt:tt + 1]))
                B.bguard[bank] = [g1]
            if not mu_waited:
                B.waitg("tensor", tru)
                mu_waited = True
            for oc in range(8):
                bank = oc % 4
                bu_ = B.P[bank]
                B.waitg("tensor", B.bguard[bank])
                for c in range(ND - 1):
                    B.em("tensor", lambda e, b=bu_, cc=c, tt=t, o=oc: e.matmul(
                        b[:], n_fm[cc][:, 128 * tt:128 * tt + 128],
                        mu[cc][:, 512 * o:512 * o + 512],
                        start=(cc == 0), stop=False))
                tg = B.op("tensor", lambda e, b=bu_, tt=t, o=oc: e.matmul(
                    b[:], n_fm[ND - 1][:, 128 * tt:128 * tt + 128],
                    mu[ND - 1][:, 512 * o:512 * o + 512],
                    start=False, stop=True))
                B.waitg("vector", [tg])
                B.waitg("vector", hguard[t % 2] if oc == 0 else [])
                gh = B.op("vector", lambda e, b=bu_, tt=t, o=oc, hs=h_sb: e.scalar_tensor_tensor(
                    out=hs[:, 512 * o:512 * o + 512], in0=b[:],
                    scalar=B.dqu_all[:, tt:tt + 1], in1=hs[:, 512 * o:512 * o + 512],
                    op0=ALU.mult, op1=ALU.mult))
                B.bguard[bank] = [gh]
            sc = 2 + (t % 2)
            B.op("vector", lambda e, hs=h_sb, c0=sc: e.reduce_max(out=st[:, c0:c0 + 1], in_=hs, axis=AX.X, apply_absolute_value=True))
            B.op("vector", lambda e, c0=sc: e.tensor_scalar_max(out=st[:, c0:c0 + 1], in0=st[:, c0:c0 + 1], scalar1=1e-5))
            B.op("vector", lambda e, tt=t, c0=sc: e.tensor_scalar_mul(out=B.dq2_all[:, tt:tt + 1], in0=st[:, c0:c0 + 1], scalar1=1.0 / 127.0))
            B.op("vector", lambda e, c0=sc: e.reciprocal(out=st[:, c0:c0 + 1], in_=st[:, c0:c0 + 1]))
            B.op("vector", lambda e, c0=sc: e.tensor_scalar_mul(out=st[:, c0:c0 + 1], in0=st[:, c0:c0 + 1], scalar1=127.0))
            B.op("vector", lambda e, hs=h_sb, c0=sc: e.tensor_scalar(out=hs, in0=hs, scalar1=st[:, c0:c0 + 1], scalar2=MAGIC, op0=ALU.mult, op1=ALU.add))
            hguard[t % 2] = []
            for hf in range(2):
                B.waitg("vector", n2guard)
                gq = B.op("vector", lambda e, h3=hf, hs=h_sb: e.tensor_scalar(
                    out=n2b, in0=hs[:, 2048 * h3:2048 * h3 + 2048], scalar1=MAGIC, scalar2=None, op0=ALU.subtract))
                hguard[t % 2].append(gq)
                B.waitg("gpsimd", [gq])
                n2guard = [B.dma("gpsimd", n2v[t, :, 2048 * hf:2048 * hf + 2048], n2b, "SN2")]
        B.gateup_tail = [tg] + n2guard
        ba.reset(); fa.reset()
        B.ba.reset()

    def ph_ffn_down(self):
        B = self
        ba, fa = B.ba, B.fa
        ba.reset()
        ba.mark(); fa.mark()
        md_all = ba.alloc(NO * 1024)
        md = [md_all[:, 1024 * c:1024 * c + 1024] for c in range(NO)]
        n2f_all = [ba.alloc(NO * 512) for _ in range(2)]
        n2f2 = [[n2f_all[p][:, 512 * c:512 * c + 512] for c in range(NO)] for p in range(2)]
        acc = [fa.alloc(1024) for _ in range(2)]
        ys = [fa.alloc(1024) for _ in range(2)]
        t_f2, s_f2 = B.lastg["tensor"], B.lastg["scalar"]
        dtr = B.dma("sync", md_all.rearrange("p (c t) -> p c t", c=NO),
              B.s_md.rearrange("t (c p) -> t c p", c=NO), "DTR", transpose=True,
              gate=B.gateup_tail + B.mdone.get("d", []) + t_f2 + s_f2)
        B.op("vector", lambda e: e.tensor_scalar(out=B.ddq_all[:], in0=B.dq2_all[:], scalar1=B.mav["d"], scalar2=None, op0=ALU.mult))
        v_f2 = B.lastg["vector"]
        B.waitg("tensor", [dtr])
        B.waitg("vector", t_f2 + s_f2)
        av = B.s_acc.rearrange("(t p) d -> t p d", p=128)
        yv = B.y.rearrange("(t p) d -> t p d", p=128)
        ld_guard = [t_f2 + v_f2 + s_f2, t_f2 + v_f2 + s_f2]
        st_guard = [[], []]
        tr_guard = [B.gateup_tail + t_f2 + v_f2 + s_f2, B.gateup_tail + t_f2 + v_f2 + s_f2]
        trs_done = [[], []]

        def load_n2(tg2):
            pp = tg2 % 2
            trs_done[pp] = [B.dma("sync", n2f_all[pp].rearrange("p (c t) -> p c t", c=NO),
                                  B.s_n2[512 * tg2:512 * tg2 + 512, :].rearrange("t (c p) -> t c p", c=NO),
                                  "TN%d" % pp, transpose=True, gate=tr_guard[pp])]
        load_n2(0)
        for tg_ in range(4):
            pp = tg_ % 2
            n2f = n2f2[pp]
            B.waitg("tensor", trs_done[pp])
            if tg_ < 3:
                load_n2(tg_ + 1)
            tr_guard[pp] = []
            for tl in range(4):
                t = 4 * tg_ + tl
                par = t % 2
                dv = B.dma("sync", acc[par], av[t], "LA%d" % par, gate=ld_guard[par])
                evs = []
                for half in range(2):
                    bank = (2 * t + half) % 4
                    Pb = B.P[bank]
                    B.waitg("tensor", B.bguard[bank])
                    for c in range(NO - 1):
                        B.em("tensor", lambda e, b=Pb, cc=c, lt=tl, hh=half, nf=n2f: e.matmul(
                            b[:], nf[cc][:, 128 * lt:128 * lt + 128],
                            md[cc][:, 512 * hh:512 * hh + 512],
                            start=(cc == 0), stop=False))
                    tgm = B.op("tensor", lambda e, b=Pb, lt=tl, hh=half, nf=n2f: e.matmul(
                        b[:], nf[NO - 1][:, 128 * lt:128 * lt + 128],
                        md[NO - 1][:, 512 * hh:512 * hh + 512],
                        start=False, stop=True))
                    tr_guard[pp].append(tgm)
                    B.waitg("vector", [tgm, dv])
                    B.waitg("vector", st_guard[par])
                    g = B.op("vector", lambda e, b=Pb, tt=t, hh=half, p=par: e.scalar_tensor_tensor(
                        out=ys[p][:, 512 * hh:512 * hh + 512], in0=b[:],
                        scalar=B.ddq_all[:, tt:tt + 1], in1=acc[p][:, 512 * hh:512 * hh + 512],
                        op0=ALU.mult, op1=ALU.add))
                    B.bguard[bank] = [g]
                    evs.append(g)
                ld_guard[par] = [evs[-1]]
                B.waitg("gpsimd", evs)
                st_guard[par] = [B.dma("gpsimd", yv[t], ys[par], "YS%d" % par)]
        ba.reset(); fa.reset()

    def finish(self):
        B = self
        for sname in list(self.C):
            if sname not in ENG_SEM.values() and self.C[sname] > 0:
                B.wait("gpsimd", sname, self.C[sname])

    def materialize(self):
        nc = self.nc
        with self.stack:
            with nc.Block() as block:
                def run(engine_name):
                    def body(eng):
                        for e, fn in self.sched:
                            if e == engine_name:
                                fn(eng)
                    return body
                block.sync(run("sync"))
                block.tensor(run("tensor"))
                block.vector(run("vector"))
                block.scalar(run("scalar"))
                block.gpsimd(run("gpsimd"))
        return nc


_BUILD_CACHE = {}


def _get_nc():
    if "nc" not in _BUILD_CACHE:
        _BUILD_CACHE["nc"] = Builder().build()
    return _BUILD_CACHE["nc"]


def kernel(L0_attn_x, L0_attn_w_in, L0_attn_b_in, L0_attn_w_out, L0_attn_b_out,
           L0_ffn_w_gate, L0_ffn_w_up, L0_ffn_w_down):
    x = np.ascontiguousarray(np.asarray(L0_attn_x, dtype=np.float32))
    w_in = np.ascontiguousarray(np.asarray(L0_attn_w_in, dtype=np.float32))
    b_in = np.ascontiguousarray(np.asarray(L0_attn_b_in, dtype=np.float32))
    w_out = np.ascontiguousarray(np.asarray(L0_attn_w_out, dtype=np.float32))
    b_out = np.ascontiguousarray(np.asarray(L0_attn_b_out, dtype=np.float32))
    w_gate = np.ascontiguousarray(np.asarray(L0_ffn_w_gate, dtype=np.float32))
    w_up = np.ascontiguousarray(np.asarray(L0_ffn_w_up, dtype=np.float32))
    w_down = np.ascontiguousarray(np.asarray(L0_ffn_w_down, dtype=np.float32))

    nc = _get_nc()
    in_maps = [dict(x=x[c], w_in=w_in, b_in=b_in, w_out=w_out, b_out=b_out,
                    w_gate=w_gate, w_up=w_up, w_down=w_down) for c in range(8)]
    res = run_bass_kernel_spmd(nc, in_maps, core_ids=list(range(8)))
    return np.stack([res.results[c]["y"] for c in range(8)], axis=0)


# version 26
# speedup vs baseline: 1.1302x; 1.0064x over previous
import contextlib
from collections import defaultdict

import numpy as np
import ml_dtypes

import concourse.bass as bass
import concourse.mybir as mybir
from concourse.bass_utils import run_bass_kernel_spmd

BFNP = ml_dtypes.bfloat16
F32 = mybir.dt.float32
BF16 = mybir.dt.bfloat16
AF = mybir.ActivationFunctionType
ALU = mybir.AluOpType
AX = mybir.AxisListType

L, D, H, DH, INNER = 2048, 1024, 16, 64, 4096
NT, ND, NO = L // 128, D // 128, INNER // 128
MAGIC = float(np.float32(1.5 * 2 ** 23))
EPS = 1e-5
SLOPES = [2.0 ** (-(h + 1) * 0.5) for h in range(H)]
GWH = [128, 256, 256, 512, 512] + [512] * 11
WTH = [127.0 + 45.0 / s for s in SLOPES]
ENG_SEM = {"tensor": "TS", "vector": "VS", "scalar": "SS", "gpsimd": "GS", "sync": "SY"}


def _attn_plan():
    plans = {}
    for h in range(H):
        gw, W = GWH[h], WTH[h]
        for qh in range(2):
            steps = []
            kiv = {j: [] for j in (2 * qh, 2 * qh + 1)}
            for ki in range(8 * qh + 8):
                k0 = 128 * ki
                for j in (2 * qh, 2 * qh + 1):
                    c0 = 512 * j
                    exps, zeros, mask = [], [], None
                    for qg0 in range(c0, c0 + 512, gw):
                        if qg0 + gw <= k0 or qg0 - k0 > W:
                            zeros.append((qg0, qg0 + gw))
                        else:
                            exps.append((qg0, qg0 - k0))
                            if k0 >= qg0:
                                if k0 > qg0:
                                    zeros.append((qg0, k0))
                                mask = k0
                    if exps:
                        wlo = min(q for q, _ in exps)
                        whi = max(q for q, _ in exps) + gw
                        steps.append(dict(ki=ki, j=j, wlo=wlo, whi=whi,
                                          exps=exps, zeros=zeros, mask=mask))
                        kiv[j].append(ki)
            plans[(h, qh)] = (steps, kiv)
    return plans


ATTN_PLANS = _attn_plan()
_EB_COLS = {}
for (_h, _qh), (_steps, _kiv) in ATTN_PLANS.items():
    for _st in _steps:
        for _qg0, _delta in _st["exps"]:
            _EB_COLS.setdefault((_h, _delta), len(_EB_COLS))


def _make_consts():
    kl = np.arange(128, dtype=np.float64)
    eb = np.zeros((128, max(8, len(_EB_COLS))), dtype=np.float32)
    for (h, delta), c in _EB_COLS.items():
        eb[:, c] = (SLOPES[h] * (kl - delta - GWH[h] / 2)).astype(np.float32)
    m01 = (np.arange(128)[:, None] <= np.arange(128)[None, :]).astype(BFNP)
    return eb, m01


class Arena:
    def __init__(self, nc, name, dtype, ncols):
        self.t = nc.alloc_sbuf_tensor(name, [128, ncols], dtype)
        self.n = ncols
        self.off = 0
        self.stack = []

    def alloc(self, cols, parts=128):
        assert self.off + cols <= self.n, f"arena overflow {self.off}+{cols}>{self.n}"
        ap = self.t[0:parts, self.off:self.off + cols]
        self.off += cols
        return ap

    def alloc_at(self, off, cols, parts=128):
        assert off + cols <= self.n
        return self.t[0:parts, off:off + cols]

    def mark(self):
        self.stack.append(self.off)

    def reset(self):
        self.off = self.stack.pop()


class Builder:
    def __init__(self):
        nc = bass.Bass("TRN2", target_bir_lowering=False)
        self.nc = nc
        self.sched = []
        self.C = defaultdict(int)
        self.stack = contextlib.ExitStack()
        self.sem = {s: self.stack.enter_context(nc.semaphore(s))
                    for s in ("TS", "VS", "SS", "GS", "SY")}
        self.lastg = {e: [] for e in ("tensor", "vector", "scalar", "gpsimd", "sync")}

        d = nc.dram_tensor
        self.x = d("x", [L, D], F32, kind="ExternalInput").ap()
        self.w_in = d("w_in", [D, 3 * D], F32, kind="ExternalInput").ap()
        self.b_in = d("b_in", [3 * D], F32, kind="ExternalInput").ap()
        self.w_out = d("w_out", [D, D], F32, kind="ExternalInput").ap()
        self.b_out = d("b_out", [D], F32, kind="ExternalInput").ap()
        self.w_gate = d("w_gate", [INNER, D], F32, kind="ExternalInput").ap()
        self.w_up = d("w_up", [INNER, D], F32, kind="ExternalInput").ap()
        self.w_down = d("w_down", [D, INNER], F32, kind="ExternalInput").ap()
        self.y = d("y", [L, D], F32, kind="ExternalOutput").ap()
        self.s_xnb = d("s_xnb", [L, D], BF16).ap()
        self.s_n = d("s_n", [L, D], BF16).ap()
        self.s_attn = d("s_attn", [D, L], BF16).ap()
        self.s_acc = d("s_acc", [L, D], F32).ap()
        self.s_mg = d("s_mg", [INNER, D], BF16).ap()
        self.s_mu = d("s_mu", [INNER, D], BF16).ap()
        self.s_md = d("s_md", [D, INNER], BF16).ap()
        self.s_n2 = d("s_n2", [L, INNER], BF16).ap()

        eb_np, m01_np = _make_consts()
        self.c_eb = nc.inline_tensor(eb_np, name="c_eb")
        self.c_m01 = nc.inline_tensor(m01_np, name="c_m01")
        self.n_eb = eb_np.shape[1]

        self.fa = Arena(nc, "fa", F32, 8832)
        self.ba = Arena(nc, "ba", BF16, 88320)

        self.eb = self.fa.alloc(self.n_eb)
        self.cpool = self.fa.alloc(4)
        self.dq1_all = self.fa.alloc(NT)
        self.dq2_all = self.fa.alloc(NT)
        self.dqg_all = self.fa.alloc(NT)
        self.dqu_all = self.fa.alloc(NT)
        self.ddq_all = self.fa.alloc(NT)
        self.b_qk = self.fa.alloc(16)
        self.wsv = {w: self.fa.alloc(1) for w in "gud"}
        self.mav = {w: self.fa.alloc(1) for w in "gud"}
        self.stat = self.fa.alloc(8)
        self.ones_col_f = self.fa.alloc(1)
        self.ones_row128 = self.fa.alloc(128, parts=1)
        self.ones_row64 = self.ones_row128[0:1, 0:64]

        self.m01 = self.ba.alloc(128)
        self.ones_row = self.ba.alloc(2048, parts=1)
        self.bv_row = self.ba.alloc(1024, parts=1)
        self.bo_row = self.ba.alloc(1024, parts=1)

        self.mdone = {}
        self.wqacc = self.fa.alloc(36)

        self.P = [nc.alloc_psum_tensor(f"pb{i}", [128, 512], F32) for i in range(8)]
        self.bguard = {i: [] for i in range(8)}

    def em(self, engine, fn):
        self.sched.append((engine, fn))

    def _sem(self, name):
        if name not in self.sem:
            self.sem[name] = self.stack.enter_context(self.nc.semaphore(name))
        return self.sem[name]

    def wait(self, engine, sem, val):
        if val > 0:
            self.em(engine, lambda e, s=self._sem(sem), v=val: e.wait_ge(s, v))

    def waitg(self, engine, guard):
        best = {}
        for s, v in guard:
            best[s] = max(best.get(s, 0), v)
        for s, v in best.items():
            self.wait(engine, s, v)

    def dma(self, queue, out, in_, sem, transpose=False, slow=False, gate=()):
        sh = self._sem(sem)
        self.waitg(queue, gate)
        self.C[sem] += 16
        v = self.C[sem]
        if transpose:
            self.em(queue, lambda e, o=out, i=in_, s=sh: e.dma_start_transpose(out=o, in_=i).then_inc(s, 16))
        else:
            self.em(queue, lambda e, o=out, i=in_, s=sh, sl=slow: e.dma_start(out=o, in_=i, allow_slow_non_contiguous=sl).then_inc(s, 16))
        return (sem, v)

    def op(self, engine, fn):
        s = ENG_SEM[engine]
        prev = 0 if engine == "tensor" else self.C[s]
        self.C[s] += 1
        v = self.C[s]

        def run(e, f=fn, sh=self.sem[s], p=prev, vv=v):
            if p > 0:
                e.wait_ge(sh, p)
            return f(e).then_inc(sh, 1)
        self.em(engine, run)
        self.lastg[engine] = [(s, v)]
        return (s, v)

    def barrier(self):
        vals = [(s, self.C[s]) for s in list(self.C) if self.C[s] > 0]
        for eng in ("sync", "tensor", "vector", "scalar", "gpsimd"):
            for s, v in vals:
                self.wait(eng, s, v)

    def build(self):
        self.ph_init()
        self.ph_ln_quant()
        self.ph_attention()
        self.ph_outproj()
        self.ph_ffn_gateup()
        self.ph_ffn_down()
        self.finish()
        return self.materialize()

    def ph_init(self):
        B = self
        B.dma("sync", B.eb, B.c_eb[:], "DIN")
        B.dma("sync", B.m01, B.c_m01[:], "DIN")
        bq = B.b_in[0:D].rearrange("(j p) -> p j", p=128)
        bk = B.b_in[D:2 * D].rearrange("(j p) -> p j", p=128)
        B.dma("sync", B.b_qk[:, 0:8], bq, "DIN", slow=True)
        B.dma("sync", B.b_qk[:, 8:16], bk, "DIN", slow=True)
        B.fa.mark()
        bvf = B.fa.alloc(1024, parts=1)
        bof = B.fa.alloc(1024, parts=1)
        B.dma("sync", bvf, B.b_in[2 * D:3 * D][None, :], "DIN")
        g = B.dma("sync", bof, B.b_out[None, :], "DIN")
        B.wait("vector", "DIN", g[1])
        B.op("vector", lambda e: e.memset(B.cpool[:, 0:1], -0.5))
        B.op("vector", lambda e: e.memset(B.cpool[:, 1:2], 0.5))
        B.op("vector", lambda e: e.memset(B.cpool[:, 2:3], 0.125))
        B.op("vector", lambda e: e.memset(B.cpool[:, 3:4], MAGIC))
        B.op("vector", lambda e: e.memset(B.ones_col_f, 1.0))
        B.op("vector", lambda e: e.memset(B.ones_row128, 1.0))
        B.op("vector", lambda e: e.memset(B.ones_row, 1.0))
        B.op("vector", lambda e, s=bvf: e.tensor_copy(out=B.bv_row, in_=s))
        B.op("vector", lambda e, s=bof: e.tensor_copy(out=B.bo_row, in_=s))
        B.op("vector", lambda e: e.tensor_scalar_mul(out=B.b_qk[:, 0:8], in0=B.b_qk[:, 0:8], scalar1=0.125))
        B.fa.reset()

    def ph_ln_quant(self):
        B = self
        B.barrier()
        fa, ba = B.fa, B.ba
        ba.mark()
        B.op_region = ba.off
        B.afm_t = [ba.alloc(2048) for _ in range(ND)]
        B.wo_sb = ba.alloc(ND * 1024)
        ba.mark()
        B.xnfm_all = ba.alloc(ND * 2048)
        B.xn_fm_t = [B.xnfm_all[:, 2048 * c:2048 * c + 2048] for c in range(ND)]
        B.wqm2 = [ba.alloc(1024) for _ in range(2)]
        B.wqsg = [ba.alloc(1024) for _ in range(2)]
        fa.mark()
        B.wqb = [fa.alloc(1024) for _ in range(2)]
        B.wq_gen = B.wquant_gen(B.wqb, B.wqm2, B.wqsg)
        B.att_wv_region = ba.off
        B.att_wv = ba.alloc(8 * 1024)
        B.att_v = ba.alloc(NT * 1040)
        B.att_wqk = [ba.alloc(2 * 1024) for _ in range(2)]
        B.att_wstage = [fa.alloc(1024) for _ in range(2)]
        B.att_stg_cons = [None, None]
        wvv0 = B.w_in[:, 2 * D:3 * D].rearrange("(c p) f -> c p f", p=128)
        wov0 = B.w_out.rearrange("(c p) f -> c p f", p=128)
        B.att_wcast = []
        for c in range(ND):
            sl = c % 2
            dv = B.dma("sync", B.att_wstage[sl], wvv0[c], "WST%d" % sl,
                       gate=[B.att_stg_cons[sl]] if B.att_stg_cons[sl] else [])
            B.waitg("gpsimd", [dv])
            B.att_stg_cons[sl] = B.op("gpsimd", lambda e, cc=c, s2=sl: e.tensor_copy(
                out=B.att_wv[:, 1024 * cc:1024 * cc + 1024], in_=B.att_wstage[s2]))
            B.att_wcast.append(B.att_stg_cons[sl])
        for c in range(ND):
            sl = c % 2
            dv = B.dma("sync", B.att_wstage[sl], wov0[c], "WST%d" % sl,
                       gate=[B.att_stg_cons[sl]] if B.att_stg_cons[sl] else [])
            B.waitg("gpsimd", [dv])
            B.att_stg_cons[sl] = B.op("gpsimd", lambda e, cc=c, s2=sl: e.tensor_copy(
                out=B.wo_sb[:, 1024 * cc:1024 * cc + 1024], in_=B.att_wstage[s2]))
        B.att_vs_ones = B.op("vector", lambda e: e.memset(B.att_v, 1.0))
        fa.mark(); ba.mark()
        xt = [fa.alloc(1024) for _ in range(2)]
        xn = [fa.alloc(1024) for _ in range(2)]
        dump = xn[0]
        ls = fa.alloc(8 * 16)
        nb = [ba.alloc(1024) for _ in range(2)]
        xb = [ba.alloc(1024) for _ in range(2)]
        xv = B.x.rearrange("(t p) d -> t p d", p=128)
        nv = B.s_n.rearrange("(t p) d -> t p d", p=128)

        def C(i):
            return ls[:, 16 * i:16 * i + 16]

        def Ct(i, t):
            return ls[:, 16 * i + t:16 * i + t + 1]

        sq_cons = [[], []]
        gq = None
        for t in range(NT):
            par = t % 2
            dv = B.dma("sync", xt[par], xv[t], "LX%d" % par, gate=sq_cons[par])
            B.waitg("scalar", [dv])
            B.op("scalar", lambda e, p=par, tt=t: e.activation(
                out=dump, in_=xt[p], func=AF.Identity, accum_out=Ct(0, tt)))
            gq = B.op("scalar", lambda e, p=par, tt=t: e.activation(
                out=dump, in_=xt[p], func=AF.Square, accum_out=Ct(1, tt)))
            sq_cons[par] = [gq]
            B.pump(2)
        B.waitg("vector", [gq])
        B.op("vector", lambda e: e.tensor_scalar_mul(out=C(2), in0=C(0), scalar1=1.0 / 1024))
        B.op("vector", lambda e: e.tensor_scalar_mul(out=C(3), in0=C(1), scalar1=1.0 / 1024))
        B.op("vector", lambda e: e.tensor_mul(out=C(5), in0=C(2), in1=C(2)))
        B.op("vector", lambda e: e.tensor_sub(out=C(3), in0=C(3), in1=C(5)))
        B.op("vector", lambda e: e.tensor_scalar_add(out=C(3), in0=C(3), scalar1=EPS))
        g = B.op("vector", lambda e: e.reciprocal(out=C(4), in_=C(3)))
        B.waitg("scalar", [g])
        g = B.op("scalar", lambda e: e.activation(out=C(4), in_=C(4), func=AF.Sqrt))
        B.waitg("vector", [g])
        for _ in range(2):
            B.op("vector", lambda e: e.tensor_mul(out=C(5), in0=C(4), in1=C(4)))
            B.op("vector", lambda e: e.tensor_mul(out=C(5), in0=C(5), in1=C(3)))
            B.op("vector", lambda e: e.tensor_scalar(out=C(5), in0=C(5), scalar1=-0.5, scalar2=1.5, op0=ALU.mult, op1=ALU.add))
            B.op("vector", lambda e: e.tensor_mul(out=C(4), in0=C(4), in1=C(5)))
        B.op("vector", lambda e: e.tensor_mul(out=C(5), in0=C(2), in1=C(4)))
        gmr = B.op("vector", lambda e: e.tensor_scalar_mul(out=C(5), in0=C(5), scalar1=-1.0))
        B.waitg("scalar", [gmr])
        B.tx_done = {}
        sguard = [[], []]
        xt_cons = [list(sq_cons[0]), list(sq_cons[1])]
        xn_cons = [[], []]
        for t in range(NT):
            par = t % 2
            dv = B.dma("sync", xt[par], xv[t], "LX%d" % par, gate=xt_cons[par])
            B.waitg("scalar", [dv])
            B.waitg("scalar", xn_cons[par])
            gap = B.op("scalar", lambda e, p=par, tt=t: e.activation(
                out=xn[p], in_=xt[p], func=AF.Identity, scale=Ct(4, tt), bias=Ct(5, tt)))
            B.waitg("scalar", sguard[par])
            gx = B.op("scalar", lambda e, p=par: e.copy(out=xb[p], in_=xn[p]))
            B.waitg("vector", [gap])
            gab = B.op("vector", lambda e, p=par, tt=t: e.reduce_max(out=Ct(6, tt), in_=xn[p], axis=AX.X, apply_absolute_value=True))
            B.op("vector", lambda e, tt=t: e.tensor_scalar_max(out=Ct(6, tt), in0=Ct(6, tt), scalar1=1e-5))
            B.op("vector", lambda e, tt=t: e.tensor_scalar_mul(out=B.dq1_all[:, tt:tt + 1], in0=Ct(6, tt), scalar1=1.0 / 127.0))
            B.op("vector", lambda e, tt=t: e.reciprocal(out=Ct(7, tt), in_=Ct(6, tt)))
            gs1 = B.op("vector", lambda e, tt=t: e.tensor_scalar_mul(out=Ct(7, tt), in0=Ct(7, tt), scalar1=127.0))
            B.waitg("scalar", [gs1])
            gy = B.op("scalar", lambda e, p=par, tt=t: e.activation(
                out=xt[p], in_=xn[p], func=AF.Identity, scale=Ct(7, tt), bias=B.cpool[:, 3:4]))
            B.waitg("vector", [gy])
            gn = B.op("vector", lambda e, p=par: e.tensor_scalar(out=nb[p], in0=xt[p], scalar1=MAGIC, scalar2=None, op0=ALU.subtract))
            xt_cons[par] = [gn]
            xn_cons[par] = [gy, gab]
            B.waitg("gpsimd", [gn])
            s2 = B.dma("gpsimd", nv[t], nb[par], "LS%d" % par)
            txg = [B.dma("sync", B.xnfm_all.rearrange("p (c t) -> p c t", c=ND)[:, :, 128 * t:128 * t + 128],
                         xb[par].rearrange("t (c p) -> t c p", c=ND),
                         "TX%d" % par, transpose=True, gate=[gx])]
            sguard[par] = [s2] + txg
            B.tx_done[t] = txg
            B.emit_v_unit(t)
            B.pump(2)
        B.ln_final = sguard[0] + sguard[1]
        fa.reset(); ba.reset()

    def ph_attention(self):
        B = self
        ba, fa = B.ba, B.fa
        ba.mark(); fa.mark()
        xn_fm = B.xn_fm_t
        wv_region = B.att_wv_region
        wv_sb = B.att_wv
        v_sb = B.att_v
        wqk = B.att_wqk
        q_fm2 = [ba.alloc_at(wv_region, 2048), ba.alloc_at(wv_region + 4096, 2048)]
        k_fm2 = [ba.alloc_at(wv_region + 2048, 2048), ba.alloc_at(wv_region + 6144, 2048)]
        p_sb = [[ba.alloc(512) for i in range(3)] for l in range(2)]
        o_attn = ba.alloc(1024)
        o_un = [ba.alloc(512, parts=65) for _ in range(8)]
        srow = fa.alloc(2048, parts=1)
        oun_guard = [[] for _ in range(8)]
        B.r_row = fa.alloc(2048, parts=1)
        rr_guard = [[] for _ in range(4)]
        wstage2 = B.att_wstage
        stg_cons = B.att_stg_cons
        tv = [("TX%d" % p, B.C["TX%d" % p]) for p in range(2)]
        wcast = B.att_wcast
        B.store_done = {h: [] for h in range(H)}
        vs_ones = B.att_vs_ones
        B.waitg("tensor", [wcast[-1], vs_ones])
        SPOOL = {0: (2, 3, 4), 1: (5, 6, 7)}
        OBANK = {0: 0, 1: 1}
        wqk_guard = [[], []]
        qk_guard = []
        pv_guard = [[[], [], []], [[], [], []]]
        sa_guard = [[], []]
        norm_pending = []
        oun_next = [0]

        def emit_norm(maxn=1):
            for _ in range(maxn):
                if not norm_pending:
                    return
                norm_pending.pop(0)()

        def load_wqk(hp2):
            par2 = hp2 % 2
            B.waitg("sync", wqk_guard[par2])
            out = []
            for qk, c0 in ((0, 128 * hp2), (1, D + 128 * hp2)):
                sl = qk
                wvw = B.w_in[:, c0:c0 + 128].rearrange("(c p) f -> p c f", p=128)
                dv = B.dma("sync", wstage2[sl].rearrange("p (c f) -> p c f", f=128), wvw, "WST%d" % sl,
                           gate=[stg_cons[sl]] if stg_cons[sl] else [])
                B.waitg("gpsimd", [dv])
                stg_cons[sl] = B.op("gpsimd", lambda e, p=par2, q=qk, s2=sl: e.tensor_copy(
                    out=wqk[p][:, 1024 * q:1024 * q + 1024], in_=wstage2[s2]))
                out.append(stg_cons[sl])
            return out

        B.waitg("tensor", tv)
        B.waitg("scalar", B.ln_final)
        B.waitg("vector", B.ln_final)
        B.ln_final = None
        wq_cast_next = load_wqk(0)
        qk_guard2 = [[], []]
        for hp in range(8):
            par = hp % 2
            q_fm, k_fm = q_fm2[par], k_fm2[par]
            wq_cast = wq_cast_next
            B.waitg("tensor", [wq_cast[-1]])
            qk_ready = []
            for qk in range(2):
                for tc in range(4):
                    bank = (qk * 4 + tc) % 4
                    Pb = B.P[bank]
                    B.waitg("tensor", B.bguard[bank])
                    for c in range(ND - 1):
                        B.em("tensor", lambda e, b=Pb, cc=c, q=qk, t=tc, p=par: e.matmul(
                            b[:], wqk[p][:, 1024 * q + 128 * cc:1024 * q + 128 * cc + 128],
                            xn_fm[cc][:, 512 * t:512 * t + 512],
                            start=(cc == 0), stop=False))
                    tg = B.op("tensor", lambda e, b=Pb, q=qk, t=tc, p=par: e.matmul(
                        b[:], wqk[p][:, 1024 * q + 128 * (ND - 1):1024 * q + 128 * ND],
                        xn_fm[ND - 1][:, 512 * t:512 * t + 512],
                        start=False, stop=True))
                    B.waitg("scalar", [tg])
                    B.waitg("scalar", qk_guard2[par])
                    qk_guard2[par] = []
                    dst = q_fm if qk == 0 else k_fm
                    bias = B.b_qk[:, 8 * qk + hp:8 * qk + hp + 1]
                    if qk == 0:
                        g = B.op("scalar", lambda e, b=Pb, t=tc, dd=dst, bi=bias: e.activation(
                            out=dd[:, 512 * t:512 * t + 512], in_=b[:], func=AF.Identity,
                            scale=B.cpool[:, 2:3], bias=bi))
                    else:
                        g = B.op("scalar", lambda e, b=Pb, t=tc, dd=dst, bi=bias: e.activation(
                            out=dd[:, 512 * t:512 * t + 512], in_=b[:], func=AF.Identity, bias=bi))
                    B.bguard[bank] = [g]
                    qk_ready.append(g)
            B.waitg("tensor", [qk_ready[-1]])
            if hp < 7:
                wq_cast_next = load_wqk(hp + 1)
            if hp >= 2:
                c2 = hp - 2
                B.dma("sync", B.afm_t[c2], B.s_attn[128 * c2:128 * c2 + 128, :], "AFM",
                      gate=B.store_done[2 * c2] + B.store_done[2 * c2 + 1])
            qk_guard = []
            for qh in range(2):
                locs = [0, 1]
                plans = {loc: ATTN_PLANS[(2 * hp + loc, qh)] for loc in locs}
                keys = sorted({(st["j"], st["ki"]) for loc in locs for st in plans[loc][0]})
                stepmap = {loc: {(st["j"], st["ki"]): st for st in plans[loc][0]} for loc in locs}
                ready = {}

                def emit_v(si2, keys=keys, stepmap=stepmap, plans=plans, ready=ready, hp=hp, qh=qh):
                    sp2 = si2 % 3
                    infos = []
                    for loc2 in locs:
                        info = ready.pop((loc2, si2), None)
                        if info is None:
                            continue
                        ki2, j2, eg2, vg2, pbuf2, h2 = info
                        kivj = plans[loc2][1][j2]
                        first = (ki2 == kivj[0])
                        lastk = (ki2 == kivj[-1])
                        B.waitg("tensor", eg2)
                        B.waitg("tensor", vg2)
                        if first:
                            B.waitg("tensor", B.bguard[OBANK[loc2]])
                        infos.append((loc2, ki2, j2, pbuf2, h2, first, lastk))
                    for loc2, ki2, j2, pbuf2, h2, first, lastk in infos:
                        obank = OBANK[loc2]
                        g = B.op("tensor", lambda e, ob=B.P[obank], k=ki2, hh=h2, pb=pbuf2, f1=first, l1=lastk: e.matmul(
                            ob[0:65, :],
                            v_sb[:, 1040 * k + 65 * hh:1040 * k + 65 * hh + 65],
                            pb[:], start=f1, stop=l1))
                        pv_guard[loc2][sp2] = [g]
                        if lastk:
                            _norm_part1(loc2, j2, hp, qh, [g])

                def _norm_part1(loc2, j2, hp2, qh2, vlast):
                    while len(norm_pending) > 3:
                        emit_norm(1)
                    h2 = 2 * hp2 + loc2
                    jj = j2 % 2
                    obank = OBANK[loc2]
                    Ob = B.P[obank]
                    pu = oun_next[0]
                    oun_next[0] = (pu + 1) % 8
                    B.waitg("scalar", vlast)
                    B.waitg("scalar", oun_guard[pu])
                    ru = pu % 4
                    B.waitg("scalar", rr_guard[ru])
                    gs0 = B.op("scalar", lambda e, b=Ob, r2=ru: e.copy(
                        out=srow[0:1, 512 * r2:512 * r2 + 512], in_=b[64:65, :]))
                    gs_ = B.op("scalar", lambda e, b=Ob, p2=pu: e.copy(out=o_un[p2][0:64, :], in_=b[0:64, :]))
                    B.bguard[obank] = [gs_]
                    B.waitg("vector", [gs0])
                    gr = B.op("vector", lambda e, r2=ru: e.reciprocal(
                        out=B.r_row[0:1, 512 * r2:512 * r2 + 512],
                        in_=srow[0:1, 512 * r2:512 * r2 + 512]))

                    def part2(loc3=loc2, h3=h2, jj3=jj, qh3=qh2, pu3=pu, ru3=ru, gr3=gr, gs3=gs_):
                        rbank = SPOOL[loc3][2]
                        Rb = B.P[rbank]
                        B.waitg("tensor", [gr3])
                        B.waitg("tensor", B.bguard[rbank])
                        gt = B.op("tensor", lambda e, b=Rb, r2=ru3: e.matmul(
                            b[0:64, :], B.ones_row64[0:1, :],
                            B.r_row[0:1, 512 * r2:512 * r2 + 512], start=True, stop=True))
                        rr_guard[ru3] = [gt]
                        B.waitg("vector", [gt, gs3])
                        B.waitg("vector", sa_guard[jj3])
                        gv = B.op("vector", lambda e, b=Rb, p2=pu3, j2=jj3: e.tensor_tensor(
                            out=o_attn[0:64, 512 * j2:512 * j2 + 512],
                            in0=o_un[p2][0:64, :], in1=b[0:64, :], op=ALU.mult))
                        oun_guard[pu3] = [gv]
                        B.bguard[rbank] = [gv]
                        B.waitg("gpsimd", [gv])
                        sa_guard[jj3] = [B.dma("gpsimd",
                            B.s_attn[64 * h3:64 * h3 + 64, 1024 * qh3 + 512 * jj3:1024 * qh3 + 512 * jj3 + 512],
                            o_attn[0:64, 512 * jj3:512 * jj3 + 512], "SA%d" % jj3)]
                        B.store_done[h3] = B.store_done[h3] + sa_guard[jj3]
                    norm_pending.append(part2)

                final_blk = (hp == 7 and qh == 1)
                for si, (j, ki) in enumerate(keys):
                    sp = si % 3
                    if len(norm_pending) > (0 if final_blk else 3):
                        emit_norm(1)
                    for loc in locs:
                        st_ = stepmap[loc].get((j, ki))
                        if st_ is None:
                            continue
                        h = 2 * hp + loc
                        sbank = SPOOL[loc][sp]
                        Sb = B.P[sbank]
                        wlo, whi = st_["wlo"], st_["whi"]
                        B.waitg("tensor", B.bguard[sbank])
                        tg = B.op("tensor", lambda e, b=Sb, l=loc, k=ki, a=wlo, z=whi, kf=k_fm, qf=q_fm: e.matmul(
                            b[0:128, 0:z - a],
                            kf[64 * l:64 * l + 64, 128 * k:128 * k + 128],
                            qf[64 * l:64 * l + 64, a:z],
                            start=True, stop=True, tile_position=(64 * l, 0)))
                        qk_guard.append(tg)
                        B.waitg("scalar", [tg])
                        B.waitg("scalar", pv_guard[loc][sp])
                        pbuf = p_sb[loc][sp]
                        eg = []
                        for qg0, delta in st_["exps"]:
                            gw = GWH[h]
                            col = _EB_COLS[(h, delta)]
                            eg.append(B.op("scalar", lambda e, b=Sb, q=qg0, g0=gw, a=wlo, c=col, pb=pbuf, jj=j: e.activation(
                                out=pb[:, q - 512 * jj:q - 512 * jj + g0],
                                in_=b[0:128, q - a:q - a + g0],
                                func=AF.Exp, bias=B.eb[:, c:c + 1])))
                        B.bguard[sbank] = [eg[-1]]
                        vg = []
                        if st_["zeros"] or st_["mask"] is not None:
                            B.waitg("vector", [eg[-1]])
                            B.waitg("vector", pv_guard[loc][sp])
                        for z0, z1 in st_["zeros"]:
                            vg.append(B.op("vector", lambda e, a=z0, b2=z1, pb=pbuf, jj=j: e.memset(
                                pb[:, a - 512 * jj:b2 - 512 * jj], 0.0)))
                        if st_["mask"] is not None:
                            mk = st_["mask"]
                            vg.append(B.op("vector", lambda e, m=mk, pb=pbuf, jj=j: e.tensor_tensor(
                                out=pb[:, m - 512 * jj:m - 512 * jj + 128],
                                in0=pb[:, m - 512 * jj:m - 512 * jj + 128],
                                in1=B.m01[:], op=ALU.mult)))
                        ready[(loc, si)] = (ki, j, eg[-1:], vg[-1:], pbuf, h)
                    if si % 2 == 1:
                        B.pump(1)
                    if si >= 2:
                        emit_v(si - 2)
                if len(keys) >= 2:
                    emit_v(len(keys) - 2)
                emit_v(len(keys) - 1)
                B.pump(2)
            wqk_guard[par] = list(qk_guard[-1:])
            qk_guard2[par] = list(qk_guard[-1:])
            qk_guard = []
        emit_norm(100)
        B.pump(10 ** 6)
        ba.reset(); fa.reset()
        B.ba.reset()
        B.fa.reset()

    def ph_outproj(self):
        B = self
        ba, fa = B.ba, B.fa
        ba.mark()
        B.nfm_all = ba.alloc(ND * 2048)
        B.n_fm_t = [B.nfm_all[:, 2048 * c:2048 * c + 2048] for c in range(ND)]
        B.mg_all = ba.alloc(ND * 4096)
        B.mg_t = [B.mg_all[:, 4096 * c:4096 * c + 4096] for c in range(ND)]
        ba.mark(); fa.mark()
        attn_fm = B.afm_t
        wo_sb = B.wo_sb
        xt = [fa.alloc(1024) for _ in range(2)]
        acc = [fa.alloc(1024) for _ in range(2)]
        afm67 = []
        for c in (6, 7):
            afm67.append(B.dma("sync", attn_fm[c], B.s_attn[128 * c:128 * c + 128, :], "AFM",
                  gate=B.store_done[2 * c] + B.store_done[2 * c + 1]))
        t_f, v_f, s_f = B.lastg["tensor"], B.lastg["vector"], B.lastg["scalar"]
        sa_f = B.store_done[14] + B.store_done[15]
        B.tr_n = B.dma("sync", B.nfm_all.rearrange("p (c t) -> p c t", c=ND),
                       B.s_n.rearrange("t (c p) -> t c p", c=ND), "TRN", transpose=True,
                       gate=t_f + v_f + sa_f)
        B.tr_g = B.dma("sync", B.mg_all.rearrange("p (c t) -> p c t", c=ND),
                       B.s_mg.rearrange("t (c p) -> t c p", c=ND), "TRG", transpose=True,
                       gate=B.mdone.get("g", []) + B.mdone.get("u", []) + B.mdone.get("d", [])
                            + t_f + v_f + sa_f)
        B.waitg("vector", t_f + s_f)
        B.waitg("tensor", afm67 + v_f + s_f)
        xv = B.x.rearrange("(t p) d -> t p d", p=128)
        av = B.s_acc.rearrange("(t p) d -> t p d", p=128)
        ld_guard = [t_f + v_f + s_f, t_f + v_f + s_f]
        stg = [[], []]
        for t in range(NT):
            par = t % 2
            dv = B.dma("scalar", xt[par], xv[t], "LX%d" % par, gate=ld_guard[par])
            evs = []
            for half in range(2):
                bank = (2 * t + half) % 4
                Pb = B.P[bank]
                B.waitg("tensor", B.bguard[bank])
                for c in range(ND):
                    B.em("tensor", lambda e, b=Pb, cc=c, tt=t, hh=half: e.matmul(
                        b[:], attn_fm[cc][:, 128 * tt:128 * tt + 128],
                        wo_sb[:, 1024 * cc + 512 * hh:1024 * cc + 512 * hh + 512],
                        start=(cc == 0), stop=False))
                tg = B.op("tensor", lambda e, b=Pb, tt=t, hh=half: e.matmul(
                    b[:], B.ones_row[0:1, 128 * tt:128 * tt + 128],
                    B.bo_row[0:1, 512 * hh:512 * hh + 512], start=False, stop=True))
                B.waitg("vector", [tg, dv])
                B.waitg("vector", stg[par])
                g = B.op("vector", lambda e, b=Pb, hh=half, p=par: e.tensor_tensor(
                    out=acc[p][:, 512 * hh:512 * hh + 512],
                    in0=xt[p][:, 512 * hh:512 * hh + 512], in1=b[:], op=ALU.add))
                B.bguard[bank] = [g]
                evs.append(g)
            ld_guard[par] = [evs[-1]]
            B.waitg("gpsimd", evs)
            stg[par] = [B.dma("gpsimd", av[t], acc[par], "LS%d" % par)]
        ba.reset(); fa.reset()
        B.outproj_done = [evs[-1]]

    def wquant_gen(self, wb, m2, sg):
        B = self
        accs = B.wqacc
        specs = [("g", B.w_gate, B.s_mg, 32, 1024), ("u", B.w_up, B.s_mu, 32, 1024),
                 ("d", B.w_down, B.s_md, 8, 4096)]
        cons_guard = [[], []]
        mstore = [[], []]
        sg_guard = []
        for wname, wsrc, mdst, ntile, fdim in specs:
            nsub = fdim // 1024
            wv = wsrc.rearrange("(t p) f -> t p f", p=128)
            gv = None
            ncols = ntile * nsub
            B.waitg("scalar", sg_guard)
            for t in range(ntile):
                for s_ in range(nsub):
                    idx = (t * nsub + s_) % 2
                    dv = B.dma("sync", wb[idx], wv[t, :, 1024 * s_:1024 * s_ + 1024], "WQ%d" % idx, gate=cons_guard[idx])
                    B.waitg("scalar", [dv])
                    col = t * nsub + s_
                    gv = B.op("scalar", lambda e, i=idx, c0=col: e.activation(
                        out=sg[0], in_=wb[i], func=AF.Abs, accum_out=accs[:, c0:c0 + 1]))
                    cons_guard[idx] = [gv]
                    yield
            B.waitg("vector", [gv])
            gv = B.op("vector", lambda e, nc2=ncols: e.reduce_sum(out=accs[:, 32:33], in_=accs[:, 0:nc2], axis=AX.X))
            B.waitg("tensor", [gv])
            B.waitg("tensor", B.bguard[4])
            gt = B.op("tensor", lambda e: e.matmul(B.P[4][0:1, 0:1], accs[:, 32:33], B.ones_col_f, start=True, stop=True))
            B.waitg("vector", [gt])
            B.op("vector", lambda e, n=float(ntile * fdim * 128): e.tensor_scalar_mul(out=accs[0:1, 33:34], in0=B.P[4][0:1, 0:1], scalar1=1.0 / n))
            B.op("vector", lambda e: e.tensor_scalar_max(out=accs[0:1, 33:34], in0=accs[0:1, 33:34], scalar1=1e-5))
            gv = B.op("vector", lambda e: e.reciprocal(out=accs[0:1, 34:35], in_=accs[0:1, 33:34]))
            B.bguard[4] = [gv]
            B.waitg("tensor", [gv])
            B.waitg("tensor", B.bguard[5])
            gt = B.op("tensor", lambda e: e.matmul(B.P[5][0:128, 0:2], B.ones_row128[0:1, :], accs[0:1, 33:35], start=True, stop=True))
            B.waitg("vector", [gt])
            B.op("vector", lambda e, w=wname: e.tensor_copy(out=B.mav[w], in_=B.P[5][0:128, 0:1]))
            gv = B.op("vector", lambda e, w=wname: e.tensor_copy(out=B.wsv[w], in_=B.P[5][0:128, 1:2]))
            B.bguard[5] = [gv]
            B.waitg("scalar", [gv])
            yield
            mv = mdst.rearrange("(t p) f -> t p f", p=128)
            for t in range(ntile):
                for s_ in range(nsub):
                    idx = (t * nsub + s_) % 2
                    dv = B.dma("sync", wb[idx], wv[t, :, 1024 * s_:1024 * s_ + 1024], "WQ%d" % idx, gate=cons_guard[idx])
                    B.waitg("vector", [dv])
                    B.waitg("vector", sg_guard)
                    B.op("vector", lambda e, i=idx, w=wname: e.tensor_scalar(
                        out=sg[0], in0=wb[i], scalar1=B.wsv[w], scalar2=0.5, op0=ALU.mult, op1=ALU.is_ge))
                    g2 = B.op("vector", lambda e, i=idx, w=wname: e.tensor_scalar(
                        out=sg[1], in0=wb[i], scalar1=B.wsv[w], scalar2=-0.5, op0=ALU.mult, op1=ALU.is_le))
                    B.waitg("vector", mstore[idx])
                    gv = B.op("vector", lambda e, i=idx: e.tensor_sub(out=m2[i], in0=sg[0], in1=sg[1]))
                    sg_guard = [gv]
                    cons_guard[idx] = [gv]
                    B.waitg("gpsimd", [gv])
                    mstore[idx] = [B.dma("gpsimd", mv[t, :, 1024 * s_:1024 * s_ + 1024], m2[idx], "MS%d" % idx)]
                    yield
            self.mdone[wname] = mstore[0] + mstore[1]

    def pump(self, n=1):
        for _ in range(n):
            if next(self.wq_gen, "done") == "done":
                break

    def emit_v_unit(self, ts_):
        B = self
        xn_fm, wv_sb, v_sb = B.xn_fm_t, B.att_wv, B.att_v
        B.waitg("tensor", B.tx_done[ts_])
        if ts_ == 0:
            B.waitg("tensor", [B.att_wcast[7], B.att_vs_ones])
        for half in range(2):
            bank = (ts_ * 2 + half) % 4
            Pb = B.P[bank]
            B.waitg("tensor", B.bguard[bank])
            for c in range(ND):
                B.em("tensor", lambda e, b=Pb, cc=c, t=ts_, hh=half: e.matmul(
                    b[:], xn_fm[cc][:, 128 * t:128 * t + 128],
                    wv_sb[:, 1024 * cc + 512 * hh:1024 * cc + 512 * hh + 512],
                    start=(cc == 0), stop=False))
            tg = B.op("tensor", lambda e, b=Pb, t=ts_, hh=half: e.matmul(
                b[:], B.ones_row[0:1, 128 * t:128 * t + 128],
                B.bv_row[0:1, 512 * hh:512 * hh + 512], start=False, stop=True))
            B.waitg("vector", [tg])
            last = None
            for hs in range(8):
                hg = half * 8 + hs
                last = B.op("vector", lambda e, b=Pb, t=ts_, h1=hs, h2=hg: e.tensor_copy(
                    out=v_sb[:, 1040 * t + 65 * h2:1040 * t + 65 * h2 + 64],
                    in_=b[:, 64 * h1:64 * h1 + 64]))
            B.bguard[bank] = [last]

    def ph_ffn_gateup(self):
        B = self
        ba, fa = B.ba, B.fa
        ba.mark(); fa.mark()
        n_fm, mg = B.n_fm_t, B.mg_t
        B.mu67_off = ba.off
        mu = [ba.alloc_at(B.op_region + 4096 * c, 4096) for c in range(6)] + [ba.alloc(4096) for _ in range(2)]
        n2b = ba.alloc(2048)
        h2 = [fa.alloc(4096) for _ in range(2)]
        st = B.stat
        B.barrier()
        mu_a = ba.alloc_at(B.op_region, 6 * 4096)
        tru = [B.dma("sync", mu_a.rearrange("p (c t) -> p c t", c=6),
                     B.s_mu[:, 0:768].rearrange("t (c p) -> t c p", c=6), "TRU", transpose=True,
                     gate=B.mdone.get("u", []) + B.outproj_done)]
        mu_b = mu[6]
        tru.append(B.dma("sync", ba.t[0:128, mu_b.offset - 0:mu_b.offset + 2 * 4096].rearrange("p (c t) -> p c t", c=2)
                         if False else None, None, "TRU", transpose=True) if False else
                   B.dma("sync", ba.alloc_at(B.mu67_off, 2 * 4096).rearrange("p (c t) -> p c t", c=2),
                         B.s_mu[:, 768:1024].rearrange("t (c p) -> t c p", c=2), "TRU", transpose=True))
        B.op("vector", lambda e: e.tensor_scalar(out=B.dqg_all[:], in0=B.dq1_all[:], scalar1=B.mav["g"], scalar2=None, op0=ALU.mult))
        gdq = B.op("vector", lambda e: e.tensor_scalar(out=B.dqu_all[:], in0=B.dq1_all[:], scalar1=B.mav["u"], scalar2=None, op0=ALU.mult))
        B.waitg("scalar", [gdq])
        n2v = B.s_n2.rearrange("(t p) f -> t p f", p=128)
        hguard = [[], []]
        n2guard = []
        mu_waited = False
        for t in range(NT):
            h_sb = h2[t % 2]
            for oc in range(8):
                bank = oc % 4
                bg_ = B.P[bank]
                B.waitg("tensor", B.bguard[bank])
                for c in range(ND - 1):
                    B.em("tensor", lambda e, b=bg_, cc=c, tt=t, o=oc: e.matmul(
                        b[:], n_fm[cc][:, 128 * tt:128 * tt + 128],
                        mg[cc][:, 512 * o:512 * o + 512],
                        start=(cc == 0), stop=False))
                tg = B.op("tensor", lambda e, b=bg_, tt=t, o=oc: e.matmul(
                    b[:], n_fm[ND - 1][:, 128 * tt:128 * tt + 128],
                    mg[ND - 1][:, 512 * o:512 * o + 512],
                    start=False, stop=True))
                B.waitg("scalar", [tg])
                B.waitg("scalar", hguard[t % 2])
                g1 = B.op("scalar", lambda e, b=bg_, tt=t, o=oc, hs=h_sb: e.activation(
                    out=hs[:, 512 * o:512 * o + 512], in_=b[:], func=AF.Silu,
                    scale=B.dqg_all[:, tt:tt + 1]))
                B.bguard[bank] = [g1]
            if not mu_waited:
                B.waitg("tensor", tru)
                mu_waited = True
            for oc in range(8):
                bank = oc % 4
                bu_ = B.P[bank]
                B.waitg("tensor", B.bguard[bank])
                for c in range(ND - 1):
                    B.em("tensor", lambda e, b=bu_, cc=c, tt=t, o=oc: e.matmul(
                        b[:], n_fm[cc][:, 128 * tt:128 * tt + 128],
                        mu[cc][:, 512 * o:512 * o + 512],
                        start=(cc == 0), stop=False))
                tg = B.op("tensor", lambda e, b=bu_, tt=t, o=oc: e.matmul(
                    b[:], n_fm[ND - 1][:, 128 * tt:128 * tt + 128],
                    mu[ND - 1][:, 512 * o:512 * o + 512],
                    start=False, stop=True))
                B.waitg("vector", [tg])
                B.waitg("vector", hguard[t % 2] if oc == 0 else [])
                gh = B.op("vector", lambda e, b=bu_, tt=t, o=oc, hs=h_sb: e.scalar_tensor_tensor(
                    out=hs[:, 512 * o:512 * o + 512], in0=b[:],
                    scalar=B.dqu_all[:, tt:tt + 1], in1=hs[:, 512 * o:512 * o + 512],
                    op0=ALU.mult, op1=ALU.mult))
                B.bguard[bank] = [gh]
            sc = 2 + (t % 2)
            B.op("vector", lambda e, hs=h_sb, c0=sc: e.reduce_max(out=st[:, c0:c0 + 1], in_=hs, axis=AX.X, apply_absolute_value=True))
            B.op("vector", lambda e, c0=sc: e.tensor_scalar_max(out=st[:, c0:c0 + 1], in0=st[:, c0:c0 + 1], scalar1=1e-5))
            B.op("vector", lambda e, tt=t, c0=sc: e.tensor_scalar_mul(out=B.dq2_all[:, tt:tt + 1], in0=st[:, c0:c0 + 1], scalar1=1.0 / 127.0))
            B.op("vector", lambda e, c0=sc: e.reciprocal(out=st[:, c0:c0 + 1], in_=st[:, c0:c0 + 1]))
            B.op("vector", lambda e, c0=sc: e.tensor_scalar_mul(out=st[:, c0:c0 + 1], in0=st[:, c0:c0 + 1], scalar1=127.0))
            B.op("vector", lambda e, hs=h_sb, c0=sc: e.tensor_scalar(out=hs, in0=hs, scalar1=st[:, c0:c0 + 1], scalar2=MAGIC, op0=ALU.mult, op1=ALU.add))
            hguard[t % 2] = []
            for hf in range(2):
                B.waitg("vector", n2guard)
                gq = B.op("vector", lambda e, h3=hf, hs=h_sb: e.tensor_scalar(
                    out=n2b, in0=hs[:, 2048 * h3:2048 * h3 + 2048], scalar1=MAGIC, scalar2=None, op0=ALU.subtract))
                hguard[t % 2].append(gq)
                B.waitg("gpsimd", [gq])
                n2guard = [B.dma("gpsimd", n2v[t, :, 2048 * hf:2048 * hf + 2048], n2b, "SN2")]
        B.gateup_tail = [tg] + n2guard
        ba.reset(); fa.reset()
        B.ba.reset()

    def ph_ffn_down(self):
        B = self
        ba, fa = B.ba, B.fa
        ba.reset()
        ba.mark(); fa.mark()
        md_all = ba.alloc(NO * 1024)
        md = [md_all[:, 1024 * c:1024 * c + 1024] for c in range(NO)]
        n2f_all = [ba.alloc(NO * 512) for _ in range(2)]
        n2f2 = [[n2f_all[p][:, 512 * c:512 * c + 512] for c in range(NO)] for p in range(2)]
        acc = [fa.alloc(1024) for _ in range(2)]
        ys = [fa.alloc(1024) for _ in range(2)]
        t_f2, s_f2 = B.lastg["tensor"], B.lastg["scalar"]
        B.op("vector", lambda e: e.tensor_scalar(out=B.ddq_all[:], in0=B.dq2_all[:], scalar1=B.mav["d"], scalar2=None, op0=ALU.mult))
        v_f2 = B.lastg["vector"]
        B.waitg("vector", t_f2 + s_f2)
        av = B.s_acc.rearrange("(t p) d -> t p d", p=128)
        yv = B.y.rearrange("(t p) d -> t p d", p=128)
        ld_guard = [t_f2 + v_f2 + s_f2, t_f2 + v_f2 + s_f2]
        st_guard = [[], []]
        tr_guard = [B.gateup_tail + t_f2 + v_f2 + s_f2, B.gateup_tail + t_f2 + v_f2 + s_f2]
        trs_done = [[], []]

        def load_n2(tg2):
            pp = tg2 % 2
            trs_done[pp] = [B.dma("sync", n2f_all[pp].rearrange("p (c t) -> p c t", c=NO),
                                  B.s_n2[512 * tg2:512 * tg2 + 512, :].rearrange("t (c p) -> t c p", c=NO),
                                  "TN%d" % pp, transpose=True, gate=tr_guard[pp])]
        load_n2(0)
        dtr_gate = B.gateup_tail + B.mdone.get("d", []) + t_f2 + s_f2
        dtr_a = B.dma("sync", md_all.rearrange("p (c t) -> p c t", c=NO)[:, 0:16, :],
                      B.s_md.rearrange("t (c p) -> t c p", c=NO)[:, 0:16, :], "DTR",
                      transpose=True, gate=dtr_gate)
        dtr_b = B.dma("sync", md_all.rearrange("p (c t) -> p c t", c=NO)[:, 16:32, :],
                      B.s_md.rearrange("t (c p) -> t c p", c=NO)[:, 16:32, :], "DTR2",
                      transpose=True)
        B.waitg("tensor", [dtr_a])
        for tg_ in range(4):
            pp = tg_ % 2
            n2f = n2f2[pp]
            B.waitg("tensor", trs_done[pp])
            if tg_ < 3:
                load_n2(tg_ + 1)
            tr_guard[pp] = []
            for tl in range(4):
                t = 4 * tg_ + tl
                par = t % 2
                dv = B.dma("scalar", acc[par], av[t], "LA%d" % par, gate=ld_guard[par])
                evs = []
                for half in range(2):
                    bank = (2 * t + half) % 4
                    Pb = B.P[bank]
                    B.waitg("tensor", B.bguard[bank])
                    for c in range(NO - 1):
                        if tg_ == 0 and tl == 0 and half == 0 and c == 16:
                            B.waitg("tensor", [dtr_b])
                        B.em("tensor", lambda e, b=Pb, cc=c, lt=tl, hh=half, nf=n2f: e.matmul(
                            b[:], nf[cc][:, 128 * lt:128 * lt + 128],
                            md[cc][:, 512 * hh:512 * hh + 512],
                            start=(cc == 0), stop=False))
                    tgm = B.op("tensor", lambda e, b=Pb, lt=tl, hh=half, nf=n2f: e.matmul(
                        b[:], nf[NO - 1][:, 128 * lt:128 * lt + 128],
                        md[NO - 1][:, 512 * hh:512 * hh + 512],
                        start=False, stop=True))
                    tr_guard[pp].append(tgm)
                    B.waitg("vector", [tgm, dv])
                    B.waitg("vector", st_guard[par])
                    g = B.op("vector", lambda e, b=Pb, tt=t, hh=half, p=par: e.scalar_tensor_tensor(
                        out=ys[p][:, 512 * hh:512 * hh + 512], in0=b[:],
                        scalar=B.ddq_all[:, tt:tt + 1], in1=acc[p][:, 512 * hh:512 * hh + 512],
                        op0=ALU.mult, op1=ALU.add))
                    B.bguard[bank] = [g]
                    evs.append(g)
                ld_guard[par] = [evs[-1]]
                B.waitg("gpsimd", evs)
                st_guard[par] = [B.dma("gpsimd", yv[t], ys[par], "YS%d" % par)]
        ba.reset(); fa.reset()

    def finish(self):
        B = self
        for sname in list(self.C):
            if sname not in ENG_SEM.values() and self.C[sname] > 0:
                B.wait("gpsimd", sname, self.C[sname])

    def materialize(self):
        nc = self.nc
        with self.stack:
            with nc.Block() as block:
                def run(engine_name):
                    def body(eng):
                        for e, fn in self.sched:
                            if e == engine_name:
                                fn(eng)
                    return body
                block.sync(run("sync"))
                block.tensor(run("tensor"))
                block.vector(run("vector"))
                block.scalar(run("scalar"))
                block.gpsimd(run("gpsimd"))
        return nc


_BUILD_CACHE = {}


def _get_nc():
    if "nc" not in _BUILD_CACHE:
        _BUILD_CACHE["nc"] = Builder().build()
    return _BUILD_CACHE["nc"]


def kernel(L0_attn_x, L0_attn_w_in, L0_attn_b_in, L0_attn_w_out, L0_attn_b_out,
           L0_ffn_w_gate, L0_ffn_w_up, L0_ffn_w_down):
    x = np.ascontiguousarray(np.asarray(L0_attn_x, dtype=np.float32))
    w_in = np.ascontiguousarray(np.asarray(L0_attn_w_in, dtype=np.float32))
    b_in = np.ascontiguousarray(np.asarray(L0_attn_b_in, dtype=np.float32))
    w_out = np.ascontiguousarray(np.asarray(L0_attn_w_out, dtype=np.float32))
    b_out = np.ascontiguousarray(np.asarray(L0_attn_b_out, dtype=np.float32))
    w_gate = np.ascontiguousarray(np.asarray(L0_ffn_w_gate, dtype=np.float32))
    w_up = np.ascontiguousarray(np.asarray(L0_ffn_w_up, dtype=np.float32))
    w_down = np.ascontiguousarray(np.asarray(L0_ffn_w_down, dtype=np.float32))

    nc = _get_nc()
    in_maps = [dict(x=x[c], w_in=w_in, b_in=b_in, w_out=w_out, b_out=b_out,
                    w_gate=w_gate, w_up=w_up, w_down=w_down) for c in range(8)]
    res = run_bass_kernel_spmd(nc, in_maps, core_ids=list(range(8)))
    return np.stack([res.results[c]["y"] for c in range(8)], axis=0)


# version 27
# speedup vs baseline: 1.1467x; 1.0146x over previous
import contextlib
from collections import defaultdict

import numpy as np
import ml_dtypes

import concourse.bass as bass
import concourse.mybir as mybir
from concourse.bass_utils import run_bass_kernel_spmd

BFNP = ml_dtypes.bfloat16
F32 = mybir.dt.float32
BF16 = mybir.dt.bfloat16
AF = mybir.ActivationFunctionType
ALU = mybir.AluOpType
AX = mybir.AxisListType

L, D, H, DH, INNER = 2048, 1024, 16, 64, 4096
NT, ND, NO = L // 128, D // 128, INNER // 128
MAGIC = float(np.float32(1.5 * 2 ** 23))
EPS = 1e-5
SLOPES = [2.0 ** (-(h + 1) * 0.5) for h in range(H)]
GWH = [128, 256, 256, 512, 512] + [512] * 11
WTH = [127.0 + 25.0 / s for s in SLOPES]
ENG_SEM = {"tensor": "TS", "vector": "VS", "scalar": "SS", "gpsimd": "GS", "sync": "SY"}


def _attn_plan():
    plans = {}
    for h in range(H):
        gw, W = GWH[h], WTH[h]
        for qh in range(2):
            steps = []
            kiv = {j: [] for j in (2 * qh, 2 * qh + 1)}
            for ki in range(8 * qh + 8):
                k0 = 128 * ki
                for j in (2 * qh, 2 * qh + 1):
                    c0 = 512 * j
                    exps, zeros, mask = [], [], None
                    for qg0 in range(c0, c0 + 512, gw):
                        if qg0 + gw <= k0 or qg0 - k0 > W:
                            zeros.append((qg0, qg0 + gw))
                        else:
                            exps.append((qg0, qg0 - k0))
                            if k0 >= qg0:
                                if k0 > qg0:
                                    zeros.append((qg0, k0))
                                mask = k0
                    if exps:
                        wlo = min(q for q, _ in exps)
                        whi = max(q for q, _ in exps) + gw
                        steps.append(dict(ki=ki, j=j, wlo=wlo, whi=whi,
                                          exps=exps, zeros=zeros, mask=mask))
                        kiv[j].append(ki)
            plans[(h, qh)] = (steps, kiv)
    return plans


ATTN_PLANS = _attn_plan()
_EB_COLS = {}
for (_h, _qh), (_steps, _kiv) in ATTN_PLANS.items():
    for _st in _steps:
        for _qg0, _delta in _st["exps"]:
            _EB_COLS.setdefault((_h, _delta), len(_EB_COLS))


def _make_consts():
    kl = np.arange(128, dtype=np.float64)
    eb = np.zeros((128, max(8, len(_EB_COLS))), dtype=np.float32)
    for (h, delta), c in _EB_COLS.items():
        eb[:, c] = (SLOPES[h] * (kl - delta - GWH[h] / 2)).astype(np.float32)
    m01 = (np.arange(128)[:, None] <= np.arange(128)[None, :]).astype(BFNP)
    return eb, m01


class Arena:
    def __init__(self, nc, name, dtype, ncols):
        self.t = nc.alloc_sbuf_tensor(name, [128, ncols], dtype)
        self.n = ncols
        self.off = 0
        self.stack = []

    def alloc(self, cols, parts=128):
        assert self.off + cols <= self.n, f"arena overflow {self.off}+{cols}>{self.n}"
        ap = self.t[0:parts, self.off:self.off + cols]
        self.off += cols
        return ap

    def alloc_at(self, off, cols, parts=128):
        assert off + cols <= self.n
        return self.t[0:parts, off:off + cols]

    def mark(self):
        self.stack.append(self.off)

    def reset(self):
        self.off = self.stack.pop()


class Builder:
    def __init__(self):
        nc = bass.Bass("TRN2", target_bir_lowering=False)
        self.nc = nc
        self.sched = []
        self.C = defaultdict(int)
        self.stack = contextlib.ExitStack()
        self.sem = {s: self.stack.enter_context(nc.semaphore(s))
                    for s in ("TS", "VS", "SS", "GS", "SY")}
        self.lastg = {e: [] for e in ("tensor", "vector", "scalar", "gpsimd", "sync")}

        d = nc.dram_tensor
        self.x = d("x", [L, D], F32, kind="ExternalInput").ap()
        self.w_in = d("w_in", [D, 3 * D], F32, kind="ExternalInput").ap()
        self.b_in = d("b_in", [3 * D], F32, kind="ExternalInput").ap()
        self.w_out = d("w_out", [D, D], F32, kind="ExternalInput").ap()
        self.b_out = d("b_out", [D], F32, kind="ExternalInput").ap()
        self.w_gate = d("w_gate", [INNER, D], F32, kind="ExternalInput").ap()
        self.w_up = d("w_up", [INNER, D], F32, kind="ExternalInput").ap()
        self.w_down = d("w_down", [D, INNER], F32, kind="ExternalInput").ap()
        self.y = d("y", [L, D], F32, kind="ExternalOutput").ap()
        self.s_xnb = d("s_xnb", [L, D], BF16).ap()
        self.s_n = d("s_n", [L, D], BF16).ap()
        self.s_attn = d("s_attn", [D, L], BF16).ap()
        self.s_acc = d("s_acc", [L, D], F32).ap()
        self.s_mg = d("s_mg", [INNER, D], BF16).ap()
        self.s_mu = d("s_mu", [INNER, D], BF16).ap()
        self.s_md = d("s_md", [D, INNER], BF16).ap()
        self.s_n2 = d("s_n2", [L, INNER], BF16).ap()

        eb_np, m01_np = _make_consts()
        self.c_eb = nc.inline_tensor(eb_np, name="c_eb")
        self.c_m01 = nc.inline_tensor(m01_np, name="c_m01")
        self.n_eb = eb_np.shape[1]

        self.fa = Arena(nc, "fa", F32, 8832)
        self.ba = Arena(nc, "ba", BF16, 88320)

        self.eb = self.fa.alloc(self.n_eb)
        self.cpool = self.fa.alloc(4)
        self.dq1_all = self.fa.alloc(NT)
        self.dq2_all = self.fa.alloc(NT)
        self.dqg_all = self.fa.alloc(NT)
        self.dqu_all = self.fa.alloc(NT)
        self.ddq_all = self.fa.alloc(NT)
        self.b_qk = self.fa.alloc(16)
        self.wsv = {w: self.fa.alloc(1) for w in "gud"}
        self.mav = {w: self.fa.alloc(1) for w in "gud"}
        self.stat = self.fa.alloc(8)
        self.ones_col_f = self.fa.alloc(1)
        self.ones_row128 = self.fa.alloc(128, parts=1)
        self.ones_row64 = self.ones_row128[0:1, 0:64]

        self.m01 = self.ba.alloc(128)
        self.ones_row = self.ba.alloc(2048, parts=1)
        self.bv_row = self.ba.alloc(1024, parts=1)
        self.bo_row = self.ba.alloc(1024, parts=1)

        self.mdone = {}
        self.wqacc = self.fa.alloc(36)

        self.P = [nc.alloc_psum_tensor(f"pb{i}", [128, 512], F32) for i in range(8)]
        self.bguard = {i: [] for i in range(8)}

    def em(self, engine, fn):
        self.sched.append((engine, fn))

    def _sem(self, name):
        if name not in self.sem:
            self.sem[name] = self.stack.enter_context(self.nc.semaphore(name))
        return self.sem[name]

    def wait(self, engine, sem, val):
        if val > 0:
            self.em(engine, lambda e, s=self._sem(sem), v=val: e.wait_ge(s, v))

    def waitg(self, engine, guard):
        best = {}
        for s, v in guard:
            best[s] = max(best.get(s, 0), v)
        for s, v in best.items():
            self.wait(engine, s, v)

    def dma(self, queue, out, in_, sem, transpose=False, slow=False, gate=()):
        sh = self._sem(sem)
        self.waitg(queue, gate)
        self.C[sem] += 16
        v = self.C[sem]
        if transpose:
            self.em(queue, lambda e, o=out, i=in_, s=sh: e.dma_start_transpose(out=o, in_=i).then_inc(s, 16))
        else:
            self.em(queue, lambda e, o=out, i=in_, s=sh, sl=slow: e.dma_start(out=o, in_=i, allow_slow_non_contiguous=sl).then_inc(s, 16))
        return (sem, v)

    def op(self, engine, fn):
        s = ENG_SEM[engine]
        prev = 0 if engine == "tensor" else self.C[s]
        self.C[s] += 1
        v = self.C[s]

        def run(e, f=fn, sh=self.sem[s], p=prev, vv=v):
            if p > 0:
                e.wait_ge(sh, p)
            return f(e).then_inc(sh, 1)
        self.em(engine, run)
        self.lastg[engine] = [(s, v)]
        return (s, v)

    def barrier(self):
        vals = [(s, self.C[s]) for s in list(self.C) if self.C[s] > 0]
        for eng in ("sync", "tensor", "vector", "scalar", "gpsimd"):
            for s, v in vals:
                self.wait(eng, s, v)

    def build(self):
        self.ph_init()
        self.ph_ln_quant()
        self.ph_attention()
        self.ph_outproj()
        self.ph_ffn_gateup()
        self.ph_ffn_down()
        self.finish()
        return self.materialize()

    def ph_init(self):
        B = self
        B.dma("sync", B.eb, B.c_eb[:], "DIN")
        B.dma("sync", B.m01, B.c_m01[:], "DIN")
        bq = B.b_in[0:D].rearrange("(j p) -> p j", p=128)
        bk = B.b_in[D:2 * D].rearrange("(j p) -> p j", p=128)
        B.dma("sync", B.b_qk[:, 0:8], bq, "DIN", slow=True)
        B.dma("sync", B.b_qk[:, 8:16], bk, "DIN", slow=True)
        B.fa.mark()
        bvf = B.fa.alloc(1024, parts=1)
        bof = B.fa.alloc(1024, parts=1)
        B.dma("sync", bvf, B.b_in[2 * D:3 * D][None, :], "DIN")
        g = B.dma("sync", bof, B.b_out[None, :], "DIN")
        B.wait("vector", "DIN", g[1])
        B.op("vector", lambda e: e.memset(B.cpool[:, 0:1], -0.5))
        B.op("vector", lambda e: e.memset(B.cpool[:, 1:2], 0.5))
        B.op("vector", lambda e: e.memset(B.cpool[:, 2:3], 0.125))
        B.op("vector", lambda e: e.memset(B.cpool[:, 3:4], MAGIC))
        B.op("vector", lambda e: e.memset(B.ones_col_f, 1.0))
        B.op("vector", lambda e: e.memset(B.ones_row128, 1.0))
        B.op("vector", lambda e: e.memset(B.ones_row, 1.0))
        B.op("vector", lambda e, s=bvf: e.tensor_copy(out=B.bv_row, in_=s))
        B.op("vector", lambda e, s=bof: e.tensor_copy(out=B.bo_row, in_=s))
        B.op("vector", lambda e: e.tensor_scalar_mul(out=B.b_qk[:, 0:8], in0=B.b_qk[:, 0:8], scalar1=0.125))
        B.fa.reset()

    def ph_ln_quant(self):
        B = self
        B.barrier()
        fa, ba = B.fa, B.ba
        ba.mark()
        B.op_region = ba.off
        B.afm_t = [ba.alloc(2048) for _ in range(ND)]
        B.wo_sb = ba.alloc(ND * 1024)
        ba.mark()
        B.xnfm_all = ba.alloc(ND * 2048)
        B.xn_fm_t = [B.xnfm_all[:, 2048 * c:2048 * c + 2048] for c in range(ND)]
        B.wqm2 = [ba.alloc(1024) for _ in range(2)]
        B.wqsg = [ba.alloc(1024) for _ in range(2)]
        fa.mark()
        B.wqb = [fa.alloc(1024) for _ in range(2)]
        B.wq_gen = B.wquant_gen(B.wqb, B.wqm2, B.wqsg)
        B.att_wv_region = ba.off
        B.att_wv = ba.alloc(8 * 1024)
        B.att_v = ba.alloc(NT * 1040)
        B.att_wqk = [ba.alloc(2 * 1024) for _ in range(2)]
        B.att_wstage = [fa.alloc(1024) for _ in range(2)]
        B.att_stg_cons = [None, None]
        wvv0 = B.w_in[:, 2 * D:3 * D].rearrange("(c p) f -> c p f", p=128)
        wov0 = B.w_out.rearrange("(c p) f -> c p f", p=128)
        B.att_wcast = []
        for c in range(ND):
            sl = c % 2
            dv = B.dma("sync", B.att_wstage[sl], wvv0[c], "WST%d" % sl,
                       gate=[B.att_stg_cons[sl]] if B.att_stg_cons[sl] else [])
            B.waitg("gpsimd", [dv])
            B.att_stg_cons[sl] = B.op("gpsimd", lambda e, cc=c, s2=sl: e.tensor_copy(
                out=B.att_wv[:, 1024 * cc:1024 * cc + 1024], in_=B.att_wstage[s2]))
            B.att_wcast.append(B.att_stg_cons[sl])
        for c in range(ND):
            sl = c % 2
            dv = B.dma("sync", B.att_wstage[sl], wov0[c], "WST%d" % sl,
                       gate=[B.att_stg_cons[sl]] if B.att_stg_cons[sl] else [])
            B.waitg("gpsimd", [dv])
            B.att_stg_cons[sl] = B.op("gpsimd", lambda e, cc=c, s2=sl: e.tensor_copy(
                out=B.wo_sb[:, 1024 * cc:1024 * cc + 1024], in_=B.att_wstage[s2]))
        B.att_vs_ones = B.op("vector", lambda e: e.memset(B.att_v, 1.0))
        fa.mark(); ba.mark()
        xt = [fa.alloc(1024) for _ in range(2)]
        xn = [fa.alloc(1024) for _ in range(2)]
        dump = xn[0]
        ls = fa.alloc(8 * 16)
        nb = [ba.alloc(1024) for _ in range(2)]
        xb = [ba.alloc(1024) for _ in range(2)]
        xv = B.x.rearrange("(t p) d -> t p d", p=128)
        nv = B.s_n.rearrange("(t p) d -> t p d", p=128)

        def C(i):
            return ls[:, 16 * i:16 * i + 16]

        def Ct(i, t):
            return ls[:, 16 * i + t:16 * i + t + 1]

        sq_cons = [[], []]
        gq = None
        for t in range(NT):
            par = t % 2
            dv = B.dma("sync", xt[par], xv[t], "LX%d" % par, gate=sq_cons[par])
            B.waitg("scalar", [dv])
            B.op("scalar", lambda e, p=par, tt=t: e.activation(
                out=dump, in_=xt[p], func=AF.Identity, accum_out=Ct(0, tt)))
            gq = B.op("scalar", lambda e, p=par, tt=t: e.activation(
                out=dump, in_=xt[p], func=AF.Square, accum_out=Ct(1, tt)))
            sq_cons[par] = [gq]
            B.pump(2)
        B.waitg("vector", [gq])
        B.op("vector", lambda e: e.tensor_scalar_mul(out=C(2), in0=C(0), scalar1=1.0 / 1024))
        B.op("vector", lambda e: e.tensor_scalar_mul(out=C(3), in0=C(1), scalar1=1.0 / 1024))
        B.op("vector", lambda e: e.tensor_mul(out=C(5), in0=C(2), in1=C(2)))
        B.op("vector", lambda e: e.tensor_sub(out=C(3), in0=C(3), in1=C(5)))
        B.op("vector", lambda e: e.tensor_scalar_add(out=C(3), in0=C(3), scalar1=EPS))
        g = B.op("vector", lambda e: e.reciprocal(out=C(4), in_=C(3)))
        B.waitg("scalar", [g])
        g = B.op("scalar", lambda e: e.activation(out=C(4), in_=C(4), func=AF.Sqrt))
        B.waitg("vector", [g])
        for _ in range(2):
            B.op("vector", lambda e: e.tensor_mul(out=C(5), in0=C(4), in1=C(4)))
            B.op("vector", lambda e: e.tensor_mul(out=C(5), in0=C(5), in1=C(3)))
            B.op("vector", lambda e: e.tensor_scalar(out=C(5), in0=C(5), scalar1=-0.5, scalar2=1.5, op0=ALU.mult, op1=ALU.add))
            B.op("vector", lambda e: e.tensor_mul(out=C(4), in0=C(4), in1=C(5)))
        B.op("vector", lambda e: e.tensor_mul(out=C(5), in0=C(2), in1=C(4)))
        gmr = B.op("vector", lambda e: e.tensor_scalar_mul(out=C(5), in0=C(5), scalar1=-1.0))
        B.waitg("scalar", [gmr])
        B.tx_done = {}
        sguard = [[], []]
        xt_cons = [list(sq_cons[0]), list(sq_cons[1])]
        xn_cons = [[], []]
        for t in range(NT):
            par = t % 2
            dv = B.dma("sync", xt[par], xv[t], "LX%d" % par, gate=xt_cons[par])
            B.waitg("scalar", [dv])
            B.waitg("scalar", xn_cons[par])
            gap = B.op("scalar", lambda e, p=par, tt=t: e.activation(
                out=xn[p], in_=xt[p], func=AF.Identity, scale=Ct(4, tt), bias=Ct(5, tt)))
            B.waitg("scalar", sguard[par])
            gx = B.op("scalar", lambda e, p=par: e.copy(out=xb[p], in_=xn[p]))
            B.waitg("vector", [gap])
            gab = B.op("vector", lambda e, p=par, tt=t: e.reduce_max(out=Ct(6, tt), in_=xn[p], axis=AX.X, apply_absolute_value=True))
            B.op("vector", lambda e, tt=t: e.tensor_scalar_max(out=Ct(6, tt), in0=Ct(6, tt), scalar1=1e-5))
            B.op("vector", lambda e, tt=t: e.tensor_scalar_mul(out=B.dq1_all[:, tt:tt + 1], in0=Ct(6, tt), scalar1=1.0 / 127.0))
            B.op("vector", lambda e, tt=t: e.reciprocal(out=Ct(7, tt), in_=Ct(6, tt)))
            gs1 = B.op("vector", lambda e, tt=t: e.tensor_scalar_mul(out=Ct(7, tt), in0=Ct(7, tt), scalar1=127.0))
            B.waitg("scalar", [gs1])
            gy = B.op("scalar", lambda e, p=par, tt=t: e.activation(
                out=xt[p], in_=xn[p], func=AF.Identity, scale=Ct(7, tt), bias=B.cpool[:, 3:4]))
            B.waitg("vector", [gy])
            gn = B.op("vector", lambda e, p=par: e.tensor_scalar(out=nb[p], in0=xt[p], scalar1=MAGIC, scalar2=None, op0=ALU.subtract))
            xt_cons[par] = [gn]
            xn_cons[par] = [gy, gab]
            B.waitg("gpsimd", [gn])
            s2 = B.dma("gpsimd", nv[t], nb[par], "LS%d" % par)
            txg = [B.dma("sync", B.xnfm_all.rearrange("p (c t) -> p c t", c=ND)[:, :, 128 * t:128 * t + 128],
                         xb[par].rearrange("t (c p) -> t c p", c=ND),
                         "TX%d" % par, transpose=True, gate=[gx])]
            sguard[par] = [s2] + txg
            B.tx_done[t] = txg
            B.emit_v_unit(t)
            B.pump(2)
        B.ln_final = sguard[0] + sguard[1]
        fa.reset(); ba.reset()

    def ph_attention(self):
        B = self
        ba, fa = B.ba, B.fa
        ba.mark(); fa.mark()
        xn_fm = B.xn_fm_t
        wv_region = B.att_wv_region
        wv_sb = B.att_wv
        v_sb = B.att_v
        wqk = B.att_wqk
        q_fm2 = [ba.alloc_at(wv_region, 2048), ba.alloc_at(wv_region + 4096, 2048)]
        k_fm2 = [ba.alloc_at(wv_region + 2048, 2048), ba.alloc_at(wv_region + 6144, 2048)]
        p_sb = [[ba.alloc(512) for i in range(3)] for l in range(2)]
        o_attn = ba.alloc(1024)
        o_un = [ba.alloc(512, parts=65) for _ in range(8)]
        srow = fa.alloc(2048, parts=1)
        oun_guard = [[] for _ in range(8)]
        B.r_row = fa.alloc(2048, parts=1)
        rr_guard = [[] for _ in range(4)]
        wstage2 = B.att_wstage
        stg_cons = B.att_stg_cons
        tv = [("TX%d" % p, B.C["TX%d" % p]) for p in range(2)]
        wcast = B.att_wcast
        B.store_done = {h: [] for h in range(H)}
        vs_ones = B.att_vs_ones
        B.waitg("tensor", [wcast[-1], vs_ones])
        SPOOL = {0: (2, 3, 4), 1: (5, 6, 7)}
        OBANK = {0: 0, 1: 1}
        wqk_guard = [[], []]
        qk_guard = []
        pv_guard = [[[], [], []], [[], [], []]]
        sa_guard = [[], []]
        norm_pending = []
        oun_next = [0]

        def emit_norm(maxn=1):
            for _ in range(maxn):
                if not norm_pending:
                    return
                norm_pending.pop(0)()

        def load_wqk(hp2):
            par2 = hp2 % 2
            B.waitg("sync", wqk_guard[par2])
            out = []
            for qk, c0 in ((0, 128 * hp2), (1, D + 128 * hp2)):
                sl = qk
                wvw = B.w_in[:, c0:c0 + 128].rearrange("(c p) f -> p c f", p=128)
                dv = B.dma("sync", wstage2[sl].rearrange("p (c f) -> p c f", f=128), wvw, "WST%d" % sl,
                           gate=[stg_cons[sl]] if stg_cons[sl] else [])
                B.waitg("gpsimd", [dv])
                stg_cons[sl] = B.op("gpsimd", lambda e, p=par2, q=qk, s2=sl: e.tensor_copy(
                    out=wqk[p][:, 1024 * q:1024 * q + 1024], in_=wstage2[s2]))
                out.append(stg_cons[sl])
            return out

        B.waitg("tensor", tv)
        B.waitg("scalar", B.ln_final)
        B.waitg("vector", B.ln_final)
        B.ln_final = None
        wq_cast_next = load_wqk(0)
        qk_guard2 = [[], []]
        for hp in range(8):
            par = hp % 2
            q_fm, k_fm = q_fm2[par], k_fm2[par]
            wq_cast = wq_cast_next
            B.waitg("tensor", [wq_cast[-1]])
            qk_ready = []
            for qk in range(2):
                for tc in range(4):
                    bank = (qk * 4 + tc) % 4
                    Pb = B.P[bank]
                    B.waitg("tensor", B.bguard[bank])
                    for c in range(ND - 1):
                        B.em("tensor", lambda e, b=Pb, cc=c, q=qk, t=tc, p=par: e.matmul(
                            b[:], wqk[p][:, 1024 * q + 128 * cc:1024 * q + 128 * cc + 128],
                            xn_fm[cc][:, 512 * t:512 * t + 512],
                            start=(cc == 0), stop=False))
                    tg = B.op("tensor", lambda e, b=Pb, q=qk, t=tc, p=par: e.matmul(
                        b[:], wqk[p][:, 1024 * q + 128 * (ND - 1):1024 * q + 128 * ND],
                        xn_fm[ND - 1][:, 512 * t:512 * t + 512],
                        start=False, stop=True))
                    B.waitg("scalar", [tg])
                    B.waitg("scalar", qk_guard2[par])
                    qk_guard2[par] = []
                    dst = q_fm if qk == 0 else k_fm
                    bias = B.b_qk[:, 8 * qk + hp:8 * qk + hp + 1]
                    if qk == 0:
                        g = B.op("scalar", lambda e, b=Pb, t=tc, dd=dst, bi=bias: e.activation(
                            out=dd[:, 512 * t:512 * t + 512], in_=b[:], func=AF.Identity,
                            scale=B.cpool[:, 2:3], bias=bi))
                    else:
                        g = B.op("scalar", lambda e, b=Pb, t=tc, dd=dst, bi=bias: e.activation(
                            out=dd[:, 512 * t:512 * t + 512], in_=b[:], func=AF.Identity, bias=bi))
                    B.bguard[bank] = [g]
                    qk_ready.append(g)
            B.waitg("tensor", [qk_ready[-1]])
            if hp < 7:
                wq_cast_next = load_wqk(hp + 1)
            if hp >= 2:
                c2 = hp - 2
                B.dma("sync", B.afm_t[c2], B.s_attn[128 * c2:128 * c2 + 128, :], "AFM",
                      gate=B.store_done[2 * c2] + B.store_done[2 * c2 + 1])
            qk_guard = []
            for qh in range(2):
                locs = [0, 1]
                plans = {loc: ATTN_PLANS[(2 * hp + loc, qh)] for loc in locs}
                keys = sorted({(st["j"], st["ki"]) for loc in locs for st in plans[loc][0]})
                stepmap = {loc: {(st["j"], st["ki"]): st for st in plans[loc][0]} for loc in locs}
                ready = {}

                def emit_v(si2, keys=keys, stepmap=stepmap, plans=plans, ready=ready, hp=hp, qh=qh):
                    sp2 = si2 % 3
                    infos = []
                    for loc2 in locs:
                        info = ready.pop((loc2, si2), None)
                        if info is None:
                            continue
                        ki2, j2, eg2, vg2, pbuf2, h2 = info
                        kivj = plans[loc2][1][j2]
                        first = (ki2 == kivj[0])
                        lastk = (ki2 == kivj[-1])
                        B.waitg("tensor", eg2)
                        B.waitg("tensor", vg2)
                        if first:
                            B.waitg("tensor", B.bguard[OBANK[loc2]])
                        infos.append((loc2, ki2, j2, pbuf2, h2, first, lastk))
                    for loc2, ki2, j2, pbuf2, h2, first, lastk in infos:
                        obank = OBANK[loc2]
                        g = B.op("tensor", lambda e, ob=B.P[obank], k=ki2, hh=h2, pb=pbuf2, f1=first, l1=lastk: e.matmul(
                            ob[0:65, :],
                            v_sb[:, 1040 * k + 65 * hh:1040 * k + 65 * hh + 65],
                            pb[:], start=f1, stop=l1))
                        pv_guard[loc2][sp2] = [g]
                        if lastk:
                            _norm_part1(loc2, j2, hp, qh, [g])

                def _norm_part1(loc2, j2, hp2, qh2, vlast):
                    while len(norm_pending) > 3:
                        emit_norm(1)
                    h2 = 2 * hp2 + loc2
                    jj = j2 % 2
                    obank = OBANK[loc2]
                    Ob = B.P[obank]
                    pu = oun_next[0]
                    oun_next[0] = (pu + 1) % 8
                    B.waitg("scalar", vlast)
                    B.waitg("scalar", oun_guard[pu])
                    ru = pu % 4
                    B.waitg("scalar", rr_guard[ru])
                    gs0 = B.op("scalar", lambda e, b=Ob, r2=ru: e.copy(
                        out=srow[0:1, 512 * r2:512 * r2 + 512], in_=b[64:65, :]))
                    gs_ = B.op("scalar", lambda e, b=Ob, p2=pu: e.copy(out=o_un[p2][0:64, :], in_=b[0:64, :]))
                    B.bguard[obank] = [gs_]
                    B.waitg("vector", [gs0])
                    gr = B.op("vector", lambda e, r2=ru: e.reciprocal(
                        out=B.r_row[0:1, 512 * r2:512 * r2 + 512],
                        in_=srow[0:1, 512 * r2:512 * r2 + 512]))

                    def part2(loc3=loc2, h3=h2, jj3=jj, qh3=qh2, pu3=pu, ru3=ru, gr3=gr, gs3=gs_):
                        rbank = SPOOL[loc3][2]
                        Rb = B.P[rbank]
                        B.waitg("tensor", [gr3])
                        B.waitg("tensor", B.bguard[rbank])
                        gt = B.op("tensor", lambda e, b=Rb, r2=ru3: e.matmul(
                            b[0:64, :], B.ones_row64[0:1, :],
                            B.r_row[0:1, 512 * r2:512 * r2 + 512], start=True, stop=True))
                        rr_guard[ru3] = [gt]
                        B.waitg("vector", [gt, gs3])
                        B.waitg("vector", sa_guard[jj3])
                        gv = B.op("vector", lambda e, b=Rb, p2=pu3, j2=jj3: e.tensor_tensor(
                            out=o_attn[0:64, 512 * j2:512 * j2 + 512],
                            in0=o_un[p2][0:64, :], in1=b[0:64, :], op=ALU.mult))
                        oun_guard[pu3] = [gv]
                        B.bguard[rbank] = [gv]
                        B.waitg("gpsimd", [gv])
                        sa_guard[jj3] = [B.dma("gpsimd",
                            B.s_attn[64 * h3:64 * h3 + 64, 1024 * qh3 + 512 * jj3:1024 * qh3 + 512 * jj3 + 512],
                            o_attn[0:64, 512 * jj3:512 * jj3 + 512], "SA%d" % jj3)]
                        B.store_done[h3] = B.store_done[h3] + sa_guard[jj3]
                    norm_pending.append(part2)

                final_blk = (hp == 7 and qh == 1)
                for si, (j, ki) in enumerate(keys):
                    sp = si % 3
                    if len(norm_pending) > (0 if final_blk else 3):
                        emit_norm(1)
                    for loc in locs:
                        st_ = stepmap[loc].get((j, ki))
                        if st_ is None:
                            continue
                        h = 2 * hp + loc
                        sbank = SPOOL[loc][sp]
                        Sb = B.P[sbank]
                        wlo, whi = st_["wlo"], st_["whi"]
                        B.waitg("tensor", B.bguard[sbank])
                        tg = B.op("tensor", lambda e, b=Sb, l=loc, k=ki, a=wlo, z=whi, kf=k_fm, qf=q_fm: e.matmul(
                            b[0:128, 0:z - a],
                            kf[64 * l:64 * l + 64, 128 * k:128 * k + 128],
                            qf[64 * l:64 * l + 64, a:z],
                            start=True, stop=True, tile_position=(64 * l, 0)))
                        qk_guard.append(tg)
                        B.waitg("scalar", [tg])
                        B.waitg("scalar", pv_guard[loc][sp])
                        pbuf = p_sb[loc][sp]
                        eg = []
                        for qg0, delta in st_["exps"]:
                            gw = GWH[h]
                            col = _EB_COLS[(h, delta)]
                            eg.append(B.op("scalar", lambda e, b=Sb, q=qg0, g0=gw, a=wlo, c=col, pb=pbuf, jj=j: e.activation(
                                out=pb[:, q - 512 * jj:q - 512 * jj + g0],
                                in_=b[0:128, q - a:q - a + g0],
                                func=AF.Exp, bias=B.eb[:, c:c + 1])))
                        B.bguard[sbank] = [eg[-1]]
                        vg = []
                        if st_["zeros"] or st_["mask"] is not None:
                            B.waitg("vector", [eg[-1]])
                            B.waitg("vector", pv_guard[loc][sp])
                        for z0, z1 in st_["zeros"]:
                            vg.append(B.op("vector", lambda e, a=z0, b2=z1, pb=pbuf, jj=j: e.memset(
                                pb[:, a - 512 * jj:b2 - 512 * jj], 0.0)))
                        if st_["mask"] is not None:
                            mk = st_["mask"]
                            vg.append(B.op("vector", lambda e, m=mk, pb=pbuf, jj=j: e.tensor_tensor(
                                out=pb[:, m - 512 * jj:m - 512 * jj + 128],
                                in0=pb[:, m - 512 * jj:m - 512 * jj + 128],
                                in1=B.m01[:], op=ALU.mult)))
                        ready[(loc, si)] = (ki, j, eg[-1:], vg[-1:], pbuf, h)
                    if si % 2 == 1:
                        B.pump(1)
                    if si >= 2:
                        emit_v(si - 2)
                if len(keys) >= 2:
                    emit_v(len(keys) - 2)
                emit_v(len(keys) - 1)
                B.pump(2)
            wqk_guard[par] = list(qk_guard[-1:])
            qk_guard2[par] = list(qk_guard[-1:])
            qk_guard = []
        emit_norm(100)
        B.pump(10 ** 6)
        ba.reset(); fa.reset()
        B.ba.reset()
        B.fa.reset()

    def ph_outproj(self):
        B = self
        ba, fa = B.ba, B.fa
        ba.mark()
        B.nfm_all = ba.alloc(ND * 2048)
        B.n_fm_t = [B.nfm_all[:, 2048 * c:2048 * c + 2048] for c in range(ND)]
        B.mg_all = ba.alloc(ND * 4096)
        B.mg_t = [B.mg_all[:, 4096 * c:4096 * c + 4096] for c in range(ND)]
        ba.mark(); fa.mark()
        attn_fm = B.afm_t
        wo_sb = B.wo_sb
        xt = [fa.alloc(1024) for _ in range(2)]
        acc = [fa.alloc(1024) for _ in range(2)]
        afm67 = []
        for c in (6, 7):
            afm67.append(B.dma("sync", attn_fm[c], B.s_attn[128 * c:128 * c + 128, :], "AFM",
                  gate=B.store_done[2 * c] + B.store_done[2 * c + 1]))
        t_f, v_f, s_f = B.lastg["tensor"], B.lastg["vector"], B.lastg["scalar"]
        sa_f = B.store_done[14] + B.store_done[15]
        B.tr_n = B.dma("sync", B.nfm_all.rearrange("p (c t) -> p c t", c=ND),
                       B.s_n.rearrange("t (c p) -> t c p", c=ND), "TRN", transpose=True,
                       gate=t_f + v_f + sa_f)
        B.tr_g = B.dma("sync", B.mg_all.rearrange("p (c t) -> p c t", c=ND),
                       B.s_mg.rearrange("t (c p) -> t c p", c=ND), "TRG", transpose=True,
                       gate=B.mdone.get("g", []) + B.mdone.get("u", []) + B.mdone.get("d", [])
                            + t_f + v_f + sa_f)
        B.waitg("vector", t_f + s_f)
        B.waitg("tensor", afm67 + v_f + s_f)
        xv = B.x.rearrange("(t p) d -> t p d", p=128)
        av = B.s_acc.rearrange("(t p) d -> t p d", p=128)
        ld_guard = [t_f + v_f + s_f, t_f + v_f + s_f]
        stg = [[], []]
        for t in range(NT):
            par = t % 2
            dv = B.dma("scalar", xt[par], xv[t], "LX%d" % par, gate=ld_guard[par])
            evs = []
            for half in range(2):
                bank = (2 * t + half) % 4
                Pb = B.P[bank]
                B.waitg("tensor", B.bguard[bank])
                for c in range(ND):
                    B.em("tensor", lambda e, b=Pb, cc=c, tt=t, hh=half: e.matmul(
                        b[:], attn_fm[cc][:, 128 * tt:128 * tt + 128],
                        wo_sb[:, 1024 * cc + 512 * hh:1024 * cc + 512 * hh + 512],
                        start=(cc == 0), stop=False))
                tg = B.op("tensor", lambda e, b=Pb, tt=t, hh=half: e.matmul(
                    b[:], B.ones_row[0:1, 128 * tt:128 * tt + 128],
                    B.bo_row[0:1, 512 * hh:512 * hh + 512], start=False, stop=True))
                B.waitg("vector", [tg, dv])
                B.waitg("vector", stg[par])
                g = B.op("vector", lambda e, b=Pb, hh=half, p=par: e.tensor_tensor(
                    out=acc[p][:, 512 * hh:512 * hh + 512],
                    in0=xt[p][:, 512 * hh:512 * hh + 512], in1=b[:], op=ALU.add))
                B.bguard[bank] = [g]
                evs.append(g)
            ld_guard[par] = [evs[-1]]
            B.waitg("gpsimd", evs)
            stg[par] = [B.dma("gpsimd", av[t], acc[par], "LS%d" % par)]
        ba.reset(); fa.reset()
        B.outproj_done = [evs[-1]]

    def wquant_gen(self, wb, m2, sg):
        B = self
        accs = B.wqacc
        specs = [("g", B.w_gate, B.s_mg, 32, 1024), ("u", B.w_up, B.s_mu, 32, 1024),
                 ("d", B.w_down, B.s_md, 8, 4096)]
        cons_guard = [[], []]
        mstore = [[], []]
        sg_guard = []
        for wname, wsrc, mdst, ntile, fdim in specs:
            nsub = fdim // 1024
            wv = wsrc.rearrange("(t p) f -> t p f", p=128)
            gv = None
            ncols = ntile * nsub
            B.waitg("scalar", sg_guard)
            for t in range(ntile):
                for s_ in range(nsub):
                    idx = (t * nsub + s_) % 2
                    dv = B.dma("sync", wb[idx], wv[t, :, 1024 * s_:1024 * s_ + 1024], "WQ%d" % idx, gate=cons_guard[idx])
                    B.waitg("scalar", [dv])
                    col = t * nsub + s_
                    gv = B.op("scalar", lambda e, i=idx, c0=col: e.activation(
                        out=sg[0], in_=wb[i], func=AF.Abs, accum_out=accs[:, c0:c0 + 1]))
                    cons_guard[idx] = [gv]
                    yield
            B.waitg("vector", [gv])
            gv = B.op("vector", lambda e, nc2=ncols: e.reduce_sum(out=accs[:, 32:33], in_=accs[:, 0:nc2], axis=AX.X))
            B.waitg("tensor", [gv])
            B.waitg("tensor", B.bguard[4])
            gt = B.op("tensor", lambda e: e.matmul(B.P[4][0:1, 0:1], accs[:, 32:33], B.ones_col_f, start=True, stop=True))
            B.waitg("vector", [gt])
            B.op("vector", lambda e, n=float(ntile * fdim * 128): e.tensor_scalar_mul(out=accs[0:1, 33:34], in0=B.P[4][0:1, 0:1], scalar1=1.0 / n))
            B.op("vector", lambda e: e.tensor_scalar_max(out=accs[0:1, 33:34], in0=accs[0:1, 33:34], scalar1=1e-5))
            gv = B.op("vector", lambda e: e.reciprocal(out=accs[0:1, 34:35], in_=accs[0:1, 33:34]))
            B.bguard[4] = [gv]
            B.waitg("tensor", [gv])
            B.waitg("tensor", B.bguard[5])
            gt = B.op("tensor", lambda e: e.matmul(B.P[5][0:128, 0:2], B.ones_row128[0:1, :], accs[0:1, 33:35], start=True, stop=True))
            B.waitg("vector", [gt])
            B.op("vector", lambda e, w=wname: e.tensor_copy(out=B.mav[w], in_=B.P[5][0:128, 0:1]))
            gv = B.op("vector", lambda e, w=wname: e.tensor_copy(out=B.wsv[w], in_=B.P[5][0:128, 1:2]))
            B.bguard[5] = [gv]
            B.waitg("scalar", [gv])
            yield
            mv = mdst.rearrange("(t p) f -> t p f", p=128)
            for t in range(ntile):
                for s_ in range(nsub):
                    idx = (t * nsub + s_) % 2
                    dv = B.dma("sync", wb[idx], wv[t, :, 1024 * s_:1024 * s_ + 1024], "WQ%d" % idx, gate=cons_guard[idx])
                    B.waitg("vector", [dv])
                    B.waitg("vector", sg_guard)
                    B.op("vector", lambda e, i=idx, w=wname: e.tensor_scalar(
                        out=sg[0], in0=wb[i], scalar1=B.wsv[w], scalar2=0.5, op0=ALU.mult, op1=ALU.is_ge))
                    g2 = B.op("vector", lambda e, i=idx, w=wname: e.tensor_scalar(
                        out=sg[1], in0=wb[i], scalar1=B.wsv[w], scalar2=-0.5, op0=ALU.mult, op1=ALU.is_le))
                    B.waitg("vector", mstore[idx])
                    gv = B.op("vector", lambda e, i=idx: e.tensor_sub(out=m2[i], in0=sg[0], in1=sg[1]))
                    sg_guard = [gv]
                    cons_guard[idx] = [gv]
                    B.waitg("gpsimd", [gv])
                    mstore[idx] = [B.dma("gpsimd", mv[t, :, 1024 * s_:1024 * s_ + 1024], m2[idx], "MS%d" % idx)]
                    yield
            self.mdone[wname] = mstore[0] + mstore[1]

    def pump(self, n=1):
        for _ in range(n):
            if next(self.wq_gen, "done") == "done":
                break

    def emit_v_unit(self, ts_):
        B = self
        xn_fm, wv_sb, v_sb = B.xn_fm_t, B.att_wv, B.att_v
        B.waitg("tensor", B.tx_done[ts_])
        if ts_ == 0:
            B.waitg("tensor", [B.att_wcast[7], B.att_vs_ones])
        for half in range(2):
            bank = (ts_ * 2 + half) % 4
            Pb = B.P[bank]
            B.waitg("tensor", B.bguard[bank])
            for c in range(ND):
                B.em("tensor", lambda e, b=Pb, cc=c, t=ts_, hh=half: e.matmul(
                    b[:], xn_fm[cc][:, 128 * t:128 * t + 128],
                    wv_sb[:, 1024 * cc + 512 * hh:1024 * cc + 512 * hh + 512],
                    start=(cc == 0), stop=False))
            tg = B.op("tensor", lambda e, b=Pb, t=ts_, hh=half: e.matmul(
                b[:], B.ones_row[0:1, 128 * t:128 * t + 128],
                B.bv_row[0:1, 512 * hh:512 * hh + 512], start=False, stop=True))
            B.waitg("vector", [tg])
            last = None
            for hs in range(8):
                hg = half * 8 + hs
                last = B.op("vector", lambda e, b=Pb, t=ts_, h1=hs, h2=hg: e.tensor_copy(
                    out=v_sb[:, 1040 * t + 65 * h2:1040 * t + 65 * h2 + 64],
                    in_=b[:, 64 * h1:64 * h1 + 64]))
            B.bguard[bank] = [last]

    def ph_ffn_gateup(self):
        B = self
        ba, fa = B.ba, B.fa
        ba.mark(); fa.mark()
        n_fm, mg = B.n_fm_t, B.mg_t
        B.mu67_off = ba.off
        mu = [ba.alloc_at(B.op_region + 4096 * c, 4096) for c in range(6)] + [ba.alloc(4096) for _ in range(2)]
        n2b = ba.alloc(2048)
        h2 = [fa.alloc(4096) for _ in range(2)]
        st = B.stat
        B.barrier()
        mu_a = ba.alloc_at(B.op_region, 6 * 4096)
        tru = [B.dma("sync", mu_a.rearrange("p (c t) -> p c t", c=6),
                     B.s_mu[:, 0:768].rearrange("t (c p) -> t c p", c=6), "TRU", transpose=True,
                     gate=B.mdone.get("u", []) + B.outproj_done)]
        mu_b = mu[6]
        tru.append(B.dma("sync", ba.t[0:128, mu_b.offset - 0:mu_b.offset + 2 * 4096].rearrange("p (c t) -> p c t", c=2)
                         if False else None, None, "TRU", transpose=True) if False else
                   B.dma("sync", ba.alloc_at(B.mu67_off, 2 * 4096).rearrange("p (c t) -> p c t", c=2),
                         B.s_mu[:, 768:1024].rearrange("t (c p) -> t c p", c=2), "TRU", transpose=True))
        B.op("vector", lambda e: e.tensor_scalar(out=B.dqg_all[:], in0=B.dq1_all[:], scalar1=B.mav["g"], scalar2=None, op0=ALU.mult))
        gdq = B.op("vector", lambda e: e.tensor_scalar(out=B.dqu_all[:], in0=B.dq1_all[:], scalar1=B.mav["u"], scalar2=None, op0=ALU.mult))
        B.waitg("scalar", [gdq])
        n2v = B.s_n2.rearrange("(t p) f -> t p f", p=128)
        hguard = [[], []]
        n2guard = []
        mu_waited = False
        for t in range(NT):
            h_sb = h2[t % 2]
            for oc in range(8):
                bank = oc % 4
                bg_ = B.P[bank]
                B.waitg("tensor", B.bguard[bank])
                for c in range(ND - 1):
                    B.em("tensor", lambda e, b=bg_, cc=c, tt=t, o=oc: e.matmul(
                        b[:], n_fm[cc][:, 128 * tt:128 * tt + 128],
                        mg[cc][:, 512 * o:512 * o + 512],
                        start=(cc == 0), stop=False))
                tg = B.op("tensor", lambda e, b=bg_, tt=t, o=oc: e.matmul(
                    b[:], n_fm[ND - 1][:, 128 * tt:128 * tt + 128],
                    mg[ND - 1][:, 512 * o:512 * o + 512],
                    start=False, stop=True))
                B.waitg("scalar", [tg])
                B.waitg("scalar", hguard[t % 2])
                g1 = B.op("scalar", lambda e, b=bg_, tt=t, o=oc, hs=h_sb: e.activation(
                    out=hs[:, 512 * o:512 * o + 512], in_=b[:], func=AF.Silu,
                    scale=B.dqg_all[:, tt:tt + 1]))
                B.bguard[bank] = [g1]
            if not mu_waited:
                B.waitg("tensor", tru)
                mu_waited = True
            for oc in range(8):
                bank = oc % 4
                bu_ = B.P[bank]
                B.waitg("tensor", B.bguard[bank])
                for c in range(ND - 1):
                    B.em("tensor", lambda e, b=bu_, cc=c, tt=t, o=oc: e.matmul(
                        b[:], n_fm[cc][:, 128 * tt:128 * tt + 128],
                        mu[cc][:, 512 * o:512 * o + 512],
                        start=(cc == 0), stop=False))
                tg = B.op("tensor", lambda e, b=bu_, tt=t, o=oc: e.matmul(
                    b[:], n_fm[ND - 1][:, 128 * tt:128 * tt + 128],
                    mu[ND - 1][:, 512 * o:512 * o + 512],
                    start=False, stop=True))
                B.waitg("vector", [tg])
                B.waitg("vector", hguard[t % 2] if oc == 0 else [])
                gh = B.op("vector", lambda e, b=bu_, tt=t, o=oc, hs=h_sb: e.scalar_tensor_tensor(
                    out=hs[:, 512 * o:512 * o + 512], in0=b[:],
                    scalar=B.dqu_all[:, tt:tt + 1], in1=hs[:, 512 * o:512 * o + 512],
                    op0=ALU.mult, op1=ALU.mult))
                B.bguard[bank] = [gh]
            sc = 2 + (t % 2)
            B.op("vector", lambda e, hs=h_sb, c0=sc: e.reduce_max(out=st[:, c0:c0 + 1], in_=hs, axis=AX.X, apply_absolute_value=True))
            B.op("vector", lambda e, c0=sc: e.tensor_scalar_max(out=st[:, c0:c0 + 1], in0=st[:, c0:c0 + 1], scalar1=1e-5))
            B.op("vector", lambda e, tt=t, c0=sc: e.tensor_scalar_mul(out=B.dq2_all[:, tt:tt + 1], in0=st[:, c0:c0 + 1], scalar1=1.0 / 127.0))
            B.op("vector", lambda e, c0=sc: e.reciprocal(out=st[:, c0:c0 + 1], in_=st[:, c0:c0 + 1]))
            B.op("vector", lambda e, c0=sc: e.tensor_scalar_mul(out=st[:, c0:c0 + 1], in0=st[:, c0:c0 + 1], scalar1=127.0))
            B.op("vector", lambda e, hs=h_sb, c0=sc: e.tensor_scalar(out=hs, in0=hs, scalar1=st[:, c0:c0 + 1], scalar2=MAGIC, op0=ALU.mult, op1=ALU.add))
            hguard[t % 2] = []
            for hf in range(2):
                B.waitg("vector", n2guard)
                gq = B.op("vector", lambda e, h3=hf, hs=h_sb: e.tensor_scalar(
                    out=n2b, in0=hs[:, 2048 * h3:2048 * h3 + 2048], scalar1=MAGIC, scalar2=None, op0=ALU.subtract))
                hguard[t % 2].append(gq)
                B.waitg("gpsimd", [gq])
                n2guard = [B.dma("gpsimd", n2v[t, :, 2048 * hf:2048 * hf + 2048], n2b, "SN2")]
        B.gateup_tail = [tg] + n2guard
        ba.reset(); fa.reset()
        B.ba.reset()

    def ph_ffn_down(self):
        B = self
        ba, fa = B.ba, B.fa
        ba.reset()
        ba.mark(); fa.mark()
        md_all = ba.alloc(NO * 1024)
        md = [md_all[:, 1024 * c:1024 * c + 1024] for c in range(NO)]
        n2f_all = [ba.alloc(NO * 512) for _ in range(2)]
        n2f2 = [[n2f_all[p][:, 512 * c:512 * c + 512] for c in range(NO)] for p in range(2)]
        acc = [fa.alloc(1024) for _ in range(2)]
        ys = [fa.alloc(1024) for _ in range(2)]
        t_f2, s_f2 = B.lastg["tensor"], B.lastg["scalar"]
        B.op("vector", lambda e: e.tensor_scalar(out=B.ddq_all[:], in0=B.dq2_all[:], scalar1=B.mav["d"], scalar2=None, op0=ALU.mult))
        v_f2 = B.lastg["vector"]
        B.waitg("vector", t_f2 + s_f2)
        av = B.s_acc.rearrange("(t p) d -> t p d", p=128)
        yv = B.y.rearrange("(t p) d -> t p d", p=128)
        ld_guard = [t_f2 + v_f2 + s_f2, t_f2 + v_f2 + s_f2]
        st_guard = [[], []]
        tr_guard = [B.gateup_tail + t_f2 + v_f2 + s_f2, B.gateup_tail + t_f2 + v_f2 + s_f2]
        trs_done = [[], []]

        def load_n2(tg2):
            pp = tg2 % 2
            trs_done[pp] = [B.dma("sync", n2f_all[pp].rearrange("p (c t) -> p c t", c=NO),
                                  B.s_n2[512 * tg2:512 * tg2 + 512, :].rearrange("t (c p) -> t c p", c=NO),
                                  "TN%d" % pp, transpose=True, gate=tr_guard[pp])]
        load_n2(0)
        dtr_gate = B.gateup_tail + B.mdone.get("d", []) + t_f2 + s_f2
        dtr_a = B.dma("sync", md_all.rearrange("p (c t) -> p c t", c=NO)[:, 0:16, :],
                      B.s_md.rearrange("t (c p) -> t c p", c=NO)[:, 0:16, :], "DTR",
                      transpose=True, gate=dtr_gate)
        dtr_b = B.dma("sync", md_all.rearrange("p (c t) -> p c t", c=NO)[:, 16:32, :],
                      B.s_md.rearrange("t (c p) -> t c p", c=NO)[:, 16:32, :], "DTR2",
                      transpose=True)
        B.waitg("tensor", [dtr_a])
        for tg_ in range(4):
            pp = tg_ % 2
            n2f = n2f2[pp]
            B.waitg("tensor", trs_done[pp])
            if tg_ < 3:
                load_n2(tg_ + 1)
            tr_guard[pp] = []
            for tl in range(4):
                t = 4 * tg_ + tl
                par = t % 2
                dv = B.dma("scalar", acc[par], av[t], "LA%d" % par, gate=ld_guard[par])
                evs = []
                for half in range(2):
                    bank = (2 * t + half) % 4
                    Pb = B.P[bank]
                    B.waitg("tensor", B.bguard[bank])
                    for c in range(NO - 1):
                        if tg_ == 0 and tl == 0 and half == 0 and c == 16:
                            B.waitg("tensor", [dtr_b])
                        B.em("tensor", lambda e, b=Pb, cc=c, lt=tl, hh=half, nf=n2f: e.matmul(
                            b[:], nf[cc][:, 128 * lt:128 * lt + 128],
                            md[cc][:, 512 * hh:512 * hh + 512],
                            start=(cc == 0), stop=False))
                    tgm = B.op("tensor", lambda e, b=Pb, lt=tl, hh=half, nf=n2f: e.matmul(
                        b[:], nf[NO - 1][:, 128 * lt:128 * lt + 128],
                        md[NO - 1][:, 512 * hh:512 * hh + 512],
                        start=False, stop=True))
                    tr_guard[pp].append(tgm)
                    B.waitg("vector", [tgm, dv])
                    B.waitg("vector", st_guard[par])
                    g = B.op("vector", lambda e, b=Pb, tt=t, hh=half, p=par: e.scalar_tensor_tensor(
                        out=ys[p][:, 512 * hh:512 * hh + 512], in0=b[:],
                        scalar=B.ddq_all[:, tt:tt + 1], in1=acc[p][:, 512 * hh:512 * hh + 512],
                        op0=ALU.mult, op1=ALU.add))
                    B.bguard[bank] = [g]
                    evs.append(g)
                ld_guard[par] = [evs[-1]]
                B.waitg("gpsimd", evs)
                st_guard[par] = [B.dma("gpsimd", yv[t], ys[par], "YS%d" % par)]
        ba.reset(); fa.reset()

    def finish(self):
        B = self
        for sname in list(self.C):
            if sname not in ENG_SEM.values() and self.C[sname] > 0:
                B.wait("gpsimd", sname, self.C[sname])

    def materialize(self):
        nc = self.nc
        with self.stack:
            with nc.Block() as block:
                def run(engine_name):
                    def body(eng):
                        for e, fn in self.sched:
                            if e == engine_name:
                                fn(eng)
                    return body
                block.sync(run("sync"))
                block.tensor(run("tensor"))
                block.vector(run("vector"))
                block.scalar(run("scalar"))
                block.gpsimd(run("gpsimd"))
        return nc


_BUILD_CACHE = {}


def _get_nc():
    if "nc" not in _BUILD_CACHE:
        _BUILD_CACHE["nc"] = Builder().build()
    return _BUILD_CACHE["nc"]


def kernel(L0_attn_x, L0_attn_w_in, L0_attn_b_in, L0_attn_w_out, L0_attn_b_out,
           L0_ffn_w_gate, L0_ffn_w_up, L0_ffn_w_down):
    x = np.ascontiguousarray(np.asarray(L0_attn_x, dtype=np.float32))
    w_in = np.ascontiguousarray(np.asarray(L0_attn_w_in, dtype=np.float32))
    b_in = np.ascontiguousarray(np.asarray(L0_attn_b_in, dtype=np.float32))
    w_out = np.ascontiguousarray(np.asarray(L0_attn_w_out, dtype=np.float32))
    b_out = np.ascontiguousarray(np.asarray(L0_attn_b_out, dtype=np.float32))
    w_gate = np.ascontiguousarray(np.asarray(L0_ffn_w_gate, dtype=np.float32))
    w_up = np.ascontiguousarray(np.asarray(L0_ffn_w_up, dtype=np.float32))
    w_down = np.ascontiguousarray(np.asarray(L0_ffn_w_down, dtype=np.float32))

    nc = _get_nc()
    in_maps = [dict(x=x[c], w_in=w_in, b_in=b_in, w_out=w_out, b_out=b_out,
                    w_gate=w_gate, w_up=w_up, w_down=w_down) for c in range(8)]
    res = run_bass_kernel_spmd(nc, in_maps, core_ids=list(range(8)))
    return np.stack([res.results[c]["y"] for c in range(8)], axis=0)
